# version 06
# baseline (speedup 1.0000x reference)
import numpy as np
import ml_dtypes
from contextlib import ExitStack

import concourse.bass as bass
import concourse.mybir as mybir
from concourse.bass_utils import run_bass_kernel_spmd
from concourse.alu_op_type import AluOpType as OP

F32 = mybir.dt.float32
BF16 = mybir.dt.bfloat16
AF = mybir.ActivationFunctionType
AXX = mybir.AxisListType.X
NB = ml_dtypes.bfloat16

NCORES = 8
B_LOC = 2
L = 2048
D = 1024
NTILE = B_LOC * 16
MAGIC = float(1.5 * 2.0 ** 23)
EPS_LN = 1e-5



_AUTO_OPS = {
    "tensor_tensor", "tensor_scalar", "tensor_copy", "tensor_reduce",
    "reduce_max", "reduce_sum", "scalar_tensor_tensor", "tensor_tensor_scan",
    "reciprocal", "memset", "memzero", "bn_stats", "bn_aggr", "select",
    "copy_predicated", "iota", "activation", "max_index", "max_with_indices",
    "tensor_tensor_reduce", "stream_shuffle", "transpose", "tensor_add",
    "tensor_mul", "tensor_sub", "tensor_max", "copy", "add", "mul",
    "affine_then_add", "affine_mul_reduce",
}


class EP:

    def __init__(self, k, e, sem):
        self._k, self._e, self._sem = k, e, sem

    def __getattr__(self, name):
        f = getattr(self._e, name)
        if name in _AUTO_OPS:
            k, sem, ee = self._k, self._sem, self._e

            def g(*a, **kw):
                ee.wait_ge(k.sem[sem], k.cnt[sem])
                ins = f(*a, **kw)
                k.inc(ins, sem)
                return ins
            return g
        return f


class K:
    def __init__(self, nc, block, stack):
        self.nc = nc
        self.block = block
        self.stack = stack
        self.sem = {}
        self.cnt = {}
        self.buf = {}
        self.pool = {}
        self.rr = {}

    def mksem(self, *names):
        for n in names:
            self.sem[n] = self.stack.enter_context(self.nc.semaphore(n))
            self.cnt[n] = 0

    def mkpool(self, tag, n):
        names = [f"{tag}{i}" for i in range(n)]
        self.mksem(*names)
        self.pool[tag] = names
        self.rr[tag] = 0

    def dma(self, e, tag, out, in_, **kw):
        names = self.pool[tag]
        s = names[self.rr[tag] % len(names)]
        self.rr[tag] += 1
        e.wait_ge(self.sem[s], self.cnt[s])
        ins = e.dma_start(out=out, in_=in_, **kw)
        self.dinc(ins, s)
        return (s, self.cnt[s])

    def dwait(self, e, h):
        e.wait_ge(self.sem[h[0]], h[1])

    def dwait_all(self, e, tag):
        for s in self.pool[tag]:
            e.wait_ge(self.sem[s], self.cnt[s])

    def inc(self, ins, name, n=1):
        ins.then_inc(self.sem[name], n)
        self.cnt[name] += n
        return self.cnt[name]

    def dinc(self, ins, name):
        return self.inc(ins, name, 16)

    def v(self, name):
        return self.cnt[name]

    def wait(self, eng, name, val=None):
        eng.wait_ge(self.sem[name], self.cnt[name] if val is None else val)

    def V(self, e):
        return EP(self, e, "vec")

    def S(self, e):
        return EP(self, e, "sc")

    def G(self, e):
        return EP(self, e, "gp")

    def mark(self, name):
        self.cnt[name] = self.cnt["vec"]

    def sb(self, name, shape, dtype):
        h = self.stack.enter_context(self.nc.sbuf_tensor(name, shape, dtype))
        self.buf[name] = h
        return h


def host_prepare(inputs):
    f = lambda a: np.ascontiguousarray(np.asarray(a, dtype=np.float32))
    c = {}
    x = f(inputs["L0_attn_x"])

    c["Wg"] = f(inputs["L0_ffn_Wg"]).reshape(8, 128, 16)
    c["Wq"] = f(inputs["L0_ffn_Wq"]).reshape(16, 8, 128, 64)
    c["Wkv"] = f(inputs["L0_ffn_Wkv"]).reshape(16, 8, 128, 128)
    c["Wo"] = f(inputs["L0_ffn_Wo"]).reshape(1024, 1024)
    c["bqT"] = f(inputs["L0_ffn_bq"]).reshape(16, 64, 1)
    c["bkvT"] = f(inputs["L0_ffn_bkv"]).reshape(16, 128, 1)
    c["bo_rep"] = np.ascontiguousarray(
        np.broadcast_to(f(inputs["L0_ffn_bo"])[None, :], (128, 1024)))
    c["lng_rep"] = np.ascontiguousarray(
        np.broadcast_to(f(inputs["L0_ffn_ln_g"])[None, :], (128, 1024)))
    c["lnb_rep"] = np.ascontiguousarray(
        np.broadcast_to(f(inputs["L0_ffn_ln_b"])[None, :], (128, 1024)))

    c["dWT"] = f(inputs["L0_attn_delta_W"]).T.copy().reshape(8, 128, 8, 128)
    c["BCWT"] = np.concatenate(
        [f(inputs["L0_attn_B_W"]).T, f(inputs["L0_attn_C_W"]).T], axis=1
    ).copy().reshape(8, 128, 32)
    c["A_log"] = f(inputs["L0_attn_A_log"]).reshape(8, 128, 16)
    c["dbias"] = f(inputs["L0_attn_delta_b"]).reshape(8, 128).T.copy()
    c["dtbias"] = f(inputs["L0_attn_dt_bias"]).reshape(8, 128).T.copy()
    c["WiT"] = f(inputs["L0_attn_Wi_W"]).T.copy().reshape(8, 128, 16)
    c["Wib"] = f(inputs["L0_attn_Wi_b"]).reshape(16, 1)
    c["WrT"] = f(inputs["L0_attn_Wr_W"]).T.copy()
    c["tau"] = f(inputs["L0_attn_tau"]).reshape(16, 1)
    c["selWT"] = f(inputs["L0_attn_sel_W"]).T.copy().reshape(16, 8, 128)
    c["selb"] = f(inputs["L0_attn_sel_b"]).reshape(8, 128).T.copy()
    c["h0T"] = f(inputs["L0_attn_h0"]).T.copy()
    c["Bb"] = f(inputs["L0_attn_B_b"]).reshape(16, 1)
    c["Cb"] = f(inputs["L0_attn_C_b"]).reshape(16, 1)

    for nm, key in [("qW", "L1_attn_qW"), ("kW", "L1_attn_kW"),
                    ("vW", "L1_attn_vW"), ("oW", "L1_attn_oW")]:
        c[nm] = f(inputs[key]).reshape(4, 2, 128, 256)
    for nm, key in [("qb", "L1_attn_qb"), ("kb", "L1_attn_kb"),
                    ("vb", "L1_attn_vb"), ("ob", "L1_attn_ob")]:
        c[nm] = f(inputs[key])
    c["J"] = f(inputs["L1_attn_J"]).reshape(8, 128).T.copy()

    c["w1T"] = f(inputs["L1_ffn_w1"]).T.copy().reshape(8, 128, 4096)
    c["w2T"] = f(inputs["L1_ffn_w2"]).T.copy().reshape(32, 128, 1024)
    c["convw"] = f(inputs["L1_ffn_conv_w"]).reshape(32, 128, 3)
    c["convb"] = f(inputs["L1_ffn_conv_b"]).reshape(32, 128, 1)

    c["cmisc"] = np.ascontiguousarray(np.broadcast_to(
        np.array([1e-5, 1e-6, 0, 0, 0, 0, 0, 0], np.float32)[None, :], (128, 8)))
    c["ident"] = np.eye(128, dtype=np.float32)
    c["identb"] = np.eye(128, dtype=NB)
    c["ones"] = np.ones((128, 1), np.float32)
    io = np.zeros((2, 128, 8, 16), np.float32)
    for s2 in range(2):
        io[s2, :, :, :] = (2 * np.arange(8)[None, :, None] + s2 - 16)
    c["iotas"] = io

    per_core = []
    for core in range(NCORES):
        own = [2 * core, 2 * core + 1]
        m = {"xin": np.ascontiguousarray(x[own[0]:own[0] + 2])}
        ct = np.full((16, 16, 8), -1.0, np.float32)
        for s2 in range(2):
            for b8 in range(8):
                ct[s2 * 8 + b8, :, b8] = float(own[s2])
        m["Ctilde"] = ct.reshape(16, 128)
        m["own0"] = np.array([[float(own[0])]], np.float32)
        m["own1"] = np.array([[float(own[1])]], np.float32)
        per_core.append(m)
    return c, per_core


def build_program(phases=("l0f1", "mamba", "l0f2", "l1f1", "l1attn", "l1f2")):
    nc = bass.Bass("TRN2", target_bir_lowering=False, debug=False)
    common_shapes, percore_shapes = _declare_shapes()
    t = {}
    for name, (shape, dt) in {**common_shapes, **percore_shapes}.items():
        t[name] = nc.dram_tensor(name, shape, dt, kind="ExternalInput").ap()
    xout = nc.dram_tensor("xout", [B_LOC, L, D], F32, kind="ExternalOutput").ap()

    g_ag_in = nc.dram_tensor("g_ag_in", [16, 4096], F32)
    g_ag_out = nc.dram_tensor("g_ag_out", [128, 4096], F32)
    att_ar_in = nc.dram_tensor("att_ar_in", [128, 256], F32)
    att_ar_out = nc.dram_tensor("att_ar_out", [128, 256], F32)
    w1q_cache = nc.dram_tensor("w1q_cache", [8, 128, 4096], BF16)
    w2q_cache = nc.dram_tensor("w2q_cache", [32, 128, 1024], BF16)
    kv_spill = nc.dram_tensor("kv_spill", [2, 2, 1024, 2048], BF16)
    esc_row = nc.dram_tensor("esc_row", [1, B_LOC * L], F32)
    s2_row = nc.dram_tensor("s2_row", [1, B_LOC * L], F32)
    s2e_row = nc.dram_tensor("s2e_row", [1, B_LOC * L], F32)

    with ExitStack() as stack:
        block = stack.enter_context(nc.Block())
        k = K(nc, block, stack)
        k.dram = dict(g_ag_in=g_ag_in, g_ag_out=g_ag_out, att_ar_in=att_ar_in,
                      att_ar_out=att_ar_out, w1q_cache=w1q_cache, w2q_cache=w2q_cache,
                      kv_spill=kv_spill, esc_row=esc_row, s2_row=s2_row, s2e_row=s2e_row)
        k.t = t
        k.mksem("pe", "vec", "sc", "cc", "gp")
        k.mkpool("ldw", 4)
        k.mkpool("st", 4)
        k.mkpool("ldg", 2)
        k.mkpool("ldx", 2)
        k.mkpool("sth", 2)

        k.pb = [stack.enter_context(nc.psum_tensor(f"pb{i}", [128, 512], F32))
                for i in range(8)]
        XRES = k.sb("XRES", [128, NTILE * D], F32)
        k.SWP = k.sb("SWP", [128, 8], F32)
        k.LNTMP = k.sb("LNTMP", [128, 12], F32)
        STATS = k.sb("STATS", [128, NTILE * 2], F32)
        CONST = k.sb("CONST", [128, 144], F32)
        CONSTB = k.sb("CONSTB", [128, 128], BF16)
        k.XRES = XRES
        k.STATS = STATS

        x3 = XRES[:].rearrange("p (g d) -> p g d", g=NTILE)
        k.x3 = x3
        st3 = STATS[:].rearrange("p (g two) -> p g two", g=NTILE)
        k.st3 = st3
        k.ident = CONST[:, 0:128]
        k.ones = CONST[:, 128:129]
        k.identb = CONSTB[:, 0:128]
        k.eps_ln = CONST[:, 136:137]
        k.eps_q = CONST[:, 137:138]

        @block.sync
        def _(e):
            k.dma(e, "ldx", x3, t["xin"].rearrange("s (ti p) d -> p (s ti) d", p=128))
            k.dma(e, "ldx", k.ident, t["ident"])
            k.dma(e, "ldx", k.ones, t["ones"])
            k.dma(e, "ldx", k.identb, t["identb"])
            k.dma(e, "ldx", CONST[:, 136:144], t["cmisc"])

        PH = dict(l0f1=lambda: phase_l0ffn(k, call=0),
                  mamba=lambda: phase_mamba(k),
                  l0f2=lambda: phase_l0ffn(k, call=1),
                  l1f1=lambda: phase_l1ffn(k, call=0),
                  l1attn=lambda: phase_l1attn(k),
                  l1f2=lambda: phase_l1ffn(k, call=1))
        for p in phases:
            PH[p]()

        @block.sync
        def _(e):
            k.wait(e, "vec")
            k.wait(e, "sc")
            k.wait(e, "pe")
            h = k.dma(e, "sth", xout.rearrange("s (ti p) d -> p (s ti) d", p=128), x3)
            k.dwait(e, h)
    return nc


def _declare_shapes():
    common = {
        "Wg": ([8, 128, 16], F32), "Wq": ([16, 8, 128, 64], F32),
        "Wkv": ([16, 8, 128, 128], F32), "Wo": ([1024, 1024], F32),
        "bqT": ([16, 64, 1], F32), "bkvT": ([16, 128, 1], F32),
        "bo_rep": ([128, 1024], F32),
        "lng_rep": ([128, 1024], F32), "lnb_rep": ([128, 1024], F32),
        "dWT": ([8, 128, 8, 128], F32), "BCWT": ([8, 128, 32], F32),
        "A_log": ([8, 128, 16], F32), "dbias": ([128, 8], F32),
        "dtbias": ([128, 8], F32), "WiT": ([8, 128, 16], F32),
        "Wib": ([16, 1], F32), "WrT": ([16, 16], F32), "tau": ([16, 1], F32),
        "selWT": ([16, 8, 128], F32), "selb": ([128, 8], F32),
        "h0T": ([16, 1], F32), "Bb": ([16, 1], F32), "Cb": ([16, 1], F32),
        "qW": ([4, 2, 128, 256], F32), "kW": ([4, 2, 128, 256], F32),
        "vW": ([4, 2, 128, 256], F32), "oW": ([4, 2, 128, 256], F32),
        "qb": ([4, 256], F32), "kb": ([4, 256], F32), "vb": ([4, 256], F32),
        "ob": ([4, 256], F32), "J": ([128, 8], F32),
        "w1T": ([8, 128, 4096], F32), "w2T": ([32, 128, 1024], F32),
        "convw": ([32, 128, 3], F32), "convb": ([32, 128, 1], F32),
        "ident": ([128, 128], F32), "identb": ([128, 128], BF16),
        "ones": ([128, 1], F32), "iotas": ([2, 128, 8, 16], F32),
        "cmisc": ([128, 8], F32),
    }
    percore = {
        "xin": ([B_LOC, L, D], F32),
        "Ctilde": ([16, 128], F32),
        "own0": ([1, 1], F32), "own1": ([1, 1], F32),
    }
    return common, percore


def _snap(k):
    return dict(k.cnt)


def _bar(k, e, snap):
    for n, v in snap.items():
        if n in k.sem and v > 0:
            e.wait_ge(k.sem[n], v)


def ln_stats_all(k):
    b = k.block
    tmp6 = k.LNTMP

    @b.vector
    def _(e):
        e = k.V(e)
        k.dwait_all(e._e, "ldx")
        for g in range(NTILE):
            e.bn_stats(out=tmp6[:, 0:6], in_=k.x3[:, g, 0:512])
            e.bn_stats(out=tmp6[:, 6:12], in_=k.x3[:, g, 512:1024])
            e.bn_aggr(out=k.st3[:, g, :], in_=tmp6[:])
    vdone = k.v("vec")

    @b.scalar
    def _(e):
        e = k.S(e)
        e._e.wait_ge(k.sem["vec"], vdone)
        var = k.st3[:, :, 1]
        e.activation(out=var, in_=var, func=AF.Ln, bias=k.eps_ln)
        e.activation(out=var, in_=var, func=AF.Exp, scale=-0.5)
    return k.v("sc")


def tr_chain(k, pairs, wait_before=None, banks=(6, 7)):
    b = k.block
    pe0 = k.v("pe")
    vec0 = k.v("vec")

    def pout(i, dt):
        bank = k.pb[banks[i % 2]]
        return bank[:, 0:64].bitcast(BF16) if dt == BF16 else bank[:, 0:128]

    @b.tensor
    def _(e):
        if wait_before is not None:
            for nm, v in wait_before:
                e.wait_ge(k.sem[nm], v)
        for i, (src, dst) in enumerate(pairs):
            if i >= 2:
                e.wait_ge(k.sem["vec"], vec0 + i - 1)
            ident = k.identb if src.dtype == BF16 else k.ident
            ins = e.transpose(out=pout(i, src.dtype), in_=src, identity=ident)
            k.inc(ins, "pe")

    @b.vector
    def _(e):
        e = k.V(e)
        for i, (src, dst) in enumerate(pairs):
            e._e.wait_ge(k.sem["pe"], pe0 + i + 1)
            e.tensor_copy(out=dst, in_=pout(i, src.dtype))


def _quant_weights(k):
    nc, b, t = k.nc, k.block, k.t
    qs = ExitStack()
    sb = lambda nm, sh, dt: qs.enter_context(nc.sbuf_tensor(nm + "q", sh, dt))
    WQF = sb("wqf", [128, 4096], F32)
    WQB = sb("wqb", [128, 4096], BF16)
    ACC = sb("accq", [128, 2], F32)
    SCR = sb("scrq", [1, 4], F32)
    k.swp_handles = []
    for (src, dst, nkt, C, swi) in [
            (k.t["w1T"], k.dram["w1q_cache"].ap(), 8, 4096, 0),
            (k.t["w2T"], k.dram["w2q_cache"].ap(), 32, 1024, 2)]:
        nchunk = (nkt * C) // 4096
        ktpc = max(1, 4096 // C)
        view = src.rearrange("kt p c -> p kt c")
        dstv = dst.rearrange("kt p c -> p kt c")
        for ch in range(nchunk):
            vnow = k.v("vec")

            @b.sync
            def _(e, ch=ch, vnow=vnow):
                e.wait_ge(k.sem["vec"], vnow)
                k.ldh = k.dma(e, "ldw", WQF[:, 0:4096], view[:, ch * ktpc:(ch + 1) * ktpc, :])
            ldh = k.ldh

            @b.vector
            def _(e, ch=ch, ldh=ldh):
                e = k.V(e)
                k.dwait(e._e, ldh)
                e.tensor_reduce(out=ACC[:, 1:2], in_=WQF[:, 0:4096], axis=AXX,
                                op=OP.add, apply_absolute_value=True)
                if ch == 0:
                    e.tensor_copy(out=ACC[:, 0:1], in_=ACC[:, 1:2])
                else:
                    e.tensor_tensor(out=ACC[:, 0:1], in0=ACC[:, 0:1],
                                    in1=ACC[:, 1:2], op=OP.add)
        accv = k.v("vec")

        @b.tensor
        def _(e, accv=accv):
            e.wait_ge(k.sem["vec"], accv)
            ins = e.matmul(out=k.pb[5][0:1, 0:1], lhsT=k.ones, rhs=ACC[:, 0:1],
                           start=True, stop=True)
            k.inc(ins, "pe")
        pesum = k.v("pe")

        @b.vector
        def _(e, pesum=pesum, nkt=nkt, C=C):
            e = k.V(e)
            e._e.wait_ge(k.sem["pe"], pesum)
            e.tensor_scalar(out=SCR[0:1, 0:1], in0=k.pb[5][0:1, 0:1],
                            scalar1=1.0 / (nkt * 128 * C), scalar2=0.0,
                            op0=OP.mult, op1=OP.bypass)
            e.tensor_scalar(out=SCR[0:1, 1:2], in0=SCR[0:1, 0:1], scalar1=1e-5,
                            scalar2=0.0, op0=OP.max, op1=OP.bypass)
            e.reciprocal(out=SCR[0:1, 0:1], in_=SCR[0:1, 1:2])
        swv = k.v("vec")

        @b.sync
        def _(e, swv=swv, swi=swi):
            e.wait_ge(k.sem["vec"], swv)
            k.swp_handles.append(k.dma(e, "ldw", k.SWP[:, swi:swi + 2],
                                       SCR[0:1, 0:2].unsqueeze(1).broadcast_to([1, 128, 2])))
        for ch in range(nchunk):
            vnow = k.v("vec")

            @b.sync
            def _(e, ch=ch, vnow=vnow):
                for hh in k.swp_handles:
                    k.dwait(e, hh)
                e.wait_ge(k.sem["vec"], vnow)
                k.ldh = k.dma(e, "ldw", WQF[:, 0:4096], view[:, ch * ktpc:(ch + 1) * ktpc, :])
            ldh = k.ldh

            @b.vector
            def _(e, ch=ch, ldh=ldh, swi=swi):
                e = k.V(e)
                k.dwait(e._e, ldh)
                k.dwait_all(e._e, "st")
                e.tensor_scalar(out=WQF[:, 0:4096], in0=WQF[:, 0:4096],
                                scalar1=k.SWP[:, swi:swi + 1], scalar2=MAGIC,
                                op0=OP.mult, op1=OP.add)
                e.tensor_scalar(out=WQF[:, 0:4096], in0=WQF[:, 0:4096],
                                scalar1=MAGIC, scalar2=-1.0, op0=OP.subtract, op1=OP.max)
                e.tensor_scalar(out=WQB[:, 0:4096], in0=WQF[:, 0:4096], scalar1=1.0,
                                scalar2=0.0, op0=OP.min, op1=OP.bypass)
            qv = k.v("vec")

            @b.gpsimd
            def _(e, ch=ch, qv=qv):
                e.wait_ge(k.sem["vec"], qv)
                k.dma(e, "st", dstv[:, ch * ktpc:(ch + 1) * ktpc, :], WQB[:, 0:4096])
    qs.close()


def phase_l1ffn(k, call):
    nc, b, t = k.nc, k.block, k.t
    tag = f"f{call}"
    SCH = 256
    NSCH = (B_LOC * L) // SCH
    HT = 32
    if "hs_spill" not in k.dram:
        k.dram["hs_spill"] = nc.dram_tensor("hs_spill", [HT, 128, B_LOC * L], F32)
        k.dram["xqT_spill"] = nc.dram_tensor("xqT_spill", [8, 128, B_LOC * L], BF16)
    hs_spill = k.dram["hs_spill"].ap()
    xqT_spill = k.dram["xqT_spill"].ap()
    w1q_c = k.dram["w1q_cache"].ap()
    w2q_c = k.dram["w2q_cache"].ap()
    esc_d = k.dram["esc_row"].ap()
    s2_d = k.dram["s2_row"].ap()
    s2e_d = k.dram["s2e_row"].ap()
    pb = k.pb

    with ExitStack() as ps:
        with ExitStack() as ps1:
            sb = lambda nm, sh, dt: ps1.enter_context(nc.sbuf_tensor(nm + tag, sh, dt))
            RS = sb("rs", [1, B_LOC * L], F32)
            LNA = sb("lna", [128, D], F32)
            XQ = sb("xq", [128, D], BF16)
            XQT = sb("xqt", [128, 8 * 128], BF16)
            SXC = sb("sxc", [128, 8], F32)
            SXT = sb("sxt", [128, 32], F32)
            SXTT = sb("sxtt", [128, 128], F32)
            CW = sb("cw", [128, HT * 4], F32)
            RMAX = sb("rmax", [128, B_LOC * L], F32)
            cw4 = CW[:].rearrange("p (h f) -> p h f", h=HT)

            snap0 = _snap(k)

            @b.sync
            def _(e):
                _bar(k, e, snap0)
                k.ldh = k.dma(e, "ldw", cw4[:, :, 0:3], t["convw"].transpose([1, 0, 2]),
                              allow_slow_non_contiguous=True)
                k.ldh2 = k.dma(e, "ldw", cw4[:, :, 3:4], t["convb"].transpose([1, 0, 2]),
                               allow_slow_non_contiguous=True)
            cwa_h, cwb_h = k.ldh, k.ldh2

            @b.vector
            def _(e, cwa_h=cwa_h, cwb_h=cwb_h):
                k.dwait(e, cwa_h)
                k.dwait(e, cwb_h)

            ln_stats_all(k)
            sc_rdy = k.v("sc")

            for g in range(NTILE):
                @b.vector
                def _(e, g=g):
                    e = k.V(e)
                    if g == 0:
                        e._e.wait_ge(k.sem["sc"], sc_rdy)
                    else:
                        k.dwait_all(e._e, "st")
                    e.tensor_scalar(out=LNA[:], in0=k.x3[:, g, :],
                                    scalar1=k.st3[:, g, 0:1], scalar2=k.st3[:, g, 1:2],
                                    op0=OP.subtract, op1=OP.mult)
                    e.tensor_reduce(out=SXC[:, 0:1], in_=LNA[:], axis=AXX, op=OP.max,
                                    apply_absolute_value=True)
                    e.tensor_scalar(out=SXC[:, 1:2], in0=SXC[:, 0:1], scalar1=1e-5,
                                    scalar2=0.0, op0=OP.max, op1=OP.bypass)
                    e.reciprocal(out=SXC[:, 2:3], in_=SXC[:, 1:2])
                    e.tensor_scalar(out=SXT[:, g:g + 1], in0=SXC[:, 2:3], scalar1=127.0,
                                    scalar2=0.0, op0=OP.mult, op1=OP.bypass)
                    e.tensor_scalar(out=LNA[:], in0=LNA[:], scalar1=SXT[:, g:g + 1],
                                    scalar2=MAGIC, op0=OP.mult, op1=OP.add)
                    e.tensor_scalar(out=LNA[:], in0=LNA[:], scalar1=MAGIC, scalar2=-128.0,
                                    op0=OP.subtract, op1=OP.max)
                    e.tensor_scalar(out=XQ[:], in0=LNA[:], scalar1=127.0,
                                    scalar2=0.0, op0=OP.min, op1=OP.bypass)
                tr_chain(k, [(XQ[:, dc * 128:(dc + 1) * 128], XQT[:, dc * 128:(dc + 1) * 128])
                             for dc in range(8)], wait_before=[("vec", k.v("vec"))])
                vq = k.v("vec")

                @b.gpsimd
                def _(e, g=g, vq=vq):
                    e.wait_ge(k.sem["vec"], vq)
                    k.dma(e, "st", xqT_spill[:, :, g * 128:(g + 1) * 128]
                          .rearrange("kt p c -> p kt c"),
                          XQT[:].rearrange("p (kt c) -> p kt c", kt=8))

            vprev = k.v("vec")

            @b.tensor
            def _(e, vprev=vprev):
                e.wait_ge(k.sem["vec"], vprev)
                ins = e.transpose(out=pb[6][0:32, 0:128], in_=SXT[:, 0:32], identity=k.ident)
                k.inc(ins, "pe")
            trd = k.v("pe")

            @b.vector
            def _(e, trd=trd):
                e = k.V(e)
                e._e.wait_ge(k.sem["pe"], trd)
                e.tensor_copy(out=SXTT[0:32, 0:128], in_=pb[6][0:32, 0:128])
            cpv = k.v("vec")

            @b.gpsimd
            def _(e, cpv=cpv):
                e.wait_ge(k.sem["vec"], cpv)
                k.ldh = k.dma(e, "ldg", RS[0:1, 0:4096].rearrange("o (g p) -> o g p", g=32),
                              SXTT[0:32, 0:128])
            sx_h = k.ldh

            if call == 0:
                _quant_weights(k)

            W1B = sb("w1b", [128, 8 * 1024], BF16)
            XQS = sb("xqs", [128, 2 * 8 * 258], BF16)
            ESL = sb("esl", [128, 2 * 258], F32)
            HTI = sb("hti", [128, 258], F32)
            T0 = sb("t0", [128, 256], F32)
            T2 = sb("t2", [128, 256], F32)
            HSB = sb("hsb", [128, 2 * 256], F32)
            TABS = sb("tabs", [128, 256], F32)
            w1blk = W1B[:].rearrange("p (kt c) -> p kt c", kt=8)
            xqs3 = XQS[:].rearrange("p (u kt c) -> p u kt c", u=2, kt=8)
            esl2 = ESL[:].rearrange("p (u c) -> p u c", u=2)
            hs2 = HSB[:].rearrange("p (u c) -> p u c", u=2)

            @b.vector
            def _(e, sx_h=sx_h):
                e = k.V(e)
                e.memset(RMAX[:], 0.0)
                k.dwait(e._e, sx_h)
                for hh in k.swp_handles:
                    k.dwait(e._e, hh)
                e.reciprocal(out=RS[0:1, 0:4096], in_=RS[0:1, 0:4096])
                e.tensor_scalar(out=RS[0:1, 0:4096], in0=RS[0:1, 0:4096],
                                scalar1=k.SWP[0:1, 1:2], scalar2=0.0,
                                op0=OP.mult, op1=OP.bypass)
            escv = k.v("vec")

            @b.gpsimd
            def _(e, escv=escv):
                e.wait_ge(k.sem["vec"], escv)
                k.ldh = k.dma(e, "ldg", esc_d, RS[0:1, 0:4096])
            esc_h = k.ldh

            k.cnt["_convV"] = k.v("vec")
            k.cnt["_pbank0"] = k.v("vec")
            k.cnt["_pbank1"] = k.v("vec")
            k.cnt["_hsG0"] = 0
            k.cnt["_hsG1"] = 0
            k.cnt["_hsV0"] = k.v("vec")
            k.cnt["_hsV1"] = k.v("vec")
            k.cnt["_hsVL"] = k.v("vec")
            k.cnt["_scv0"] = k.v("vec")
            k.cnt["_scv1"] = k.v("vec")
            hs_h = {0: None, 1: None}

            for hb in range(4):
                @b.sync
                def _(e, hb=hb):
                    if hb >= 1:
                        e.wait_ge(k.sem["pe"], k.cnt[f"_hb{hb-1}pe"])
                    if hb == 0:
                        k.dwait_all(e, "st")
                        k.wait(e, "vec")
                    k.ldh = k.dma(e, "ldw", w1blk,
                                  w1q_c[:, :, hb * 1024:(hb + 1) * 1024]
                                  .rearrange("kt p c -> p kt c"))
                w1_h = k.ldh

                for sc_i in range(NSCH):
                    s0 = sc_i * SCH
                    first = (s0 % L) == 0
                    last = ((s0 + SCH) % L) == 0
                    n = 257 if (first or last) else 258
                    off = 0 if first else 1
                    left = s0 if first else s0 - 1

                    @b.sync
                    def _(e, sc_i=sc_i, left=left, n=n, hb=hb, esc_h=esc_h):
                        if sc_i >= 2:
                            e.wait_ge(k.sem["pe"], k.cnt[f"_s{sc_i-2}"])
                            e.wait_ge(k.sem["vec"], k.cnt[f"_scv{sc_i % 2}"])
                        elif hb > 0:
                            e.wait_ge(k.sem["pe"], k.cnt[f"_hb{hb-1}pe"])
                            e.wait_ge(k.sem["vec"], k.cnt[f"_scv{sc_i % 2}"])
                        k.dwait(e, esc_h)
                        k.ldh = k.dma(e, "ldw", xqs3[:, sc_i % 2, :, 0:n],
                                      xqT_spill[:, :, left:left + n]
                                      .rearrange("kt p c -> p kt c"))
                        k.ldh2 = k.dma(e, "ldw", esl2[:, sc_i % 2, 0:n],
                                       esc_d[0:1, left:left + n].unsqueeze(1)
                                       .broadcast_to([1, 128, n]))
                    xq_h, esl_h = k.ldh, k.ldh2

                    for hj in range(8):
                        ht = hb * 8 + hj
                        par = hj % 2

                        @b.tensor
                        def _(e, hj=hj, n=n, sc_i=sc_i, w1_h=w1_h, xq_h=xq_h):
                            k.dwait(e, w1_h)
                            k.dwait(e, xq_h)
                            e.wait_ge(k.sem["vec"], k.cnt[f"_pbank{hj % 2}"])
                            for kt in range(8):
                                ins = e.matmul(out=pb[hj % 2][:, 0:n],
                                               lhsT=w1blk[:, kt, hj * 128:(hj + 1) * 128],
                                               rhs=xqs3[:, sc_i % 2, kt, 0:n],
                                               start=(kt == 0), stop=(kt == 7))
                            k.inc(ins, "pe")
                        mm_done = k.v("pe")
                        vec0c = k.v("vec")

                        @b.vector
                        def _(e, ht=ht, hj=hj, n=n, off=off, first=first,
                              last=last, sc_i=sc_i, mm_done=mm_done, esl_h=esl_h):
                            e = k.V(e)
                            e._e.wait_ge(k.sem["pe"], mm_done)
                            k.dwait(e._e, esl_h)
                            e.tensor_tensor(out=HTI[:, 0:n], in0=pb[hj % 2][:, 0:n],
                                            in1=esl2[:, sc_i % 2, 0:n], op=OP.mult)
                            if first:
                                e.memset(T0[:, 0:1], 0.0)
                                e.tensor_scalar(out=T0[:, 1:256], in0=HTI[:, 0:255],
                                                scalar1=cw4[:, ht, 0:1], scalar2=0.0,
                                                op0=OP.mult, op1=OP.bypass)
                                e.tensor_scalar(out=T0[:], in0=T0[:],
                                                scalar1=cw4[:, ht, 3:4], scalar2=0.0,
                                                op0=OP.add, op1=OP.bypass)
                            else:
                                e.tensor_scalar(out=T0[:], in0=HTI[:, off - 1:off + 255],
                                                scalar1=cw4[:, ht, 0:1], scalar2=cw4[:, ht, 3:4],
                                                op0=OP.mult, op1=OP.add)
                            if last:
                                e.memset(T2[:, 255:256], 0.0)
                                e.tensor_scalar(out=T2[:, 0:255], in0=HTI[:, off + 1:off + 256],
                                                scalar1=cw4[:, ht, 2:3], scalar2=0.0,
                                                op0=OP.mult, op1=OP.bypass)
                            else:
                                e.tensor_scalar(out=T2[:], in0=HTI[:, off + 1:off + 257],
                                                scalar1=cw4[:, ht, 2:3], scalar2=0.0,
                                                op0=OP.mult, op1=OP.bypass)
                            e.scalar_tensor_tensor(out=T0[:], in0=HTI[:, off:off + 256],
                                                   scalar=cw4[:, ht, 1:2], in1=T0[:],
                                                   op0=OP.mult, op1=OP.add)
                            e.tensor_tensor(out=T0[:], in0=T0[:], in1=T2[:], op=OP.add)
                            k.cnt["_convV"] = k.v("vec")
                        k.cnt[f"_pbank{hj % 2}"] = vec0c + 1
                        cv_done = k.v("vec")

                        @b.scalar
                        def _(e, cv_done=cv_done, par=par):
                            e = k.S(e)
                            e._e.wait_ge(k.sem["vec"], cv_done)
                            if hs_h[par] is not None:
                                k.dwait(e._e, hs_h[par])
                            e._e.wait_ge(k.sem["vec"],
                                         max(k.cnt[f"_hsV{par}"], k.cnt["_hsVL"]))
                            e.activation(out=hs2[:, par, :], in_=T0[:], func=AF.Silu)
                            e.activation(out=TABS[:], in_=hs2[:, par, :], func=AF.Abs)
                        si_done = k.v("sc")

                        @b.vector
                        def _(e, si_done=si_done, s0=s0, par=par):
                            e = k.V(e)
                            e._e.wait_ge(k.sem["sc"], si_done)
                            e.tensor_tensor(out=RMAX[:, s0:s0 + 256],
                                            in0=RMAX[:, s0:s0 + 256],
                                            in1=TABS[:], op=OP.max)
                            k.cnt[f"_hsV{par}"] = k.v("vec")
                            k.cnt["_hsVL"] = k.v("vec")

                        @b.gpsimd
                        def _(e, ht=ht, s0=s0, si_done=si_done, par=par):
                            e.wait_ge(k.sem["sc"], si_done)
                            k.ldh = k.dma(e, "st", hs_spill[ht, :, s0:s0 + 256],
                                          hs2[:, par, :])
                        hs_h[par] = k.ldh
                    k.cnt[f"_s{sc_i}"] = k.v("pe")
                    k.cnt[f"_scv{sc_i % 2}"] = k.v("vec")
                k.cnt[f"_hb{hb}pe"] = k.v("pe")

            SMAX = sb("smax", [128, 32], F32)
            SMAXT = sb("smaxt", [128, 128], F32)
            p1v = k.v("vec")
            k.cnt["_p2v"] = p1v
            for j in range(32):
                @b.tensor
                def _(e, j=j, p1v=p1v):
                    if j == 0:
                        e.wait_ge(k.sem["vec"], p1v)
                    else:
                        e.wait_ge(k.sem["vec"], k.cnt["_p2v"])
                    ins = e.transpose(out=pb[6 + (j % 2)][:, 0:128],
                                      in_=RMAX[:, j * 128:(j + 1) * 128], identity=k.ident)
                    k.inc(ins, "pe")
                trd = k.v("pe")

                @b.vector
                def _(e, j=j, trd=trd):
                    e = k.V(e)
                    e._e.wait_ge(k.sem["pe"], trd)
                    e.tensor_reduce(out=SMAX[:, j:j + 1], in_=pb[6 + (j % 2)][:, 0:128],
                                    axis=AXX, op=OP.max)
                    k.cnt["_p2v"] = k.v("vec")
            p2v = k.v("vec")

            @b.tensor
            def _(e, p2v=p2v):
                e.wait_ge(k.sem["vec"], p2v)
                ins = e.transpose(out=pb[6][0:32, 0:128], in_=SMAX[:, 0:32], identity=k.ident)
                k.inc(ins, "pe")
            t2d = k.v("pe")

            @b.vector
            def _(e, t2d=t2d):
                e = k.V(e)
                e._e.wait_ge(k.sem["pe"], t2d)
                e.tensor_copy(out=SMAXT[0:32, 0:128], in_=pb[6][0:32, 0:128])
            cpd = k.v("vec")

            @b.gpsimd
            def _(e, cpd=cpd):
                e.wait_ge(k.sem["vec"], cpd)
                k.ldh = k.dma(e, "ldg", RS[0:1, 0:4096].rearrange("o (g p) -> o g p", g=32),
                              SMAXT[0:32, 0:128])
            pk_h = k.ldh

            @b.vector
            def _(e, pk_h=pk_h):
                e = k.V(e)
                k.dwait(e._e, pk_h)
                e.tensor_scalar(out=RS[0:1, 0:4096], in0=RS[0:1, 0:4096], scalar1=1e-5,
                                scalar2=0.0, op0=OP.max, op1=OP.bypass)
            clipv = k.v("vec")

            @b.gpsimd
            def _(e, clipv=clipv):
                e.wait_ge(k.sem["vec"], clipv)
                k.ldh = k.dma(e, "ldg", s2e_d, RS[0:1, 0:4096])
            clip_h = k.ldh

            @b.vector
            def _(e, clip_h=clip_h):
                e = k.V(e)
                k.dwait(e._e, clip_h)
                e.reciprocal(out=RS[0:1, 0:4096], in_=RS[0:1, 0:4096])
                e.tensor_scalar(out=RS[0:1, 0:4096], in0=RS[0:1, 0:4096], scalar1=127.0,
                                scalar2=0.0, op0=OP.mult, op1=OP.bypass)
            s2v = k.v("vec")

            @b.gpsimd
            def _(e, s2v=s2v):
                e.wait_ge(k.sem["vec"], s2v)
                k.ldh = k.dma(e, "ldg", s2_d, RS[0:1, 0:4096])
                k.dwait(e, k.ldh)
                k.ldh2 = k.dma(e, "ldg", RS[0:1, 0:4096], s2e_d)
            s2_h, eld_h = k.ldh, k.ldh2

            @b.vector
            def _(e, s2_h=s2_h, eld_h=eld_h):
                e = k.V(e)
                k.dwait(e._e, s2_h)
                k.dwait(e._e, eld_h)
                e.tensor_scalar(out=RS[0:1, 0:4096], in0=RS[0:1, 0:4096],
                                scalar1=1.0 / 127.0, scalar2=0.0, op0=OP.mult, op1=OP.bypass)
                e.tensor_scalar(out=RS[0:1, 0:4096], in0=RS[0:1, 0:4096],
                                scalar1=k.SWP[0:1, 3:4], scalar2=0.0, op0=OP.mult, op1=OP.bypass)
            ev2 = k.v("vec")

            @b.gpsimd
            def _(e, ev2=ev2):
                e.wait_ge(k.sem["vec"], ev2)
                k.ldh = k.dma(e, "ldg", s2e_d, RS[0:1, 0:4096])
            s2e_h = k.ldh

        fence = _snap(k)
        W2R = ps.enter_context(nc.sbuf_tensor("w2r" + tag, [128, HT * 1024], BF16))
        HSL = ps.enter_context(nc.sbuf_tensor("hsl" + tag, [128, 2 * 256], F32))
        HQ = ps.enter_context(nc.sbuf_tensor("hq" + tag, [128, 2 * 256], BF16))
        S2L = ps.enter_context(nc.sbuf_tensor("s2l" + tag, [128, 2 * 256], F32))
        S2E = ps.enter_context(nc.sbuf_tensor("s2e" + tag, [128, 2 * 256], F32))
        YTMP = ps.enter_context(nc.sbuf_tensor("ytmp" + tag, [128, 256], F32))
        w2r3 = W2R[:].rearrange("p (ht c) -> p ht c", ht=HT)
        hsl2 = HSL[:].rearrange("p (u c) -> p u c", u=2)
        hq2 = HQ[:].rearrange("p (u c) -> p u c", u=2)
        s2l2 = S2L[:].rearrange("p (u c) -> p u c", u=2)
        s2e2 = S2E[:].rearrange("p (u c) -> p u c", u=2)

        @b.sync
        def _(e):
            _bar(k, e, fence)
            k.ldh = k.dma(e, "ldw", w2r3, w2q_c.rearrange("kt p c -> p kt c"))
        w2_h = k.ldh

        k.cnt["_p3res0"] = k.v("vec")
        k.cnt["_p3res1"] = k.v("vec")
        k.cnt["_p3ev"] = k.v("vec")
        k.cnt["_p3q0"] = k.v("vec")
        k.cnt["_p3q1"] = k.v("vec")
        hl_h = {0: None, 1: None}
        sl_h = {0: None, 1: None}

        for sc_i in range(NSCH):
            s0 = sc_i * SCH
            par = sc_i % 2

            @b.sync
            def _(e, s0=s0, par=par, sc_i=sc_i, s2_h=s2_h, s2e_h=s2e_h):
                if sc_i >= 2:
                    e.wait_ge(k.sem["vec"], k.cnt[f"_p3res{par}"])
                k.dwait(e, s2_h)
                k.dwait(e, s2e_h)
                k.ldh = k.dma(e, "ldw", s2l2[:, par, :],
                              s2_d[0:1, s0:s0 + 256].unsqueeze(1).broadcast_to([1, 128, 256]))
                k.ldh2 = k.dma(e, "ldw", s2e2[:, par, :],
                               s2e_d[0:1, s0:s0 + 256].unsqueeze(1).broadcast_to([1, 128, 256]))
            sl_h[par] = (k.ldh, k.ldh2)

            for ht in range(HT):
                hpar = ht % 2

                @b.sync
                def _(e, ht=ht, s0=s0, hpar=hpar, sc_i=sc_i):
                    e.wait_ge(k.sem["vec"], k.cnt[f"_p3q{hpar}"])
                    k.ldh = k.dma(e, "ldw", hsl2[:, hpar, :], hs_spill[ht, :, s0:s0 + 256])
                h_h = k.ldh

                @b.vector
                def _(e, h_h=h_h, hpar=hpar, par=par, ht=ht, sc_i=sc_i):
                    e = k.V(e)
                    k.dwait(e._e, h_h)
                    for hh in sl_h[par]:
                        k.dwait(e._e, hh)
                    if ht >= 2:
                        e._e.wait_ge(k.sem["pe"], k.cnt[f"_p3mm{hpar}"])
                    e.tensor_tensor(out=hsl2[:, hpar, :], in0=hsl2[:, hpar, :],
                                    in1=s2l2[:, par, :], op=OP.mult)
                    e.tensor_scalar(out=hsl2[:, hpar, :], in0=hsl2[:, hpar, :],
                                    scalar1=MAGIC, scalar2=MAGIC, op0=OP.add, op1=OP.subtract)
                    e.tensor_scalar(out=hsl2[:, hpar, :], in0=hsl2[:, hpar, :],
                                    scalar1=-128.0, scalar2=127.0, op0=OP.max, op1=OP.min)
                    e.tensor_copy(out=hq2[:, hpar, :], in_=hsl2[:, hpar, :])
                    k.cnt[f"_p3q{hpar}"] = k.v("vec")
                q_rdy = k.v("vec")

                @b.tensor
                def _(e, ht=ht, hpar=hpar, q_rdy=q_rdy, w2_h=w2_h):
                    k.dwait(e, w2_h)
                    e.wait_ge(k.sem["vec"], q_rdy)
                    if ht == 0:
                        e.wait_ge(k.sem["vec"], k.cnt["_p3ev"])
                    for dc in range(8):
                        ins = e.matmul(out=pb[dc][:, 0:256],
                                       lhsT=w2r3[:, ht, dc * 128:(dc + 1) * 128],
                                       rhs=hq2[:, hpar, :],
                                       start=(ht == 0), stop=(ht == HT - 1))
                    k.inc(ins, "pe")
                    k.cnt[f"_p3mm{hpar}"] = k.v("pe")
            mm2_done = k.v("pe")

            for dc in range(8):
                @b.vector
                def _(e, dc=dc, par=par, mm2_done=mm2_done):
                    e = k.V(e)
                    e._e.wait_ge(k.sem["pe"], mm2_done)
                    e.tensor_tensor(out=YTMP[:], in0=pb[dc][:, 0:256],
                                    in1=s2e2[:, par, :], op=OP.mult)
                yv = k.v("vec")
                g0 = s0 // 128

                @b.tensor
                def _(e, yv=yv, dc=dc):
                    e.wait_ge(k.sem["vec"], yv)
                    for j in range(2):
                        ins = e.transpose(out=pb[dc][:, j * 128:(j + 1) * 128],
                                          in_=YTMP[:, j * 128:(j + 1) * 128],
                                          identity=k.ident)
                    k.inc(ins, "pe")
                tv = k.v("pe")

                @b.vector
                def _(e, tv=tv, dc=dc, g0=g0):
                    e = k.V(e)
                    e._e.wait_ge(k.sem["pe"], tv)
                    for j in range(2):
                        e.scalar_tensor_tensor(
                            out=k.x3[:, g0 + j, dc * 128:(dc + 1) * 128],
                            in0=pb[dc][:, j * 128:(j + 1) * 128], scalar=0.5,
                            in1=k.x3[:, g0 + j, dc * 128:(dc + 1) * 128],
                            op0=OP.mult, op1=OP.add)
                k.cnt["_p3ev"] = k.v("vec")
            k.cnt[f"_p3res{par}"] = k.v("vec")


def phase_l0ffn(k, call):
    nc, b, t = k.nc, k.block, k.t
    pb = k.pb
    H = 16
    if "lnl0_spill" not in k.dram:
        k.dram["lnl0_spill"] = nc.dram_tensor("lnl0_spill", [2, 16, 128, D], BF16)
        k.dram["vtok_spill"] = nc.dram_tensor("vtok_spill", [16, 16, 128, 64], BF16)
    lnl0 = k.dram["lnl0_spill"].ap()
    vtok = k.dram["vtok_spill"].ap()
    gag_i = k.dram["g_ag_in"].ap()
    gag_o = k.dram["g_ag_out"].ap()
    aar_i = k.dram["att_ar_in"].ap()
    aar_o = k.dram["att_ar_out"].ap()

    with ExitStack() as ps:
        sb = lambda nm, sh, dt: ps.enter_context(
            nc.sbuf_tensor(f"{nm}g{call}", sh, dt))
        MKB = sb("mkb", [128, 2 * 16 * 16], F32)
        ATT = sb("att", [128, 256], F32)
        PS = sb("ps", [128, 256], F32)
        OWNC = sb("ownc", [128, 2], F32)
        WGB = sb("wgb", [128, 8 * 16], F32)
        WOB = sb("wob", [128, 8 * 1024], BF16)
        LNGB = sb("lngb", [128, D], F32)
        LNBB = sb("lnbb", [128, D], F32)
        BOB = sb("bob", [128, D], F32)
        IOT = sb("iot", [128, 2 * 128], F32)
        mk3 = MKB[:].rearrange("p (s g h) -> p s g h", s=2, g=16)
        att3 = ATT[:].rearrange("p (g h) -> p g h", g=16)
        ps3 = PS[:].rearrange("p (g h) -> p g h", g=16)
        wg3 = WGB[:].rearrange("p (dc h) -> p dc h", dc=8)
        wo3 = WOB[:].rearrange("p (kt d) -> p kt d", kt=8)

        _fence(k)

        @b.sync
        def _(e):
            k.dma(e, "ldw", WGB[:].rearrange("p (dc h) -> p dc h", dc=8),
                  t["Wg"].rearrange("dc p h -> p dc h"))
            k.dma(e, "ldw", LNGB[:], t["lng_rep"])
            k.dma(e, "ldw", LNBB[:], t["lnb_rep"])
            k.dma(e, "ldw", BOB[:], t["bo_rep"])
            k.ldh = k.dma(e, "ldw", IOT[:].rearrange("p (s x) -> p s x", s=2),
                          t["iotas"].rearrange("s p c h -> p s (c h)"))
        iot_h = k.ldh

        @b.gpsimd
        def _(e):
            k.dma(e, "ldg", OWNC[:, 0:1],
                  t["own0"].unsqueeze(1).broadcast_to([1, 128, 1]))
            k.ldh = k.dma(e, "ldg", OWNC[:, 1:2],
                          t["own1"].unsqueeze(1).broadcast_to([1, 128, 1]))
            k.ldh2 = k.dma(e, "ldg", WOB[:].rearrange("p (kt d) -> p kt d", kt=8),
                           t["Wo"].rearrange("(kt p) d -> p kt d", p=128))
        own_h, wo_h = k.ldh, k.ldh2

        sc_rdy = ln_stats_all(k)

        with ExitStack() as s1:
            sb1 = lambda nm, sh, dt: s1.enter_context(
                nc.sbuf_tensor(f"{nm}g{call}", sh, dt))
            LNA = sb1("lna", [128, 2 * D], F32)
            XLT = sb1("xlt", [128, 2 * 8 * 128], F32)
            GSB = sb1("gsb", [16, 128], F32)
            _fence(k)
            gv_rd = 0
            lnw_p = {0: 0, 1: 0}
            gmm_p = {0: 0, 1: 0}
            for g in range(NTILE):
                s2, ti = g // 16, g % 16
                par = g % 2
                lna = LNA[:, par * D:(par + 1) * D]
                xlt = XLT[:, par * 1024:(par + 1) * 1024]

                @b.vector
                def _(e, g=g, lnw=lnw_p[par], sc_rdy=sc_rdy, lna=lna):
                    e = k.V(e)
                    e._e.wait_ge(k.sem["sc"], sc_rdy)
                    e._e.wait_ge(k.sem["pe"], lnw)
                    k.dwait_all(e._e, "st")
                    e.tensor_scalar(out=lna, in0=k.x3[:, g, :],
                                    scalar1=k.st3[:, g, 0:1], scalar2=k.st3[:, g, 1:2],
                                    op0=OP.subtract, op1=OP.mult)
                cst = k.v("vec")
                tr_chain(k, [(lna[:, dc * 128:(dc + 1) * 128],
                              xlt[:, dc * 128:(dc + 1) * 128])
                             for dc in range(8)],
                         wait_before=[("vec", cst), ("pe", gmm_p[par])])
                tr_v = k.v("vec")
                lnw_p[par] = k.v("pe")

                @b.gpsimd
                def _(e, s2=s2, ti=ti, cst=cst, lna=lna):
                    e.wait_ge(k.sem["vec"], cst)
                    k.dma(e, "st", lnl0[s2, ti], lna)

                @b.tensor
                def _(e, tr_v=tr_v, gv_rd=gv_rd, xlt=xlt):
                    e.wait_ge(k.sem["vec"], tr_v)
                    e.wait_ge(k.sem["vec"], gv_rd)
                    for dc in range(8):
                        ins = e.matmul(out=pb[2][0:16, 0:128],
                                       lhsT=wg3[:, dc, :],
                                       rhs=xlt[:, dc * 128:(dc + 1) * 128],
                                       start=(dc == 0), stop=(dc == 7))
                    k.inc(ins, "pe")
                gmm = k.v("pe")
                gmm_p[par] = gmm

                @b.vector
                def _(e, gmm=gmm):
                    e = k.V(e)
                    e._e.wait_ge(k.sem["pe"], gmm)
                    k.dwait_all(e._e, "sth")
                    e.tensor_copy(out=GSB[0:16, :], in_=pb[2][0:16, 0:128])
                gv_rd = k.v("vec")

                @b.sync
                def _(e, s2=s2, ti=ti, gv_rd=gv_rd):
                    e.wait_ge(k.sem["vec"], gv_rd)
                    k.dma(e, "sth", gag_i[:, s2 * 2048 + ti * 128:
                                          s2 * 2048 + (ti + 1) * 128], GSB[0:16, :])

        @b.gpsimd
        def _(e):
            k.dwait_all(e, "sth")
            k.dwait_all(e, "st")
            ins = e.collective_compute(
                "AllGather", OP.bypass, replica_groups=[list(range(8))],
                ins=[gag_i.opt()], outs=[gag_o.opt()])
            k.inc(ins, "cc")
        ag_done = k.v("cc")

        with ExitStack() as s2s:
            sb2 = lambda nm, sh, dt: s2s.enter_context(
                nc.sbuf_tensor(f"{nm}g{call}", sh, dt))
            GF = sb2("gf", [128, 128], F32)
            SCR = sb2("scr", [128, 2 * 128], F32)
            ESC = sb2("esc", [128, 128], F32)
            MC = sb2("mc", [128, 8], F32)
            NM = sb2("nm", [128, 8], F32)
            SC = sb2("scs", [128, 8], F32)
            KC = sb2("kc", [128, 8], F32)
            MH = sb2("mh", [128, 2 * 16], F32)
            PMT = sb2("pmt", [128, 16], F32)
            EQ = sb2("eq", [128, 128], F32)
            CM = sb2("cm", [128, 2 * 16], F32)
            _fence(k)
            gf_h = None
            for ti in range(16):
                for s2 in range(2):
                    vnow = k.v("vec")

                    @b.sync
                    def _(e, s2=s2, ti=ti, vnow=vnow, ag_done=ag_done):
                        k.wait(e, "cc", ag_done)
                        e.wait_ge(k.sem["vec"], vnow)
                        k.ldh = k.dma(
                            e, "ldw", GF[:],
                            gag_o[:, s2 * 2048 + ti * 128:s2 * 2048 + (ti + 1) * 128])
                    gf_h = k.ldh
                    tr_chain(k, [(GF[:], SCR[:, s2 * 128:(s2 + 1) * 128])],
                             wait_before=[(gf_h[0], gf_h[1]), ("sc", k.v("sc"))])
                    scr3 = SCR[:, s2 * 128:(s2 + 1) * 128].rearrange(
                        "p (c h) -> p c h", c=8)

                    @b.vector
                    def _(e, scr3=scr3):
                        e = k.V(e)
                        for c in range(8):
                            e.tensor_reduce(out=MC[:, c:c + 1], in_=scr3[:, c, :],
                                            axis=AXX, op=OP.max)
                        e.tensor_scalar(out=NM[:], in0=MC[:], scalar1=-1.0,
                                        scalar2=0.0, op0=OP.mult, op1=OP.bypass)
                    nm_v = k.v("vec")

                    @b.scalar
                    def _(e, scr3=scr3, nm_v=nm_v):
                        e = k.S(e)
                        e._e.wait_ge(k.sem["vec"], nm_v)
                        esc3 = ESC[:].rearrange("p (c h) -> p c h", c=8)
                        for c in range(8):
                            e.activation(out=esc3[:, c, :], in_=scr3[:, c, :],
                                         func=AF.Exp, bias=NM[:, c:c + 1])
                    ex_v = k.v("sc")

                    @b.vector
                    def _(e, ex_v=ex_v):
                        e = k.V(e)
                        e._e.wait_ge(k.sem["sc"], ex_v)
                        esc3 = ESC[:].rearrange("p (c h) -> p c h", c=8)
                        for c in range(8):
                            e.tensor_reduce(out=SC[:, c:c + 1], in_=esc3[:, c, :],
                                            axis=AXX, op=OP.add)
                    sc_v = k.v("vec")

                    @b.scalar
                    def _(e, sc_v=sc_v):
                        e = k.S(e)
                        e._e.wait_ge(k.sem["vec"], sc_v)
                        e.activation(out=SC[:], in_=SC[:], func=AF.Ln)
                    ln_v = k.v("sc")

                    @b.vector
                    def _(e, s2=s2, scr3=scr3, ln_v=ln_v):
                        e = k.V(e)
                        e._e.wait_ge(k.sem["sc"], ln_v)
                        e.tensor_tensor(out=KC[:], in0=MC[:], in1=SC[:], op=OP.add)
                        e.tensor_tensor(out=scr3, in0=scr3,
                                        in1=KC[:].unsqueeze(2).broadcast_to([128, 8, 16]),
                                        op=OP.subtract)
                        for h in range(16):
                            e.tensor_reduce(out=MH[:, s2 * 16 + h:s2 * 16 + h + 1],
                                            in_=scr3[:, :, h], axis=AXX, op=OP.max)

                @b.vector
                def _(e, ti=ti, iot_h=iot_h, own_h=own_h):
                    e = k.V(e)
                    k.dwait(e._e, iot_h)
                    k.dwait(e._e, own_h)
                    e.tensor_tensor(out=PMT[:], in0=MH[:, 0:16], in1=MH[:, 16:32],
                                    op=OP.max)
                    for s2 in range(2):
                        scr3 = SCR[:, s2 * 128:(s2 + 1) * 128].rearrange(
                            "p (c h) -> p c h", c=8)
                        eq3 = EQ[:].rearrange("p (c h) -> p c h", c=8)
                        e.tensor_tensor(
                            out=eq3, in0=scr3,
                            in1=PMT[:].unsqueeze(1).broadcast_to([128, 8, 16]),
                            op=OP.is_equal)
                        e.tensor_tensor(out=EQ[:], in0=EQ[:],
                                        in1=IOT[:, s2 * 128:(s2 + 1) * 128],
                                        op=OP.mult)
                        for h in range(16):
                            e.tensor_reduce(
                                out=CM[:, s2 * 16 + h:s2 * 16 + h + 1],
                                in_=eq3[:, :, h], axis=AXX, op=OP.min)
                    e.tensor_tensor(out=PMT[:], in0=CM[:, 0:16], in1=CM[:, 16:32],
                                    op=OP.min)
                    e.tensor_scalar(out=PMT[:], in0=PMT[:], scalar1=16.0,
                                    scalar2=0.0, op0=OP.add, op1=OP.bypass)
                    for s2 in range(2):
                        e.tensor_scalar(out=mk3[:, s2, ti, :], in0=PMT[:],
                                        scalar1=OWNC[:, s2:s2 + 1], scalar2=0.0,
                                        op0=OP.is_equal, op1=OP.bypass)

        if "wqb_cache" not in k.dram:
            k.dram["wqb_cache"] = nc.dram_tensor("wqb_cache", [16, 128, 8, 64], BF16)
            k.dram["wkvb_cache"] = nc.dram_tensor("wkvb_cache", [16, 128, 8, 128], BF16)

            @b.gpsimd
            def _(e):
                for h in range(16):
                    k.dma(e, "st", k.dram["wqb_cache"].ap()[h],
                          t["Wq"][h].rearrange("dc p e -> p dc e"))
                    k.dma(e, "st", k.dram["wkvb_cache"].ap()[h],
                          t["Wkv"][h].rearrange("dc p e -> p dc e"))
        wqc = k.dram["wqb_cache"].ap()
        wkvc = k.dram["wkvb_cache"].ap()

        with ExitStack() as s3a:
            sb3 = lambda nm, sh, dt: s3a.enter_context(
                nc.sbuf_tensor(f"{nm}g{call}", sh, dt))
            LNSLAB = sb3("lnsl", [128, 2 * 4 * 8 * 128], BF16)
            HGT = sb3("hgt", [128, 8 * 512], BF16)
            DDB = sb3("ddb", [128, 8 * 128], BF16)
            QTE = sb3("qte", [64, 512], BF16)
            KVTE = sb3("kvte", [128, 512], BF16)
            QTOK = sb3("qtok", [128, 4 * 64], BF16)
            TTRS = sb3("ttrs", [128, 64], F32)
            KTOK = sb3("ktok", [128, 4 * 64], BF16)
            VTOK = sb3("vtokb", [128, 4 * 64], BF16)
            WQB = sb3("wqb", [128, 2 * 8 * 64], BF16)
            WKVB = sb3("wkvb", [128, 2 * 8 * 128], BF16)
            BQC = sb3("bqc", [64, 16], F32)
            BKVC = sb3("bkvc", [128, 16], F32)
            MSK = sb3("msk", [128, 16], F32)
            ln5 = LNSLAB[:].rearrange("p (s g dc c) -> p s g dc c", s=2, g=4, dc=8)
            hgt3 = HGT[:].rearrange("p (dc c) -> p dc c", dc=8)
            dd3 = DDB[:].rearrange("p (x c) -> p x c", x=8)
            wq4 = WQB[:].rearrange("p (u dc e) -> p u dc e", u=2, dc=8)
            wkv4 = WKVB[:].rearrange("p (u dc e) -> p u dc e", u=2, dc=8)
            _fence(k)
            sc_bank = {0: 0, 1: 0, 2: 0, 3: 0}
            vec_bank = {0: 0, 1: 0}
            hgt_rd = 0
            dd_rd = 0
            qte_rd = 0
            tb_rd = {0: 0, 1: 0}
            wqb_rd = {0: 0, 1: 0}
            wq_h = {0: None, 1: None}
            vt_st = None

            @b.sync
            def _(e):
                k.dwait_all(e, "st")
                k.ldh = k.dma(e, "ldw", BQC[0:64, :].unsqueeze(2),
                              t["bqT"].rearrange("h p o -> p h o"),
                              allow_slow_non_contiguous=True)
                k.ldh2 = k.dma(e, "ldw", BKVC[:, :].unsqueeze(2),
                               t["bkvT"].rearrange("h p o -> p h o"),
                               allow_slow_non_contiguous=True)
                k.ldh3 = k.dma(e, "ldw", wq4[:, 0], wqc[0])
                k.ldh4 = k.dma(e, "ldw", wkv4[:, 0], wkvc[0])
            bq_h = (k.ldh, k.ldh2)
            wq_h[0] = (k.ldh3, k.ldh4)

            for rc in range(4):
                penow = k.v("pe")

                @b.sync
                def _(e, rc=rc, penow=penow):
                    e.wait_ge(k.sem["pe"], penow)
                    k.lhs = {}
                    for s2 in range(2):
                        for ti2 in range(4):
                            k.lhs[(s2, ti2)] = k.dma(
                                e, "ldw", ln5[:, s2, ti2],
                                lnl0[s2, rc * 4 + ti2]
                                .rearrange("p (dc c) -> p dc c", c=128))
                ls_hs = k.lhs

                for hh in range(16):
                    if not (rc == 3 and hh == 15):
                        nh = (hh + 1) % 16

                        @b.sync
                        def _(e, nh=nh, wrd=wqb_rd[(hh + 1) % 2]):
                            e.wait_ge(k.sem["pe"], wrd)
                            k.ldh = k.dma(e, "ldw", wq4[:, nh % 2], wqc[nh])
                            k.ldh2 = k.dma(e, "ldw", wkv4[:, nh % 2], wkvc[nh])
                        wq_h[(hh + 1) % 2] = (k.ldh, k.ldh2)

                    @b.gpsimd
                    def _(e, rc=rc, hh=hh, dd_rd=dd_rd):
                        e.wait_ge(k.sem["pe"], dd_rd)
                        g = k.G(e)
                        for ti2 in range(4):
                            for s2 in range(2):
                                g.tensor_scalar(
                                    out=dd3[:, ti2 * 2 + s2, :], in0=k.identb,
                                    scalar1=mk3[:, s2, rc * 4 + ti2, hh:hh + 1],
                                    scalar2=0.0, op0=OP.mult, op1=OP.bypass)
                    dd_v = k.v("gp")

                    pe0g = k.v("pe")
                    sc0g = k.v("sc")
                    vc0g = k.v("vec")

                    @b.tensor
                    def _(e, hh=hh, dd_v=dd_v, ls_hs=ls_hs, wqh=wq_h[hh % 2],
                          sb0=sc_bank[0], sb1=sc_bank[1], vb0=vec_bank[0],
                          vb1=vec_bank[1],
                          sb2w=sc_bank[2], sb3w=sc_bank[3], sc0g=sc0g,
                          vc0g=vc0g):
                        e.wait_ge(k.sem["gp"], dd_v)
                        for hd in ls_hs.values():
                            k.dwait(e, hd)
                        for dc in range(8):
                            if dc == 0:
                                e.wait_ge(k.sem["sc"], sb0)
                                e.wait_ge(k.sem["vec"], vb0)
                            elif dc == 1:
                                e.wait_ge(k.sem["sc"], sb1)
                                e.wait_ge(k.sem["vec"], vb1)
                            elif dc < 6:
                                e.wait_ge(k.sem["sc"], sc0g + dc - 1)
                            else:
                                e.wait_ge(k.sem["vec"], vc0g + dc - 5)
                            first = True
                            for ti2 in range(4):
                                for s2 in range(2):
                                    ins = e.matmul(
                                        out=pb[dc % 2][:, ti2 * 128:(ti2 + 1) * 128],
                                        lhsT=ln5[:, s2, ti2, dc, :],
                                        rhs=dd3[:, ti2 * 2 + s2, :],
                                        start=first,
                                        stop=(ti2 == 3 and s2 == 1))
                                    first = False
                            k.inc(ins, "pe")
                        k.dwait(e, wqh[0])
                        k.dwait(e, wqh[1])
                        e.wait_ge(k.sem["sc"], sc0g + 8)
                        e.wait_ge(k.sem["sc"], sb2w)
                        for dc in range(8):
                            ins = e.matmul(out=pb[2][0:64, 0:512],
                                           lhsT=wq4[:, hh % 2, dc, :],
                                           rhs=hgt3[:, dc, :],
                                           start=(dc == 0), stop=(dc == 7))
                        k.inc(ins, "pe")
                        e.wait_ge(k.sem["sc"], sb3w)
                        for dc in range(8):
                            ins = e.matmul(out=pb[3][0:128, 0:512],
                                           lhsT=wkv4[:, hh % 2, dc, :],
                                           rhs=hgt3[:, dc, :],
                                           start=(dc == 0), stop=(dc == 7))
                        k.inc(ins, "pe")

                    @b.scalar
                    def _(e, hh=hh, pe0g=pe0g, hgt_rd=hgt_rd, qte_rd=qte_rd,
                          bq_h=bq_h, vc0g=vc0g):
                        e = k.S(e)
                        e._e.wait_ge(k.sem["pe"], hgt_rd)
                        for dc in range(4):
                            e._e.wait_ge(k.sem["pe"], pe0g + dc + 1)
                            e.activation(out=hgt3[:, dc, :],
                                         in_=pb[dc % 2][:, 0:512], func=AF.Copy)
                        e._e.wait_ge(k.sem["vec"], vc0g + 4)
                        for dc in range(4, 8):
                            e.activation(out=hgt3[:, 0, 0:1],
                                         in_=hgt3[:, 0, 0:1], func=AF.Copy)
                        e._e.wait_ge(k.sem["pe"], qte_rd)
                        k.dwait(e._e, bq_h[0])
                        k.dwait(e._e, bq_h[1])
                        e._e.wait_ge(k.sem["pe"], pe0g + 9)
                        e.activation(out=QTE[0:64, :], in_=pb[2][0:64, 0:512],
                                     func=AF.Identity, bias=BQC[0:64, hh:hh + 1])
                        e._e.wait_ge(k.sem["pe"], pe0g + 10)
                        e.activation(out=KVTE[:], in_=pb[3][:, 0:512],
                                     func=AF.Identity, bias=BKVC[:, hh:hh + 1])

                    @b.vector
                    def _(e, pe0g=pe0g, hgt_rd=hgt_rd, vt_st=vt_st):
                        e = k.V(e)
                        e._e.wait_ge(k.sem["pe"], hgt_rd)
                        for dc in range(4, 8):
                            e._e.wait_ge(k.sem["pe"], pe0g + dc + 1)
                            e.tensor_copy(out=hgt3[:, dc, :],
                                          in_=pb[dc % 2][:, 0:512])
                    assert k.v("pe") == pe0g + 10
                    assert k.v("sc") == sc0g + 10
                    assert k.v("vec") == vc0g + 4
                    dd_rd = pe0g + 8
                    hgt_rd = pe0g + 10
                    wqb_rd[hh % 2] = pe0g + 10
                    sc_bank[0] = sc0g + 3
                    sc_bank[1] = sc0g + 4
                    vec_bank[0] = vc0g + 3
                    vec_bank[1] = vc0g + 4
                    sc_bank[2] = sc0g + 9
                    sc_bank[3] = sc0g + 10
                    qkv_sc = k.v("sc")

                    pe0 = k.v("pe")
                    vec0v = k.v("vec")

                    @b.tensor
                    def _(e, qkv_sc=qkv_sc, vec0v=vec0v,
                          tb0=tb_rd[0], tb1=tb_rd[1]):
                        e.wait_ge(k.sem["sc"], qkv_sc)
                        for i in range(12):
                            src, ti2 = i % 3, i // 3
                            if i == 0:
                                e.wait_ge(k.sem["vec"], tb0)
                            elif i == 1:
                                e.wait_ge(k.sem["vec"], tb1)
                            else:
                                e.wait_ge(k.sem["vec"], vec0v + i - 1)
                            if src == 0:
                                in_ = QTE[0:64, ti2 * 128:(ti2 + 1) * 128]
                                idn = k.identb[0:64, 0:64]
                            elif src == 1:
                                in_ = KVTE[0:64, ti2 * 128:(ti2 + 1) * 128]
                                idn = k.identb[0:64, 0:64]
                            else:
                                in_ = KVTE[64:128, ti2 * 128:(ti2 + 1) * 128]
                                idn = k.identb[64:128, 64:128]
                            ins = e.transpose(
                                out=pb[4 + i % 2][:, 0:32].bitcast(BF16),
                                in_=in_, identity=idn)
                            k.inc(ins, "pe")
                    qte_rd = k.v("pe")

                    @b.vector
                    def _(e, hh=hh, rc=rc, pe0=pe0, vt_st=vt_st):
                        e = k.V(e)
                        if vt_st is not None:
                            k.dwait(e._e, vt_st)
                        for i in range(12):
                            src, ti2 = i % 3, i // 3
                            e._e.wait_ge(k.sem["pe"], pe0 + i + 1)
                            dst = (QTOK, KTOK, VTOK)[src]
                            e.tensor_copy(
                                out=dst[:, ti2 * 64:(ti2 + 1) * 64],
                                in_=pb[4 + i % 2][:, 0:32].bitcast(BF16))
                        for ti2 in range(4):
                            e.tensor_tensor(
                                out=TTRS[:],
                                in0=QTOK[:, ti2 * 64:(ti2 + 1) * 64],
                                in1=KTOK[:, ti2 * 64:(ti2 + 1) * 64], op=OP.mult)
                            e.tensor_reduce(
                                out=att3[:, rc * 4 + ti2, hh:hh + 1],
                                in_=TTRS[:], axis=AXX, op=OP.add)
                    tb_rd[0] = k.v("vec")
                    tb_rd[1] = k.v("vec")
                    tok_v = k.v("vec")

                    @b.gpsimd
                    def _(e, hh=hh, rc=rc, tok_v=tok_v):
                        e.wait_ge(k.sem["vec"], tok_v)
                        k.ldh = k.dma(e, "st",
                                      vtok[hh, rc * 4:(rc + 1) * 4]
                                      .rearrange("g p e -> p g e"),
                                      VTOK[:].rearrange("p (g e) -> p g e", g=4))
                    vt_st = k.ldh

            @b.vector
            def _(e):
                e = k.V(e)
                for ti in range(16):
                    e.tensor_tensor(out=MSK[:], in0=mk3[:, 0, ti, :],
                                    in1=mk3[:, 1, ti, :], op=OP.add)
                    e.scalar_tensor_tensor(out=att3[:, ti, :], in0=att3[:, ti, :],
                                           scalar=0.125, in1=MSK[:],
                                           op0=OP.mult, op1=OP.mult)
            attm_v = k.v("vec")

            @b.sync
            def _(e, attm_v=attm_v):
                e.wait_ge(k.sem["vec"], attm_v)
                k.ldh = k.dma(e, "sth", aar_i, ATT[:])
            ar_st = k.ldh

            @b.gpsimd
            def _(e, ar_st=ar_st):
                k.dwait(e, ar_st)
                ins = e.collective_compute(
                    "AllReduce", OP.add, replica_groups=[list(range(8))],
                    ins=[aar_i.opt()], outs=[aar_o.opt()])
                k.inc(ins, "cc")
            ar_done = k.v("cc")

            @b.sync
            def _(e, ar_done=ar_done):
                k.wait(e, "cc", ar_done)
                k.ldh = k.dma(e, "ldw", ATT[:], aar_o)
            att_h = k.ldh

        with ExitStack() as s3b:
            sb4 = lambda nm, sh, dt: s3b.enter_context(
                nc.sbuf_tensor(f"{nm}g{call}", sh, dt))
            MXC = sb4("mxc", [128, 16], F32)
            SMC = sb4("smc", [128, 16], F32)
            MREP = sb4("mrep", [128, 16], F32)
            LREP = sb4("lrep", [128, 16], F32)
            MCOL = sb4("mcol", [16, 1], F32)
            MROW = sb4("mrow", [1, 16], F32)
            LROW = sb4("lrow", [1, 16], F32)
            AVST0 = sb4("avst0", [128, 8 * 128], BF16)
            AVST1 = sb4("avst1", [128, 8 * 128], BF16)
            VTR = sb4("vtr", [128, 16 * 64], BF16)
            PMS = sb4("pms", [128, 32], F32)
            AVH = sb4("avh", [128, 2 * 64], BF16)
            XLN = sb4("xln", [128, D], F32)
            YB = sb4("yb", [128, D], F32)
            NSC = sb4("nsc", [128, 2], F32)
            YST = sb4("yst", [128, 2], F32)
            vtr3 = VTR[:].rearrange("p (h e) -> p h e", h=16)
            _fence(k)

            @b.vector
            def _(e, att_h=att_h):
                e = k.V(e)
                k.dwait(e._e, att_h)
                for h in range(16):
                    e.tensor_reduce(out=MXC[:, h:h + 1], in_=att3[:, :, h],
                                    axis=AXX, op=OP.max)
            mx_v = k.v("vec")

            @b.tensor
            def _(e, mx_v=mx_v):
                e.wait_ge(k.sem["vec"], mx_v)
                ins = e.transpose(out=pb[6][0:16, 0:128], in_=MXC[:],
                                  identity=k.ident)
                k.inc(ins, "pe")
            t1_v = k.v("pe")

            @b.vector
            def _(e, t1_v=t1_v):
                e = k.V(e)
                e._e.wait_ge(k.sem["pe"], t1_v)
                e.tensor_reduce(out=MCOL[0:16, :], in_=pb[6][0:16, 0:128],
                                axis=AXX, op=OP.max)
            mc_v = k.v("vec")

            @b.tensor
            def _(e, mc_v=mc_v):
                e.wait_ge(k.sem["vec"], mc_v)
                ins = e.transpose(out=pb[7][0:1, 0:16], in_=MCOL[0:16, :],
                                  identity=k.ident[0:16, 0:16])
                k.inc(ins, "pe")
            t2_v = k.v("pe")

            @b.vector
            def _(e, t2_v=t2_v):
                e = k.V(e)
                e._e.wait_ge(k.sem["pe"], t2_v)
                e.tensor_copy(out=MROW[0:1, :], in_=pb[7][0:1, 0:16])
            mr_v = k.v("vec")

            @b.gpsimd
            def _(e, mr_v=mr_v):
                e.wait_ge(k.sem["vec"], mr_v)
                k.ldh = k.dma(e, "ldg", MREP[:],
                              MROW[0:1, :].unsqueeze(1).broadcast_to([1, 128, 16]))
            mrep_h = k.ldh

            @b.vector
            def _(e, mrep_h=mrep_h):
                e = k.V(e)
                k.dwait(e._e, mrep_h)
                for ti in range(16):
                    e.tensor_tensor(out=ps3[:, ti, :], in0=att3[:, ti, :],
                                    in1=MREP[:], op=OP.subtract)
            sub_v = k.v("vec")

            @b.scalar
            def _(e, sub_v=sub_v):
                e = k.S(e)
                e._e.wait_ge(k.sem["vec"], sub_v)
                for ti in range(16):
                    e.activation(out=PS[:, ti * 16:(ti + 1) * 16],
                                 in_=PS[:, ti * 16:(ti + 1) * 16], func=AF.Exp)
            exp_v = k.v("sc")

            @b.vector
            def _(e, exp_v=exp_v):
                e = k.V(e)
                e._e.wait_ge(k.sem["sc"], exp_v)
                for h in range(16):
                    e.tensor_reduce(out=SMC[:, h:h + 1], in_=ps3[:, :, h],
                                    axis=AXX, op=OP.add)
            smc_v = k.v("vec")

            @b.tensor
            def _(e, smc_v=smc_v):
                e.wait_ge(k.sem["vec"], smc_v)
                ins = e.matmul(out=pb[2][0:1, 0:16], lhsT=k.ones, rhs=SMC[:],
                               start=True, stop=True)
                k.inc(ins, "pe")
            lsum_v = k.v("pe")

            @b.vector
            def _(e, lsum_v=lsum_v):
                e = k.V(e)
                e._e.wait_ge(k.sem["pe"], lsum_v)
                e.reciprocal(out=LROW[0:1, :], in_=pb[2][0:1, 0:16])
            lr_v = k.v("vec")

            @b.gpsimd
            def _(e, lr_v=lr_v):
                e.wait_ge(k.sem["vec"], lr_v)
                k.ldh = k.dma(e, "ldg", LREP[:],
                              LROW[0:1, :].unsqueeze(1).broadcast_to([1, 128, 16]))
            lrep_h = k.ldh

            @b.vector
            def _(e, lrep_h=lrep_h):
                e = k.V(e)
                k.dwait(e._e, lrep_h)
                for ti in range(16):
                    e.tensor_tensor(out=ps3[:, ti, :], in0=ps3[:, ti, :],
                                    in1=LREP[:], op=OP.mult)
            p_done = k.v("vec")

            a8_rd = {0: 0, 1: 0, 2: 0, 3: 0}
            avh_rd = 0
            avst_rd = 0
            tb2 = {0: 0, 1: 0}
            vtr_rd = 0
            for ti in range(16):
                @b.sync
                def _(e, ti=ti, vtr_rd=vtr_rd):
                    e.wait_ge(k.sem["vec"], vtr_rd)
                    k.dwait_all(e, "st")
                    k.ldh = k.dma(e, "ldw", vtr3,
                                  vtok[:, ti].rearrange("h p e -> p h e"))
                vtr_h = k.ldh

                @b.vector
                def _(e, ti=ti, vtr_h=vtr_h, p_done=p_done):
                    e = k.V(e)
                    k.dwait(e._e, vtr_h)
                    e._e.wait_ge(k.sem["vec"], p_done)
                    for s2 in range(2):
                        e.tensor_tensor(out=PMS[:, s2 * 16:(s2 + 1) * 16],
                                        in0=ps3[:, ti, :], in1=mk3[:, s2, ti, :],
                                        op=OP.mult)

                pe0t = k.v("pe")
                vec0t = k.v("vec")
                avh_prd = {0: avh_rd, 1: avh_rd}
                for idx in range(32):
                    hh, s2 = idx // 2, idx % 2
                    pbase = (hh % 2) * 64

                    @b.vector
                    def _(e, idx=idx, hh=hh, s2=s2, avw=avh_prd[idx % 2]):
                        e = k.V(e)
                        e._e.wait_ge(k.sem["pe"], avw)
                        e.tensor_scalar(out=AVH[:, (idx % 2) * 64:(idx % 2 + 1) * 64],
                                        in0=vtr3[:, hh, :],
                                        scalar1=PMS[:, s2 * 16 + hh:s2 * 16 + hh + 1],
                                        scalar2=0.0, op0=OP.mult, op1=OP.bypass)
                    avh_v = k.v("vec")

                    @b.tensor
                    def _(e, idx=idx, pbase=pbase, avh_v=avh_v, vec0t=vec0t,
                          tbw=tb2[idx % 2]):
                        e.wait_ge(k.sem["vec"], avh_v)
                        if idx < 2:
                            e.wait_ge(k.sem["vec"], tbw)
                        else:
                            e.wait_ge(k.sem["vec"], vec0t + 2 * idx - 2)
                        ins = e.transpose(
                            out=pb[4 + idx % 2][pbase:pbase + 64, 0:64].bitcast(BF16),
                            in_=AVH[:, (idx % 2) * 64:(idx % 2 + 1) * 64],
                            identity=k.identb)
                        k.inc(ins, "pe")
                    avh_prd[idx % 2] = k.v("pe")
                    avh_rd = k.v("pe")
                    trv = k.v("pe")

                    @b.vector
                    def _(e, idx=idx, hh=hh, s2=s2, pbase=pbase, trv=trv,
                          avst_rd=avst_rd):
                        e = k.V(e)
                        e._e.wait_ge(k.sem["pe"], trv)
                        e._e.wait_ge(k.sem["pe"], avst_rd)
                        dst = AVST0 if s2 == 0 else AVST1
                        e.tensor_copy(
                            out=dst[pbase:pbase + 64,
                                    (hh // 2) * 128:(hh // 2 + 1) * 128],
                            in_=pb[4 + idx % 2][pbase:pbase + 64, 0:64]
                            .bitcast(BF16))
                tb2[0] = k.v("vec")
                tb2[1] = k.v("vec")
                av_done = k.v("vec")
                vtr_rd = av_done

                @b.tensor
                def _(e, av_done=av_done, wo_h=wo_h, a8_rd=dict(a8_rd)):
                    e.wait_ge(k.sem["vec"], av_done)
                    k.dwait(e, wo_h)
                    for s2 in range(2):
                        src = AVST0 if s2 == 0 else AVST1
                        for dh in range(2):
                            bank = s2 * 2 + dh
                            e.wait_ge(k.sem["vec"], a8_rd[bank])
                            for kt in range(8):
                                ins = e.matmul(
                                    out=pb[bank][:, 0:512],
                                    lhsT=src[:, kt * 128:(kt + 1) * 128],
                                    rhs=wo3[:, kt, dh * 512:(dh + 1) * 512],
                                    start=(kt == 0), stop=(kt == 7))
                            k.inc(ins, "pe")
                avst_rd = k.v("pe")
                omm = k.v("pe")

                for s2 in range(2):
                    g = s2 * 16 + ti

                    @b.vector
                    def _(e, s2=s2, g=g, ti=ti, omm=omm):
                        e = k.V(e)
                        e._e.wait_ge(k.sem["pe"], omm)
                        e.tensor_reduce(out=NSC[:, s2:s2 + 1],
                                        in_=mk3[:, s2, ti, :], axis=AXX, op=OP.add)
                        e.tensor_scalar(out=XLN[:], in0=k.x3[:, g, :],
                                        scalar1=k.st3[:, g, 0:1],
                                        scalar2=k.st3[:, g, 1:2],
                                        op0=OP.subtract, op1=OP.mult)
                        for dh in range(2):
                            e.scalar_tensor_tensor(
                                out=YB[:, dh * 512:(dh + 1) * 512],
                                in0=BOB[:, dh * 512:(dh + 1) * 512],
                                scalar=NSC[:, s2:s2 + 1],
                                in1=pb[s2 * 2 + dh][:, 0:512],
                                op0=OP.mult, op1=OP.add)
                        e.tensor_tensor(out=YB[:], in0=YB[:], in1=XLN[:], op=OP.add)
                        e.bn_stats(out=k.LNTMP[:, 0:6], in_=YB[:, 0:512])
                        e.bn_stats(out=k.LNTMP[:, 6:12], in_=YB[:, 512:1024])
                        e.bn_aggr(out=YST[:], in_=k.LNTMP[:])
                    yst_v = k.v("vec")

                    @b.scalar
                    def _(e, yst_v=yst_v):
                        e = k.S(e)
                        e._e.wait_ge(k.sem["vec"], yst_v)
                        e.activation(out=YST[:, 1:2], in_=YST[:, 1:2],
                                     func=AF.Ln, bias=k.eps_ln)
                        e.activation(out=YST[:, 1:2], in_=YST[:, 1:2],
                                     func=AF.Exp, scale=-0.5)
                    rstd_v = k.v("sc")

                    @b.vector
                    def _(e, g=g, rstd_v=rstd_v):
                        e = k.V(e)
                        e._e.wait_ge(k.sem["sc"], rstd_v)
                        e.tensor_scalar(out=YB[:], in0=YB[:],
                                        scalar1=YST[:, 0:1], scalar2=YST[:, 1:2],
                                        op0=OP.subtract, op1=OP.mult)
                        e.tensor_tensor(out=YB[:], in0=YB[:], in1=LNGB[:],
                                        op=OP.mult)
                        e.scalar_tensor_tensor(out=k.x3[:, g, :], in0=YB[:],
                                               scalar=0.5, in1=k.x3[:, g, :],
                                               op0=OP.mult, op1=OP.add)
                        e.scalar_tensor_tensor(out=k.x3[:, g, :], in0=LNBB[:],
                                               scalar=0.5, in1=k.x3[:, g, :],
                                               op0=OP.mult, op1=OP.add)
                    a8_rd[s2 * 2 + 0] = k.v("vec")
                    a8_rd[s2 * 2 + 1] = k.v("vec")


def phase_mamba(k):
    nc, b, t = k.nc, k.block, k.t
    SEG = 512
    NSEG = L // SEG
    NB = 2
    NBLK = 16 // NB
    pb = k.pb
    if "bmc_rows" not in k.dram:
        k.dram["bmc_rows"] = nc.dram_tensor("bmc_rows", [2, 2, 16 * SEG], F32)
    bmc = k.dram["bmc_rows"].ap()

    with ExitStack() as ps:
        sb = lambda nm, sh, dt: ps.enter_context(nc.sbuf_tensor(nm + "mb", sh, dt))
        BCW = sb("bcw", [128, 8 * 32], F32)
        ABUF = sb("abuf", [128, 8 * 16], F32)
        RBUF = sb("rbuf", [128, 8 * 16], F32)
        DBIA = sb("dbia", [128, 8], F32)
        SELB = sb("selbm", [128, 8], F32)
        SEL = sb("selm", [128, 8 * 2], F32)
        ECHL = sb("echl", [128, 16], F32)
        CELL = sb("cellm", [32, 16], F32)
        LNA = sb("lnam", [128, D], F32)
        XST = sb("xstm", [128, 8 * SEG], F32)
        DSEG = sb("dsegm", [128, 8 * SEG], F32)
        TMPR = sb("tmprm", [128, SEG], F32)
        YAC = sb("yacm", [128, SEG], F32)
        BCM = sb("bcmm", [32, SEG], F32)
        HST = sb("hstm", [128, 8 * 2 * 16], F32)
        hst4 = HST[:].rearrange("p (dc bb n) -> p dc bb n", dc=8, bb=2)
        bcw3 = BCW[:].rearrange("p (kt c) -> p kt c", kt=8)
        ab3 = ABUF[:].rearrange("p (dc n) -> p dc n", dc=8)
        rb3 = RBUF[:].rearrange("p (dc n) -> p dc n", dc=8)
        sel3 = SEL[:].rearrange("p (dc bb) -> p dc bb", dc=8)
        xst3 = XST[:].rearrange("p (dc tt) -> p dc tt", dc=8)
        dseg3 = DSEG[:].rearrange("p (dd tt) -> p dd tt", dd=8)

        snap0 = _snap(k)

        @b.sync
        def _(e):
            _bar(k, e, snap0)
            k.dma(e, "ldw", bcw3, t["BCWT"].rearrange("a p c -> p a c"))
            k.dma(e, "ldw", ab3, t["A_log"].rearrange("a p c -> p a c"))
            k.dma(e, "ldw", DBIA[:, 0:8], t["dbias"])
            k.dma(e, "ldw", TMPR[:, 0:8], t["dtbias"])
            k.dma(e, "ldw", SELB[:, 0:8], t["selb"])
            k.dma(e, "ldw", CELL[0:16, 2:3], t["Wib"])
            k.dma(e, "ldw", CELL[0:16, 3:4], t["h0T"])
            k.dma(e, "ldw", CELL[0:16, 4:5], t["tau"])
            k.dma(e, "ldw", CELL[0:16, 7:8], t["Bb"])
            k.ldh = k.dma(e, "ldw", CELL[16:32, 7:8], t["Cb"])
        wl_h = k.ldh

        @b.scalar
        def _(e, wl_h=wl_h):
            e = k.S(e)
            k.dwait(e._e, wl_h)
            k.dwait_all(e._e, "ldw")
            e.activation(out=ABUF[:], in_=ABUF[:], func=AF.Exp)
        aex = k.v("sc")

        @b.vector
        def _(e, aex=aex):
            e = k.V(e)
            e._e.wait_ge(k.sem["sc"], aex)
            k.dwait_all(e._e, "ldw")
            e.tensor_scalar(out=ABUF[:], in0=ABUF[:], scalar1=-1.0, scalar2=0.0,
                            op0=OP.mult, op1=OP.bypass)
            e.reciprocal(out=RBUF[:], in_=ABUF[:])
            e.tensor_scalar(out=SELB[:, 0:8], in0=SELB[:, 0:8], scalar1=-1.0,
                            scalar2=0.0, op0=OP.mult, op1=OP.bypass)
            e.tensor_tensor(out=DBIA[:, 0:8], in0=DBIA[:, 0:8], in1=TMPR[:, 0:8], op=OP.add)
            e.tensor_scalar(out=CELL[0:16, 5:6], in0=CELL[0:16, 4:5], scalar1=0.1,
                            scalar2=10.0, op0=OP.max, op1=OP.min)
            e.reciprocal(out=CELL[0:16, 6:7], in_=CELL[0:16, 5:6])

        ln_stats_all(k)
        sc_rdy = k.v("sc")

        with ExitStack() as p0s:
            sb0 = lambda nm, sh, dt: p0s.enter_context(nc.sbuf_tensor(nm + "m0", sh, dt))
            WIT = sb0("wit", [128, 8 * 16], F32)
            SWT = sb0("swt", [16, 8 * 128], F32)
            WRT = sb0("wrt", [16, 16], F32)
            GCT = sb0("gct", [128, 18], F32)
            PRD = sb0("prd", [128, 32], F32)
            wit3 = WIT[:].rearrange("p (kt c) -> p kt c", kt=8)
            swt3 = SWT[:].rearrange("p (dc c) -> p dc c", dc=8)

            @b.sync
            def _(e):
                k.dma(e, "ldw", wit3, t["WiT"].rearrange("a p c -> p a c"))
                k.dma(e, "ldw", swt3, t["selWT"])
                k.ldh = k.dma(e, "ldw", WRT[0:16, 0:16], t["WrT"])
            w2_h = k.ldh

            @b.vector
            def _(e, sc_rdy=sc_rdy):
                e = k.V(e)
                e._e.wait_ge(k.sem["sc"], sc_rdy)
                e.tensor_tensor(out=PRD[:, 0:32], in0=k.st3[:, :, 0], in1=k.st3[:, :, 1],
                                op=OP.mult)
            prdv = k.v("vec")

            @b.tensor
            def _(e, prdv=prdv, sc_rdy=sc_rdy):
                e.wait_ge(k.sem["vec"], prdv)
                e.wait_ge(k.sem["sc"], sc_rdy)
                for bb in range(2):
                    for dc in range(8):
                        for g in range(16):
                            ins = e.matmul(out=pb[5][:, dc * 2 + bb:dc * 2 + bb + 1],
                                           lhsT=k.x3[:, bb * 16 + g, dc * 128:(dc + 1) * 128],
                                           rhs=k.st3[:, bb * 16 + g, 1:2],
                                           start=(g == 0), stop=(g == 15))
                ins2 = e.matmul(out=pb[4][0:1, 0:32], lhsT=k.ones, rhs=PRD[:, 0:32],
                                start=True, stop=True)
                k.inc(ins2, "pe")
            gcp = k.v("pe")

            @b.vector
            def _(e, gcp=gcp):
                e = k.V(e)
                e._e.wait_ge(k.sem["pe"], gcp)
                e.tensor_reduce(out=CELL[0:1, 0:1], in_=pb[4][0:1, 0:16], axis=AXX, op=OP.add)
                e.tensor_reduce(out=CELL[0:1, 1:2], in_=pb[4][0:1, 16:32], axis=AXX, op=OP.add)
            sbv = k.v("vec")

            @b.gpsimd
            def _(e, sbv=sbv):
                e.wait_ge(k.sem["vec"], sbv)
                k.ldh = k.dma(e, "ldg", GCT[:, 16:18],
                              CELL[0:1, 0:2].unsqueeze(1).broadcast_to([1, 128, 2]))
            sb_h = k.ldh

            @b.vector
            def _(e, sb_h=sb_h, gcp=gcp):
                e = k.V(e)
                k.dwait(e._e, sb_h)
                e._e.wait_ge(k.sem["pe"], gcp)
                for bb in range(2):
                    for dc in range(8):
                        e.tensor_scalar(out=GCT[:, dc * 2 + bb:dc * 2 + bb + 1],
                                        in0=pb[5][:, dc * 2 + bb:dc * 2 + bb + 1],
                                        scalar1=GCT[:, 16 + bb:17 + bb], scalar2=1.0 / L,
                                        op0=OP.subtract, op1=OP.mult)
                e.tensor_copy(out=CELL[0:16, 12:13], in_=CELL[0:16, 3:4])
                e.tensor_copy(out=CELL[0:16, 13:14], in_=CELL[0:16, 3:4])
            gctv = k.v("vec")

            @b.tensor
            def _(e, gctv=gctv, w2_h=w2_h):
                e.wait_ge(k.sem["vec"], gctv)
                k.dwait_all(e, "ldw")
                for dc in range(8):
                    ins = e.matmul(out=pb[6][0:16, 0:2], lhsT=wit3[:, dc, :],
                                   rhs=GCT[:, 0:16].rearrange("p (dc bb) -> p dc bb", dc=8)[:, dc, :],
                                   start=(dc == 0), stop=(dc == 7))
                ins2 = e.matmul(out=pb[7][0:16, 0:2], lhsT=WRT[0:16, 0:16],
                                rhs=CELL[0:16, 12:14], start=True, stop=True)
                k.inc(ins2, "pe")
            cellp = k.v("pe")

            @b.scalar
            def _(e, cellp=cellp):
                e = k.S(e)
                e._e.wait_ge(k.sem["pe"], cellp)
                e.activation(out=CELL[0:16, 8:10], in_=pb[6][0:16, 0:2], func=AF.Identity,
                             bias=CELL[0:16, 2:3])
            isv = k.v("sc")

            @b.vector
            def _(e, isv=isv, cellp=cellp):
                e = k.V(e)
                e._e.wait_ge(k.sem["sc"], isv)
                e._e.wait_ge(k.sem["pe"], cellp)
                e.tensor_tensor(out=CELL[0:16, 8:10], in0=CELL[0:16, 8:10],
                                in1=pb[7][0:16, 0:2], op=OP.add)
            zv = k.v("vec")

            @b.scalar
            def _(e, zv=zv):
                e = k.S(e)
                e._e.wait_ge(k.sem["vec"], zv)
                e.activation(out=CELL[0:16, 10:12], in_=CELL[0:16, 8:10], func=AF.Exp,
                             scale=2.0)
            ze = k.v("sc")

            @b.vector
            def _(e, ze=ze):
                e = k.V(e)
                e._e.wait_ge(k.sem["sc"], ze)
                e.tensor_scalar(out=CELL[0:16, 10:12], in0=CELL[0:16, 10:12], scalar1=1.0,
                                scalar2=0.0, op0=OP.add, op1=OP.bypass)
                e.reciprocal(out=CELL[0:16, 10:11], in_=CELL[0:16, 10:11])
                e.reciprocal(out=CELL[0:16, 11:12], in_=CELL[0:16, 11:12])
                e.tensor_scalar(out=CELL[0:16, 10:12], in0=CELL[0:16, 10:12], scalar1=-2.0,
                                scalar2=1.0, op0=OP.mult, op1=OP.add)
                e.tensor_tensor(out=CELL[0:16, 10:12], in0=CELL[0:16, 10:12],
                                in1=CELL[0:16, 12:14], op=OP.subtract)
                e.tensor_scalar(out=CELL[0:16, 10:12], in0=CELL[0:16, 10:12],
                                scalar1=CELL[0:16, 6:7], scalar2=0.1, op0=OP.mult, op1=OP.mult)
                e.tensor_tensor(out=CELL[0:16, 14:16], in0=CELL[0:16, 10:12],
                                in1=CELL[0:16, 12:14], op=OP.add)
            nhv = k.v("vec")

            @b.tensor
            def _(e, nhv=nhv):
                e.wait_ge(k.sem["vec"], nhv)
                k.dwait_all(e, "ldw")
                for dc in range(8):
                    ins = e.matmul(out=pb[6][:, dc * 2:dc * 2 + 2], lhsT=swt3[0:16, dc, :],
                                   rhs=CELL[0:16, 14:16], start=True, stop=True)
                k.inc(ins, "pe")
            selp = k.v("pe")

            @b.scalar
            def _(e, selp=selp):
                e = k.S(e)
                e._e.wait_ge(k.sem["pe"], selp)
                for dc in range(8):
                    e.activation(out=ECHL[:, dc * 2:dc * 2 + 2],
                                 in_=pb[6][:, dc * 2:dc * 2 + 2],
                                 func=AF.Exp, scale=-1.0, bias=SELB[:, dc:dc + 1])
            ech = k.v("sc")

            @b.vector
            def _(e, ech=ech):
                e = k.V(e)
                e._e.wait_ge(k.sem["sc"], ech)
                e.tensor_scalar(out=ECHL[:], in0=ECHL[:], scalar1=1.0, scalar2=0.0,
                                op0=OP.add, op1=OP.bypass)
                e.reciprocal(out=SEL[:], in_=ECHL[:])
                e.memset(HST[:], 0.0)
            selv = k.v("vec")

        DWTC = sb("dwtc", [128, 2 * 8 * 128], F32)
        GB = sb("gbm", [128, 2 * NB * SEG], F32)
        WB = sb("wbm", [128, NB * SEG], F32)
        HB = sb("hbm", [128, NB * SEG], F32)
        BMR = sb("bmrm", [128, NB * SEG], F32)
        CMR = sb("cmrm", [128, NB * SEG], F32)
        dwtc4 = DWTC[:].rearrange("p (u kt c) -> p u kt c", u=2, kt=8)

        k.cnt["_mbxst"] = k.v("vec")
        k.cnt["_mbscan"] = k.v("vec")
        k.cnt["_mbscb0"] = k.v("vec")
        k.cnt["_mbscb1"] = k.v("vec")
        k.cnt["_mbyev"] = k.v("vec")
        k.cnt["_mbdse"] = k.v("sc")
        k.cnt["_mbbcv"] = k.v("sc")
        dwt_h = {0: None, 1: None}
        pk_h = {}
        for bb in range(2):
            for sg in range(NSEG):
                s0 = sg * SEG
                tiles = [bb * 16 + (s0 // 128) + i for i in range(4)]

                for i, g in enumerate(tiles):
                    lnw = k.v("pe")

                    @b.vector
                    def _(e, g=g, lnw=lnw, selv=selv):
                        e = k.V(e)
                        e._e.wait_ge(k.sem["vec"], selv)
                        e._e.wait_ge(k.sem["pe"], lnw)
                        e.tensor_scalar(out=LNA[:], in0=k.x3[:, g, :],
                                        scalar1=k.st3[:, g, 0:1], scalar2=k.st3[:, g, 1:2],
                                        op0=OP.subtract, op1=OP.mult)
                    lnv = k.v("vec")
                    for dc in range(8):
                        @b.tensor
                        def _(e, lnv=lnv, dc=dc, i=i):
                            e.wait_ge(k.sem["vec"], lnv)
                            if dc >= 2:
                                e.wait_ge(k.sem["vec"], k.cnt["_mbxst"])
                            elif i == 0 and dc == 0:
                                e.wait_ge(k.sem["vec"], k.cnt["_mbxstall"]
                                          if "_mbxstall" in k.cnt else 0)
                            ins = e.transpose(out=pb[6 + (dc % 2)][:, 0:128],
                                              in_=LNA[:, dc * 128:(dc + 1) * 128],
                                              identity=k.ident)
                            k.inc(ins, "pe")
                        trp = k.v("pe")

                        @b.vector
                        def _(e, trp=trp, dc=dc, i=i, bb=bb):
                            e = k.V(e)
                            e._e.wait_ge(k.sem["pe"], trp)
                            e.tensor_scalar(out=xst3[:, dc, i * 128:(i + 1) * 128],
                                            in0=pb[6 + (dc % 2)][:, 0:128],
                                            scalar1=sel3[:, dc, bb:bb + 1], scalar2=0.0,
                                            op0=OP.mult, op1=OP.bypass)
                        k.cnt["_mbxst"] = k.v("vec")
                k.cnt["_mbxstall"] = k.v("vec")
                xstv = k.v("vec")

                scb = k.v("sc")
                for dd in range(8):
                    @b.sync
                    def _(e, dd=dd, bb=bb, sg=sg):
                        if bb == 0 and sg == 0 and dd == 0:
                            k.dwait_all(e, "ldw")
                            k.wait(e, "pe")
                            k.wait(e, "vec")
                            k.wait(e, "sc")
                        if dwt_h[dd % 2] is not None:
                            e.wait_ge(k.sem["pe"], k.cnt[f"_mbdmm{dd % 2}"])
                        k.ldh = k.dma(e, "ldw", dwtc4[:, dd % 2, :, :],
                                      t["dWT"].rearrange("a p dd c -> p a dd c")[:, :, dd, :])
                    dwt_h[dd % 2] = k.ldh
                    dh = k.ldh

                    @b.tensor
                    def _(e, dd=dd, xstv=xstv, dh=dh, scb=scb):
                        e.wait_ge(k.sem["vec"], xstv)
                        k.dwait(e, dh)
                        if dd >= 2:
                            e.wait_ge(k.sem["sc"], scb + 2 * (dd - 1))
                        for kt in range(8):
                            ins = e.matmul(out=pb[dd % 2][:, 0:512],
                                           lhsT=dwtc4[:, dd % 2, kt, :], rhs=xst3[:, kt, :],
                                           start=(kt == 0), stop=(kt == 7))
                        k.inc(ins, "pe")
                        k.cnt[f"_mbdmm{dd % 2}"] = k.v("pe")
                    dmm = k.v("pe")

                    @b.scalar
                    def _(e, dd=dd, dmm=dmm):
                        e = k.S(e)
                        e._e.wait_ge(k.sem["pe"], dmm)
                        e._e.wait_ge(k.sem["vec"], k.cnt["_mbscan"])
                        e.activation(out=dseg3[:, dd, :], in_=pb[dd % 2][:, 0:512],
                                     func=AF.Exp, bias=DBIA[:, dd:dd + 1])
                        e.activation(out=dseg3[:, dd, :], in_=dseg3[:, dd, :],
                                     func=AF.Ln, bias=1.0)
                dse = k.v("sc")

                @b.tensor
                def _(e, xstv=xstv):
                    e.wait_ge(k.sem["vec"], xstv)
                    e.wait_ge(k.sem["sc"], k.cnt["_mbbcv"])
                    for kt in range(8):
                        ins = e.matmul(out=pb[2][0:32, 0:512], lhsT=bcw3[:, kt, :],
                                       rhs=xst3[:, kt, :], start=(kt == 0), stop=(kt == 7))
                    k.inc(ins, "pe")
                bcp = k.v("pe")

                @b.scalar
                def _(e, bcp=bcp, sg=sg):
                    e = k.S(e)
                    e._e.wait_ge(k.sem["pe"], bcp)
                    for par in (0, 1):
                        for hh in pk_h.get(par, []):
                            k.dwait(e._e, hh)
                    e.activation(out=BCM[0:32, 0:512], in_=pb[2][0:32, 0:512],
                                 func=AF.Identity, bias=CELL[0:32, 7:8])
                bce = k.v("sc")
                k.cnt["_mbbcv"] = bce

                @b.gpsimd
                def _(e, bce=bce, sg=sg):
                    e.wait_ge(k.sem["sc"], bce)
                    e.wait_ge(k.sem["vec"], k.cnt["_mbscan"])
                    h1 = k.dma(e, "ldg", bmc[sg % 2, 0, :].unsqueeze(0)
                               .rearrange("o (n tt) -> o n tt", n=16),
                               BCM[0:16, 0:512])
                    h2 = k.dma(e, "ldg", bmc[sg % 2, 1, :].unsqueeze(0)
                               .rearrange("o (n tt) -> o n tt", n=16),
                               BCM[16:32, 0:512])
                    pk_h[sg % 2] = [h1, h2]
                    k.cnt[f"_mbpk{sg % 2}"] = 1
                pkb, pkc = pk_h[sg % 2]

                for dc in range(8):
                    for nb in range(NBLK):
                        n0 = nb * NB
                        it = dc * NBLK + nb
                        par = it % 2
                        gb = GB[:, par * NB * SEG:(par + 1) * NB * SEG]
                        bmr = BMR[:]
                        cmr = CMR[:]

                        @b.sync
                        def _(e, sg=sg, n0=n0, pkb=pkb, pkc=pkc, par=par,
                              bmr=bmr, cmr=cmr):
                            k.dwait(e, pkb)
                            k.dwait(e, pkc)
                            e.wait_ge(k.sem["vec"], k.cnt[f"_mbscb{par}"])
                            e.wait_ge(k.sem["vec"], k.cnt["_mbscan"])
                            k.ldh = k.dma(e, "ldw", bmr,
                                          bmc[sg % 2, 0, n0 * SEG:(n0 + NB) * SEG]
                                          .unsqueeze(0).unsqueeze(1)
                                          .broadcast_to([1, 128, NB * SEG]))
                            k.ldh2 = k.dma(e, "ldw", cmr,
                                           bmc[sg % 2, 1, n0 * SEG:(n0 + NB) * SEG]
                                           .unsqueeze(0).unsqueeze(1)
                                           .broadcast_to([1, 128, NB * SEG]))
                        bm_h, cm_h = k.ldh, k.ldh2

                        @b.scalar
                        def _(e, dc=dc, n0=n0, dse=dse, par=par, gb=gb):
                            e = k.S(e)
                            e._e.wait_ge(k.sem["sc"], dse)
                            e._e.wait_ge(k.sem["vec"], k.cnt[f"_mbscb{par}"])
                            for jn in range(NB):
                                e.activation(out=gb[:, jn * SEG:(jn + 1) * SEG],
                                             in_=dseg3[:, dc, :], func=AF.Exp,
                                             scale=ab3[:, dc, n0 + jn:n0 + jn + 1])
                        gse = k.v("sc")

                        @b.vector
                        def _(e, dc=dc, n0=n0, gse=gse, bm_h=bm_h, cm_h=cm_h,
                              bb=bb, gb=gb, bmr=bmr, cmr=cmr):
                            e = k.V(e)
                            e._e.wait_ge(k.sem["sc"], gse)
                            k.dwait(e._e, bm_h)
                            e.tensor_tensor(out=WB[:].rearrange("p (n tt) -> p n tt", n=NB),
                                            in0=bmr.rearrange("p (n tt) -> p n tt", n=NB),
                                            in1=xst3[:, dc, :].unsqueeze(1)
                                            .broadcast_to([128, NB, SEG]),
                                            op=OP.mult)
                            e.scalar_tensor_tensor(out=WB[:], in0=gb, scalar=-1.0,
                                                   in1=WB[:], op0=OP.add, op1=OP.mult)
                            for jn in range(NB):
                                e.tensor_tensor_scan(
                                    out=HB[:, jn * SEG:(jn + 1) * SEG],
                                    data0=gb[:, jn * SEG:(jn + 1) * SEG],
                                    data1=WB[:, jn * SEG:(jn + 1) * SEG],
                                    initial=hst4[:, dc, bb, n0 + jn:n0 + jn + 1],
                                    op0=OP.mult, op1=OP.add)
                                e.tensor_copy(out=hst4[:, dc, bb, n0 + jn:n0 + jn + 1],
                                              in_=HB[:, (jn + 1) * SEG - 1:(jn + 1) * SEG])
                                if jn == 0:
                                    k.dwait(e._e, cm_h)
                                e.scalar_tensor_tensor(
                                    out=HB[:, jn * SEG:(jn + 1) * SEG],
                                    in0=HB[:, jn * SEG:(jn + 1) * SEG],
                                    scalar=rb3[:, dc, n0 + jn:n0 + jn + 1],
                                    in1=cmr[:, jn * SEG:(jn + 1) * SEG],
                                    op0=OP.mult, op1=OP.mult)
                            if n0 == 0:
                                e.tensor_tensor(out=YAC[:], in0=HB[:, 0:SEG],
                                                in1=HB[:, SEG:2 * SEG], op=OP.add)
                            else:
                                e.tensor_tensor(out=TMPR[:], in0=HB[:, 0:SEG],
                                                in1=HB[:, SEG:2 * SEG], op=OP.add)
                                e.tensor_tensor(out=YAC[:], in0=YAC[:], in1=TMPR[:],
                                                op=OP.add)
                            k.cnt["_mbscan"] = k.v("vec")
                            k.cnt[f"_mbscb{par}"] = k.v("vec")
                    yv = k.v("vec")

                    for i in range(4):
                        @b.tensor
                        def _(e, yv=yv, i=i):
                            e.wait_ge(k.sem["vec"], yv)
                            if i >= 2:
                                e.wait_ge(k.sem["vec"], k.cnt["_mbyev"])
                            ins = e.transpose(out=pb[4 + (i % 2)][:, 0:128],
                                              in_=YAC[:, i * 128:(i + 1) * 128],
                                              identity=k.ident)
                            k.inc(ins, "pe")
                        ytp = k.v("pe")

                        @b.vector
                        def _(e, ytp=ytp, dc=dc, i=i, tiles=tiles):
                            e = k.V(e)
                            e._e.wait_ge(k.sem["pe"], ytp)
                            e.tensor_tensor(out=k.x3[:, tiles[i], dc * 128:(dc + 1) * 128],
                                            in0=k.x3[:, tiles[i], dc * 128:(dc + 1) * 128],
                                            in1=pb[4 + (i % 2)][:, 0:128], op=OP.add)
                        k.cnt["_mbyev"] = k.v("vec")


_QUAT_TABLE = {
    0: [(0, 1.0), (1, -1.0), (2, -1.0), (3, -1.0)],
    1: [(1, 1.0), (0, 1.0), (3, 1.0), (2, -1.0)],
    2: [(2, 1.0), (3, -1.0), (0, 1.0), (1, 1.0)],
    3: [(3, 1.0), (2, 1.0), (1, -1.0), (0, 1.0)],
}
_QUAT_CW = {}
for _comp, _lst in _QUAT_TABLE.items():
    for _w, (_c, _s) in enumerate(_lst):
        _QUAT_CW[(_c, _comp)] = (_w, _s)


def _fence(k):
    b = k.block
    snap = _snap(k)

    @b.sync
    def _(e):
        _bar(k, e, snap)

    @b.tensor
    def _(e):
        _bar(k, e, snap)

    @b.vector
    def _(e):
        _bar(k, e, snap)

    @b.scalar
    def _(e):
        _bar(k, e, snap)

    @b.gpsimd
    def _(e):
        _bar(k, e, snap)


def phase_l1attn(k):
    nc, b, t = k.nc, k.block, k.t
    pb = k.pb
    H = 16
    if "lnt_spill" not in k.dram:
        k.dram["lnt_spill"] = nc.dram_tensor("lnt_spill", [8, 128, L], BF16)
    if "bo_row" not in k.dram:
        k.dram["bo_row"] = nc.dram_tensor("bo_row", [1, 512], BF16)
    lnt_sp = k.dram["lnt_spill"].ap()
    bo_row = k.dram["bo_row"].ap()
    kv_sp = k.dram["kv_spill"].ap()

    with ExitStack() as ps:
        sb = lambda nm, sh, dt: ps.enter_context(nc.sbuf_tensor(nm + "at", sh, dt))
        QBUF = sb("qbuf", [128, 8 * L], BF16)
        QL = sb("ql", [128, 8 * 4 * 128], BF16)
        BIA = sb("bia", [128, 8], F32)
        BL = sb("bl", [128, 8], F32)
        BO8 = sb("bo8", [4, 128], BF16)
        BOROW = sb("borow", [1, 512], BF16)
        ONER = sb("oner", [1, 128], BF16)
        ONERF = sb("onerf", [1, 128], F32)
        AVS = sb("avs", [64, 512], BF16)
        RLS = sb("rls", [1, 512], F32)
        JC = sb("jc", [128, 8], F32)
        CB = sb("cb", [128, 2], F32)
        QBUF3 = QBUF[:].rearrange("p (dc c) -> p dc c", dc=8)
        ql4 = QL[:].rearrange("p (dc comp c) -> p dc comp c", dc=8, comp=4)
        x34 = k.x3.rearrange("p g (comp two c) -> p g comp two c", two=2, c=128)

        _fence(k)

        @b.sync
        def _(e):
            k.ldh = k.dma(e, "ldw", JC[:, 0:8], t["J"])
        j_h = k.ldh

        @b.vector
        def _(e, j_h=j_h):
            e = k.V(e)
            e.memset(ONER[0:1, 0:128], 1.0)
            e.memset(ONERF[0:1, 0:128], 1.0)
            k.dwait(e._e, j_h)
            TWO_PI = float(2.0 * np.pi)
            e.tensor_scalar(out=BIA[:, 0:8], in0=JC[:, 0:8], scalar1=1.0 / TWO_PI,
                            scalar2=MAGIC, op0=OP.mult, op1=OP.add)
            e.tensor_scalar(out=BIA[:, 0:8], in0=BIA[:, 0:8], scalar1=MAGIC, scalar2=0.0,
                            op0=OP.subtract, op1=OP.bypass)
            e.scalar_tensor_tensor(out=BIA[:, 0:8], in0=BIA[:, 0:8], scalar=-TWO_PI,
                                   in1=JC[:, 0:8], op0=OP.mult, op1=OP.add)
        jr_v = k.v("vec")

        @b.scalar
        def _(e, jr_v=jr_v):
            e = k.S(e)
            e._e.wait_ge(k.sem["vec"], jr_v)
            e.activation(out=BIA[:, 0:8], in_=BIA[:, 0:8], func=AF.Sin)
        sin_v = k.v("sc")

        @b.vector
        def _(e, sin_v=sin_v):
            e = k.V(e)
            e._e.wait_ge(k.sem["sc"], sin_v)
            e.tensor_reduce(out=CB[:, 1:2], in_=BIA[:, 0:8], axis=AXX, op=OP.add)
        jsum = k.v("vec")

        @b.tensor
        def _(e, jsum=jsum):
            e.wait_ge(k.sem["vec"], jsum)
            ins = e.matmul(out=pb[5][0:1, 0:1], lhsT=k.ones, rhs=CB[:, 1:2],
                           start=True, stop=True)
            k.inc(ins, "pe")
        jp = k.v("pe")

        @b.vector
        def _(e, jp=jp):
            e = k.V(e)
            e._e.wait_ge(k.sem["pe"], jp)
            e.tensor_scalar(out=CB[0:1, 1:2], in0=pb[5][0:1, 0:1],
                            scalar1=0.1 / 1024.0, scalar2=0.0, op0=OP.mult, op1=OP.bypass)
        cb_v = k.v("vec")

        @b.gpsimd
        def _(e, cb_v=cb_v):
            e.wait_ge(k.sem["vec"], cb_v)
            k.ldh = k.dma(e, "ldg", CB[:, 0:1],
                          CB[0:1, 1:2].unsqueeze(1).broadcast_to([1, 128, 1]))
        cbr_h = k.ldh

        sc_rdy = ln_stats_all(k)

        sc_bank = {0: 0, 1: 0, 4: 0, 5: 0}
        pt_rd = {0: 0, 1: 0, 2: 0, 3: 0}
        av_rd = {0: 0, 1: 0}
        k.cnt["_atfl"] = k.v("pe")
        qf_rd = 0
        vst_h = None
        avs_h = None
        rlh = None

        for bb in range(2):
            with ExitStack() as s1:
                sb1 = lambda nm, sh, dt: s1.enter_context(
                    nc.sbuf_tensor(f"{nm}at{bb}", sh, dt))
                LTS = sb1("lts", [128, D], BF16)
                LTS2 = sb1("lts2", [128, D], BF16)
                _fence(k)
                st_h = None
                for ti in range(16):
                    g = bb * 16 + ti
                    lnw = k.v("pe")

                    @b.vector
                    def _(e, g=g, lnw=lnw, sc_rdy=sc_rdy):
                        e = k.V(e)
                        e._e.wait_ge(k.sem["sc"], sc_rdy)
                        e._e.wait_ge(k.sem["pe"], lnw)
                        e.tensor_scalar(out=LTS[:], in0=k.x3[:, g, :],
                                        scalar1=k.st3[:, g, 0:1], scalar2=k.st3[:, g, 1:2],
                                        op0=OP.subtract, op1=OP.mult)
                    cst = k.v("vec")
                    wb = [("vec", cst)]
                    if st_h is not None:
                        wb.append((st_h[0], st_h[1]))
                    tr_chain(k, [(LTS[:, dc * 128:(dc + 1) * 128],
                                  LTS2[:, dc * 128:(dc + 1) * 128])
                                 for dc in range(8)], wait_before=wb)
                    vq = k.v("vec")

                    @b.gpsimd
                    def _(e, ti=ti, vq=vq):
                        e.wait_ge(k.sem["vec"], vq)
                        k.ldh = k.dma(
                            e, "st",
                            lnt_sp[:, :, ti * 128:(ti + 1) * 128]
                            .rearrange("dc p c -> p dc c"),
                            LTS2[:].rearrange("p (dc c) -> p dc c", dc=8))
                    st_h = k.ldh

            with ExitStack() as s2:
                sb2 = lambda nm, sh, dt: s2.enter_context(
                    nc.sbuf_tensor(f"{nm}at{bb}", sh, dt))
                WS = sb2("ws", [128, 4 * 2 * 256], F32)
                QF = sb2("qf", [128, 4 * 256], F32)
                QS = sb2("qs", [128, 4 * 256], BF16)
                N2 = sb2("n2", [128, 256], F32)
                SQ = sb2("sq", [128, 256], F32)
                INV = sb2("inv", [128, 256], F32)
                LNC = sb2("lnc", [128, 2 * 8 * 256], BF16)
                ws4 = WS[:].rearrange("p (w u c) -> p w u c", w=4, u=2)
                lnc3 = LNC[:].rearrange("p (u dc c) -> p u dc c", u=2, dc=8)
                qs3 = QS[:].rearrange("p (comp c) -> p comp c", comp=4)
                _fence(k)
                mm_rd = {0: 0, 1: 0}

                for proj in ("q", "k", "v"):
                    penow = k.v("pe")
                    scnow = k.v("sc")
                    vecnow = k.v("vec")

                    @b.sync
                    def _(e, proj=proj, penow=penow, scnow=scnow, vecnow=vecnow):
                        e.wait_ge(k.sem["pe"], penow)
                        e.wait_ge(k.sem["sc"], scnow)
                        e.wait_ge(k.sem["vec"], vecnow)
                        k.dwait_all(e, "st")
                        k.ldh = k.dma(e, "ldw",
                                      WS[:].rearrange("p (w u c) -> p w u c", w=4, u=2),
                                      t[proj + "W"].rearrange("w u p c -> p w u c"))
                        k.ldh2 = k.dma(e, "ldw", BL[:, 0:8],
                                       t[proj + "b"].rearrange("w (u c) -> c (w u)", u=2),
                                       allow_slow_non_contiguous=True)
                    w_h, b_h = k.ldh, k.ldh2

                    for oh in range(2):
                        penow = k.v("pe")
                        scnow = k.v("sc")

                        @b.vector
                        def _(e, oh=oh, w_h=w_h, b_h=b_h, penow=penow, scnow=scnow):
                            e = k.V(e)
                            k.dwait(e._e, w_h)
                            k.dwait(e._e, b_h)
                            e._e.wait_ge(k.sem["pe"], penow)
                            e._e.wait_ge(k.sem["sc"], scnow)
                            for c in range(4):
                                for comp in range(4):
                                    w, s = _QUAT_CW[(c, comp)]
                                    for rh in range(2):
                                        e.tensor_scalar(
                                            out=ql4[:, c * 2 + rh, comp, :],
                                            in0=ws4[:, w, rh, oh * 128:(oh + 1) * 128],
                                            scalar1=float(s), scalar2=0.0,
                                            op0=OP.mult, op1=OP.bypass)
                            for comp in range(4):
                                for wi, (c, s) in enumerate(_QUAT_TABLE[comp]):
                                    if wi == 0:
                                        e.tensor_scalar(
                                            out=BIA[:, comp:comp + 1],
                                            in0=BL[:, wi * 2 + oh:wi * 2 + oh + 1],
                                            scalar1=float(s), scalar2=0.0,
                                            op0=OP.mult, op1=OP.bypass)
                                    else:
                                        e.scalar_tensor_tensor(
                                            out=BIA[:, comp:comp + 1],
                                            in0=BL[:, wi * 2 + oh:wi * 2 + oh + 1],
                                            scalar=float(s),
                                            in1=BIA[:, comp:comp + 1],
                                            op0=OP.mult, op1=OP.add)
                        ql_v = k.v("vec")

                        for rc in range(8):
                            @b.sync
                            def _(e, rc=rc):
                                e.wait_ge(k.sem["pe"], mm_rd[rc % 2])
                                k.dwait_all(e, "st")
                                k.ldh = k.dma(e, "ldw", lnc3[:, rc % 2, :, :],
                                              lnt_sp[:, :, rc * 256:(rc + 1) * 256]
                                              .rearrange("dc p c -> p dc c"))
                            ln_h = k.ldh

                            pe0p = k.v("pe")
                            sc0p = k.v("sc")
                            vec0p = k.v("vec")

                            @b.tensor
                            def _(e, rc=rc, ln_h=ln_h, ql_v=ql_v, sc0p=sc0p,
                                  sb0=sc_bank[0], sb1=sc_bank[1]):
                                e.wait_ge(k.sem["vec"], ql_v)
                                k.dwait(e, ln_h)
                                for comp in range(4):
                                    if comp == 0:
                                        e.wait_ge(k.sem["sc"], sb0)
                                    elif comp == 1:
                                        e.wait_ge(k.sem["sc"], sb1)
                                    else:
                                        e.wait_ge(k.sem["sc"], sc0p + comp - 1)
                                    for dc in range(8):
                                        ins = e.matmul(out=pb[comp % 2][:, 0:256],
                                                       lhsT=ql4[:, dc, comp, :],
                                                       rhs=lnc3[:, rc % 2, dc, :],
                                                       start=(dc == 0), stop=(dc == 7))
                                    k.inc(ins, "pe")
                            mm_rd[rc % 2] = pe0p + 4

                            @b.scalar
                            def _(e, pe0p=pe0p, vec0p=vec0p, proj=proj,
                                  qfw=qf_rd, vh=vst_h):
                                e = k.S(e)
                                if proj == "v" and vh is not None:
                                    k.dwait(e._e, vh)
                                if proj != "v":
                                    e._e.wait_ge(k.sem["vec"], qfw)
                                for comp in range(4):
                                    e._e.wait_ge(k.sem["pe"], pe0p + comp + 1)
                                    if proj == "v":
                                        e.activation(out=qs3[:, comp, :],
                                                     in_=pb[comp % 2][:, 0:256],
                                                     func=AF.Identity,
                                                     bias=BIA[:, comp:comp + 1])
                                    else:
                                        e.activation(
                                            out=QF[:, comp * 256:(comp + 1) * 256],
                                            in_=pb[comp % 2][:, 0:256],
                                            func=AF.Identity,
                                            bias=BIA[:, comp:comp + 1])
                                if proj != "v":
                                    e._e.wait_ge(k.sem["vec"], vec0p + 7)
                                    e.activation(out=INV[:], in_=N2[:],
                                                 func=AF.Ln, bias=k.eps_q)
                                    e.activation(out=INV[:], in_=INV[:],
                                                 func=AF.Exp, scale=-0.5)
                            sc_bank[0] = sc0p + 3
                            sc_bank[1] = sc0p + 4
                            ev4 = sc0p + 4
                            inv_v = k.v("sc")

                            if proj != "v":
                                @b.vector
                                def _(e, ev4=ev4):
                                    e = k.V(e)
                                    e._e.wait_ge(k.sem["sc"], ev4)
                                    e.tensor_tensor(out=N2[:], in0=QF[:, 0:256],
                                                    in1=QF[:, 0:256], op=OP.mult)
                                    for comp in range(1, 4):
                                        e.tensor_tensor(
                                            out=SQ[:],
                                            in0=QF[:, comp * 256:(comp + 1) * 256],
                                            in1=QF[:, comp * 256:(comp + 1) * 256],
                                            op=OP.mult)
                                        e.tensor_tensor(out=N2[:], in0=N2[:],
                                                        in1=SQ[:], op=OP.add)
                                assert k.v("vec") == vec0p + 7

                                @b.vector
                                def _(e, inv_v=inv_v, proj=proj, oh=oh, rc=rc,
                                      vh=vst_h):
                                    e = k.V(e)
                                    e._e.wait_ge(k.sem["sc"], inv_v)
                                    if proj == "k" and vh is not None:
                                        k.dwait(e._e, vh)
                                    for comp in range(4):
                                        if proj == "q":
                                            outp = QBUF3[:, comp * 2 + oh,
                                                         rc * 256:(rc + 1) * 256]
                                        else:
                                            outp = qs3[:, comp, :]
                                        e.tensor_tensor(
                                            out=outp,
                                            in0=QF[:, comp * 256:(comp + 1) * 256],
                                            in1=INV[:], op=OP.mult)
                                qf_rd = k.v("vec")
                            ship_v = k.v("vec")

                            if proj != "q":
                                jdx = 0 if proj == "k" else 1

                                @b.gpsimd
                                def _(e, jdx=jdx, oh=oh, rc=rc, bb=bb,
                                      ship_v=ship_v, ev4=ev4, proj=proj):
                                    if proj == "k":
                                        e.wait_ge(k.sem["vec"], ship_v)
                                    else:
                                        e.wait_ge(k.sem["sc"], ev4)
                                    kvv = kv_sp[bb, jdx].rearrange(
                                        "(comp two p) c -> p comp two c", two=2, p=128)
                                    k.ldh = k.dma(e, "st",
                                                  kvv[:, :, oh, rc * 256:(rc + 1) * 256],
                                                  qs3)
                                vst_h = k.ldh

            with ExitStack() as s3:
                sb3 = lambda nm, sh, dt: s3.enter_context(
                    nc.sbuf_tensor(f"{nm}at{bb}", sh, dt))
                KH = sb3("kh", [128, L], BF16)
                VHF = sb3("vhf", [64, L], BF16)
                VT = sb3("vt", [128, 16 * 65], BF16)
                PT = sb3("pt", [128, 4 * 512], BF16)
                RL = sb3("rl", [128, 512], F32)
                _fence(k)

                @b.vector
                def _(e):
                    e = k.V(e)
                    for kk in range(16):
                        e.memset(VT[:, kk * 65 + 64:kk * 65 + 65], 1.0)
                vt_ones = k.v("vec")

                for hh in range(H):
                    dch = hh // 2
                    pbase = (hh % 2) * 64
                    atfl = k.cnt["_atfl"]

                    @b.sync
                    def _(e, hh=hh, bb=bb, atfl=atfl, pbase=pbase):
                        e.wait_ge(k.sem["pe"], atfl)
                        k.dwait_all(e, "st")
                        k.ldh = k.dma(e, "ldw", KH[pbase:pbase + 64, :],
                                      kv_sp[bb, 0, hh * 64:(hh + 1) * 64, :])
                        k.ldh2 = k.dma(e, "ldw", VHF[0:64, :],
                                       kv_sp[bb, 1, hh * 64:(hh + 1) * 64, :])
                    kh_h, vh_h = k.ldh, k.ldh2

                    pe0 = k.v("pe")
                    vec0v = k.v("vec")

                    @b.tensor
                    def _(e, vh_h=vh_h, vec0v=vec0v, vecnow=k.v("vec")):
                        k.dwait(e, vh_h)
                        e.wait_ge(k.sem["vec"], vecnow)
                        for ck in range(16):
                            if ck >= 2:
                                e.wait_ge(k.sem["vec"], vec0v + ck - 1)
                            ins = e.transpose(
                                out=pb[6 + ck % 2][:, 0:32].bitcast(BF16),
                                in_=VHF[0:64, ck * 128:(ck + 1) * 128],
                                identity=k.identb[0:64, 0:64])
                            k.inc(ins, "pe")

                    @b.vector
                    def _(e, pe0=pe0, atfl=atfl, vt_ones=vt_ones):
                        e = k.V(e)
                        e._e.wait_ge(k.sem["pe"], atfl)
                        e._e.wait_ge(k.sem["vec"], vt_ones)
                        for ck in range(16):
                            e._e.wait_ge(k.sem["pe"], pe0 + ck + 1)
                            e.tensor_copy(out=VT[:, ck * 65:ck * 65 + 64],
                                          in_=pb[6 + ck % 2][:, 0:32].bitcast(BF16))
                    vt_rdy = k.v("vec")

                    for tqb in range(4):
                        avb = 2 + tqb % 2
                        BK = (0, 1, 4, 5)
                        pe0f = k.v("pe")
                        sc0f = k.v("sc")
                        smap, pmap = {}, {}
                        cnt = pe0f
                        for u in range(4):
                            cnt += 1
                            smap[u] = cnt
                        for tt in range(16):
                            cnt += 1
                            pmap[tt] = cnt
                            if tt + 4 < 16:
                                cnt += 1
                                smap[tt + 4] = cnt

                        @b.tensor
                        def _(e, tqb=tqb, hh=hh, kh_h=kh_h, dch=dch,
                              pbase=pbase, sbk=dict(sc_bank), BK=BK,
                              vt_rdy=vt_rdy, avw=av_rd[tqb % 2], avb=avb,
                              sc0f=sc0f):
                            k.dwait(e, kh_h)
                            e.wait_ge(k.sem["vec"], vt_rdy)

                            def sc_mm(u):
                                if u < 4:
                                    e.wait_ge(k.sem["sc"], sbk[BK[u % 4]])
                                else:
                                    e.wait_ge(k.sem["sc"], sc0f + u - 3)
                                ins = e.matmul(
                                    out=pb[BK[u % 4]][:, 0:512],
                                    lhsT=KH[pbase:pbase + 64,
                                            u * 128:(u + 1) * 128],
                                    rhs=QBUF3[pbase:pbase + 64, dch,
                                              tqb * 512:(tqb + 1) * 512],
                                    start=True, stop=True)
                                k.inc(ins, "pe")

                            def pv_mm(t):
                                e.wait_ge(k.sem["sc"], sc0f + t + 1)
                                if t == 0:
                                    e.wait_ge(k.sem["vec"], avw)
                                ins = e.matmul(
                                    out=pb[avb][0:65, 0:512],
                                    lhsT=VT[:, t * 65:t * 65 + 65],
                                    rhs=PT[:, (t % 4) * 512:(t % 4 + 1) * 512],
                                    start=(t == 0), stop=(t == 15))
                                k.inc(ins, "pe")

                            for u in range(4):
                                sc_mm(u)
                            for t in range(16):
                                pv_mm(t)
                                if t + 4 < 16:
                                    sc_mm(t + 4)

                        @b.scalar
                        def _(e, smap=dict(smap), pmap=dict(pmap),
                              ptd=dict(pt_rd), cbr_h=cbr_h, BK=BK):
                            e = k.S(e)
                            k.dwait(e._e, cbr_h)
                            for t in range(16):
                                e._e.wait_ge(k.sem["pe"], smap[t])
                                if t < 4:
                                    e._e.wait_ge(k.sem["pe"], ptd[t % 4])
                                else:
                                    e._e.wait_ge(k.sem["pe"], pmap[t - 4])
                                e.activation(
                                    out=PT[:, (t % 4) * 512:(t % 4 + 1) * 512],
                                    in_=pb[BK[t % 4]][:, 0:512],
                                    func=AF.Exp, scale=0.125, bias=CB[:, 0:1])
                        assert k.v("pe") == pe0f + 32
                        assert k.v("sc") == sc0f + 16
                        sc_bank[0] = sc0f + 13
                        sc_bank[1] = sc0f + 14
                        sc_bank[4] = sc0f + 15
                        sc_bank[5] = sc0f + 16
                        for tt in range(12, 16):
                            pt_rd[tt % 4] = pmap[tt]
                        pv_last = k.v("pe")

                        rcb = 6 + tqb % 2

                        @b.vector
                        def _(e, pv_last=pv_last, avb=avb,
                              rlw=k.cnt.get("_rlmm", 0)):
                            e = k.V(e)
                            e._e.wait_ge(k.sem["pe"], pv_last)
                            e._e.wait_ge(k.sem["pe"], rlw)
                            e.reciprocal(out=RLS[0:1, :], in_=pb[avb][64:65, 0:512])
                        rec_v = k.v("vec")

                        @b.tensor
                        def _(e, rec_v=rec_v, rcb=rcb, vt_rdy=vt_rdy,
                              rlrd=k.cnt.get("_rlrd%d" % (6 + tqb % 2), 0)):
                            e.wait_ge(k.sem["vec"], rec_v)
                            e.wait_ge(k.sem["vec"], rlrd)
                            e.wait_ge(k.sem["vec"], vt_rdy)
                            ins = e.matmul(out=pb[rcb][:, 0:512],
                                           lhsT=ONERF[0:1, :], rhs=RLS[0:1, :],
                                           start=True, stop=True)
                            k.inc(ins, "pe")
                        k.cnt["_rlmm"] = k.v("pe")
                        rlp = k.v("pe")

                        odd = pbase == 64

                        @b.vector
                        def _(e, rlp=rlp, avb=avb, rcb=rcb, odd=odd, dch=dch,
                              tqb=tqb, avs_h=avs_h):
                            e = k.V(e)
                            e._e.wait_ge(k.sem["pe"], rlp)
                            if odd and avs_h is not None:
                                k.dwait(e._e, avs_h)
                            e.tensor_copy(out=RL[0:64, :], in_=pb[rcb][0:64, 0:512])
                            outp = (AVS[0:64, :] if odd else
                                    QBUF3[0:64, dch, tqb * 512:(tqb + 1) * 512])
                            e.tensor_tensor(out=outp, in0=pb[avb][0:64, 0:512],
                                            in1=RL[0:64, :], op=OP.mult)
                        ttv = k.v("vec")
                        av_rd[avb % 2] = ttv
                        k.cnt["_rlrd%d" % rcb] = ttv

                        if odd:
                            @b.gpsimd
                            def _(e, ttv=ttv, dch=dch, tqb=tqb):
                                e.wait_ge(k.sem["vec"], ttv)
                                k.ldh = k.dma(e, "st",
                                              QBUF3[64:128, dch,
                                                    tqb * 512:(tqb + 1) * 512],
                                              AVS[0:64, :])
                            avs_h = k.ldh
                    k.cnt["_atfl"] = k.v("pe")

            with ExitStack() as s4:
                sb4 = lambda nm, sh, dt: s4.enter_context(
                    nc.sbuf_tensor(f"{nm}at{bb}", sh, dt))
                WS2 = sb4("ws2", [128, 4 * 2 * 256], F32)
                ws4o = WS2[:].rearrange("p (w u c) -> p w u c", w=4, u=2)
                _fence(k)
                ob_rd = {0: 0, 1: 0}

                @b.sync
                def _(e):
                    k.dwait_all(e, "st")
                    k.ldh = k.dma(e, "ldw",
                                  WS2[:].rearrange("p (w u c) -> p w u c", w=4, u=2),
                                  t["oW"].rearrange("w u p c -> p w u c"))
                    k.ldh2 = k.dma(e, "ldw", BL[:, 0:8],
                                   t["ob"].rearrange("w (u c) -> c (w u)", u=2),
                                   allow_slow_non_contiguous=True)
                w_h, b_h = k.ldh, k.ldh2

                boh = None
                for oh in range(2):
                    penow = k.v("pe")
                    scnow = k.v("sc")

                    @b.vector
                    def _(e, oh=oh, w_h=w_h, b_h=b_h, penow=penow, scnow=scnow):
                        e = k.V(e)
                        k.dwait(e._e, w_h)
                        k.dwait(e._e, b_h)
                        e._e.wait_ge(k.sem["pe"], penow)
                        e._e.wait_ge(k.sem["sc"], scnow)
                        for c in range(4):
                            for comp in range(4):
                                w, s = _QUAT_CW[(c, comp)]
                                for rh in range(2):
                                    e.tensor_scalar(
                                        out=ql4[:, c * 2 + rh, comp, :],
                                        in0=ws4o[:, w, rh, oh * 128:(oh + 1) * 128],
                                        scalar1=float(s), scalar2=0.0,
                                        op0=OP.mult, op1=OP.bypass)
                        for comp in range(4):
                            for wi, (c, s) in enumerate(_QUAT_TABLE[comp]):
                                if wi == 0:
                                    e.tensor_scalar(
                                        out=BIA[:, comp:comp + 1],
                                        in0=BL[:, wi * 2 + oh:wi * 2 + oh + 1],
                                        scalar1=float(s), scalar2=0.0,
                                        op0=OP.mult, op1=OP.bypass)
                                else:
                                    e.scalar_tensor_tensor(
                                        out=BIA[:, comp:comp + 1],
                                        in0=BL[:, wi * 2 + oh:wi * 2 + oh + 1],
                                        scalar=float(s),
                                        in1=BIA[:, comp:comp + 1],
                                        op0=OP.mult, op1=OP.add)
                    asm_v = k.v("vec")

                    @b.tensor
                    def _(e, asm_v=asm_v, vecnow=k.v("vec")):
                        e.wait_ge(k.sem["vec"], vecnow)
                        ins = e.transpose(out=pb[6][0:4, 0:128], in_=BIA[:, 0:4],
                                          identity=k.ident)
                        k.inc(ins, "pe")
                    btr = k.v("pe")

                    @b.vector
                    def _(e, btr=btr):
                        e = k.V(e)
                        e._e.wait_ge(k.sem["pe"], btr)
                        e.tensor_copy(out=BO8[0:4, :], in_=pb[6][0:4, 0:128])
                    bo_v = k.v("vec")

                    @b.gpsimd
                    def _(e, bo_v=bo_v, boh=boh):
                        e.wait_ge(k.sem["vec"], bo_v)
                        if boh is not None:
                            k.dwait(e, boh)
                        k.ldh = k.dma(e, "st",
                                      bo_row.rearrange("one (a c) -> a one c", a=4),
                                      BO8[0:4, :].unsqueeze(1))
                    bst = k.ldh

                    @b.sync
                    def _(e, bst=bst, penow=k.v("pe")):
                        k.dwait(e, bst)
                        e.wait_ge(k.sem["pe"], penow)
                        k.ldh = k.dma(e, "ldw", BOROW[0:1, :], bo_row)
                    boh = k.ldh

                    for g in range(16):
                        og = oh * 16 + g
                        gg = bb * 16 + g

                        @b.tensor
                        def _(e, g=g, og=og, oh=oh, boh=boh, asm_v=asm_v,
                              obw=ob_rd[og % 2]):
                            e.wait_ge(k.sem["vec"], asm_v)
                            e.wait_ge(k.sem["vec"], obw)
                            k.dwait(e, boh)
                            for dc in range(8):
                                e.matmul(out=pb[og % 2][:, 0:512],
                                         lhsT=QBUF3[:, dc, g * 128:(g + 1) * 128],
                                         rhs=QL[:].rearrange("p (dc c) -> p dc c", dc=8)
                                         [:, dc, :],
                                         start=(dc == 0), stop=False)
                            ins = e.matmul(out=pb[og % 2][:, 0:512],
                                           lhsT=ONER[0:1, :], rhs=BOROW[0:1, :],
                                           start=False, stop=True)
                            k.inc(ins, "pe")
                        omm = k.v("pe")

                        @b.vector
                        def _(e, og=og, gg=gg, oh=oh, omm=omm):
                            e = k.V(e)
                            e._e.wait_ge(k.sem["pe"], omm)
                            e.tensor_tensor(
                                out=x34[:, gg, :, oh, :],
                                in0=pb[og % 2][:, 0:512]
                                .rearrange("p (comp c) -> p comp c", c=128),
                                in1=x34[:, gg, :, oh, :], op=OP.add)
                        ob_rd[og % 2] = k.v("vec")


_PHASES_ALL = ("l0f1", "mamba", "l0f2", "l1f1", "l1attn", "l1f2")


def kernel(**inputs):
    common, per_core = host_prepare(inputs)
    nc = build_program(_PHASES_ALL)
    in_maps = [{**common, **m} for m in per_core]
    res = run_bass_kernel_spmd(nc, in_maps, core_ids=list(range(NCORES)))
    out = np.empty((16, L, D), np.float32)
    for c in range(NCORES):
        out[2 * c:2 * c + 2] = res.results[c]["xout"]
    return out


# version 07 (final)
# speedup vs baseline: 1.0200x; 1.0200x over previous
import numpy as np
import ml_dtypes
from contextlib import ExitStack

import concourse.bass as bass
import concourse.mybir as mybir
from concourse.bass_utils import run_bass_kernel_spmd
from concourse.alu_op_type import AluOpType as OP

F32 = mybir.dt.float32
BF16 = mybir.dt.bfloat16
AF = mybir.ActivationFunctionType
AXX = mybir.AxisListType.X
NB = ml_dtypes.bfloat16

NCORES = 8
B_LOC = 2
L = 2048
D = 1024
NTILE = B_LOC * 16
MAGIC = float(1.5 * 2.0 ** 23)
EPS_LN = 1e-5



_AUTO_OPS = {
    "tensor_tensor", "tensor_scalar", "tensor_copy", "tensor_reduce",
    "reduce_max", "reduce_sum", "scalar_tensor_tensor", "tensor_tensor_scan",
    "reciprocal", "memset", "memzero", "bn_stats", "bn_aggr", "select",
    "copy_predicated", "iota", "activation", "max_index", "max_with_indices",
    "tensor_tensor_reduce", "stream_shuffle", "transpose", "tensor_add",
    "tensor_mul", "tensor_sub", "tensor_max", "copy", "add", "mul",
    "affine_then_add", "affine_mul_reduce",
}


class EP:

    def __init__(self, k, e, sem):
        self._k, self._e, self._sem = k, e, sem

    def __getattr__(self, name):
        f = getattr(self._e, name)
        if name in _AUTO_OPS:
            k, sem, ee = self._k, self._sem, self._e

            def g(*a, **kw):
                ee.wait_ge(k.sem[sem], k.cnt[sem])
                ins = f(*a, **kw)
                k.inc(ins, sem)
                return ins
            return g
        return f


class K:
    def __init__(self, nc, block, stack):
        self.nc = nc
        self.block = block
        self.stack = stack
        self.sem = {}
        self.cnt = {}
        self.buf = {}
        self.pool = {}
        self.rr = {}

    def mksem(self, *names):
        for n in names:
            self.sem[n] = self.stack.enter_context(self.nc.semaphore(n))
            self.cnt[n] = 0

    def mkpool(self, tag, n):
        names = [f"{tag}{i}" for i in range(n)]
        self.mksem(*names)
        self.pool[tag] = names
        self.rr[tag] = 0

    def dma(self, e, tag, out, in_, **kw):
        names = self.pool[tag]
        s = names[self.rr[tag] % len(names)]
        self.rr[tag] += 1
        e.wait_ge(self.sem[s], self.cnt[s])
        ins = e.dma_start(out=out, in_=in_, **kw)
        self.dinc(ins, s)
        return (s, self.cnt[s])

    def dwait(self, e, h):
        e.wait_ge(self.sem[h[0]], h[1])

    def dwait_all(self, e, tag):
        for s in self.pool[tag]:
            e.wait_ge(self.sem[s], self.cnt[s])

    def inc(self, ins, name, n=1):
        ins.then_inc(self.sem[name], n)
        self.cnt[name] += n
        return self.cnt[name]

    def dinc(self, ins, name):
        return self.inc(ins, name, 16)

    def v(self, name):
        return self.cnt[name]

    def wait(self, eng, name, val=None):
        eng.wait_ge(self.sem[name], self.cnt[name] if val is None else val)

    def V(self, e):
        return EP(self, e, "vec")

    def S(self, e):
        return EP(self, e, "sc")

    def G(self, e):
        return EP(self, e, "gp")

    def mark(self, name):
        self.cnt[name] = self.cnt["vec"]

    def sb(self, name, shape, dtype):
        h = self.stack.enter_context(self.nc.sbuf_tensor(name, shape, dtype))
        self.buf[name] = h
        return h


def host_prepare(inputs):
    f = lambda a: np.ascontiguousarray(np.asarray(a, dtype=np.float32))
    c = {}
    x = f(inputs["L0_attn_x"])

    c["Wg"] = f(inputs["L0_ffn_Wg"]).reshape(8, 128, 16)
    c["Wq"] = f(inputs["L0_ffn_Wq"]).reshape(16, 8, 128, 64)
    c["Wkv"] = f(inputs["L0_ffn_Wkv"]).reshape(16, 8, 128, 128)
    c["Wo"] = f(inputs["L0_ffn_Wo"]).reshape(1024, 1024)
    c["bqT"] = f(inputs["L0_ffn_bq"]).reshape(16, 64, 1)
    c["bkvT"] = f(inputs["L0_ffn_bkv"]).reshape(16, 128, 1)
    c["bo_rep"] = np.ascontiguousarray(
        np.broadcast_to(f(inputs["L0_ffn_bo"])[None, :], (128, 1024)))
    c["lng_rep"] = np.ascontiguousarray(
        np.broadcast_to(f(inputs["L0_ffn_ln_g"])[None, :], (128, 1024)))
    c["lnb_rep"] = np.ascontiguousarray(
        np.broadcast_to(f(inputs["L0_ffn_ln_b"])[None, :], (128, 1024)))

    c["dWT"] = f(inputs["L0_attn_delta_W"]).T.copy().reshape(8, 128, 8, 128)
    c["BCWT"] = np.concatenate(
        [f(inputs["L0_attn_B_W"]).T, f(inputs["L0_attn_C_W"]).T], axis=1
    ).copy().reshape(8, 128, 32)
    c["A_log"] = f(inputs["L0_attn_A_log"]).reshape(8, 128, 16)
    c["dbias"] = f(inputs["L0_attn_delta_b"]).reshape(8, 128).T.copy()
    c["dtbias"] = f(inputs["L0_attn_dt_bias"]).reshape(8, 128).T.copy()
    c["WiT"] = f(inputs["L0_attn_Wi_W"]).T.copy().reshape(8, 128, 16)
    c["Wib"] = f(inputs["L0_attn_Wi_b"]).reshape(16, 1)
    c["WrT"] = f(inputs["L0_attn_Wr_W"]).T.copy()
    c["tau"] = f(inputs["L0_attn_tau"]).reshape(16, 1)
    c["selWT"] = f(inputs["L0_attn_sel_W"]).T.copy().reshape(16, 8, 128)
    c["selb"] = f(inputs["L0_attn_sel_b"]).reshape(8, 128).T.copy()
    c["h0T"] = f(inputs["L0_attn_h0"]).T.copy()
    c["Bb"] = f(inputs["L0_attn_B_b"]).reshape(16, 1)
    c["Cb"] = f(inputs["L0_attn_C_b"]).reshape(16, 1)

    for nm, key in [("qW", "L1_attn_qW"), ("kW", "L1_attn_kW"),
                    ("vW", "L1_attn_vW"), ("oW", "L1_attn_oW")]:
        c[nm] = f(inputs[key]).reshape(4, 2, 128, 256)
    for nm, key in [("qb", "L1_attn_qb"), ("kb", "L1_attn_kb"),
                    ("vb", "L1_attn_vb"), ("ob", "L1_attn_ob")]:
        c[nm] = f(inputs[key])
    c["J"] = f(inputs["L1_attn_J"]).reshape(8, 128).T.copy()

    c["w1T"] = f(inputs["L1_ffn_w1"]).T.copy().reshape(8, 128, 4096)
    c["w2T"] = f(inputs["L1_ffn_w2"]).T.copy().reshape(32, 128, 1024)
    c["convw"] = f(inputs["L1_ffn_conv_w"]).reshape(32, 128, 3)
    c["convb"] = f(inputs["L1_ffn_conv_b"]).reshape(32, 128, 1)

    c["cmisc"] = np.ascontiguousarray(np.broadcast_to(
        np.array([1e-5, 1e-6, 0, 0, 0, 0, 0, 0], np.float32)[None, :], (128, 8)))
    c["ident"] = np.eye(128, dtype=np.float32)
    c["identb"] = np.eye(128, dtype=NB)
    c["ones"] = np.ones((128, 1), np.float32)
    io = np.zeros((2, 128, 8, 16), np.float32)
    for s2 in range(2):
        io[s2, :, :, :] = (2 * np.arange(8)[None, :, None] + s2 - 16)
    c["iotas"] = io

    per_core = []
    for core in range(NCORES):
        own = [2 * core, 2 * core + 1]
        m = {"xin": np.ascontiguousarray(x[own[0]:own[0] + 2])}
        ct = np.full((16, 16, 8), -1.0, np.float32)
        for s2 in range(2):
            for b8 in range(8):
                ct[s2 * 8 + b8, :, b8] = float(own[s2])
        m["Ctilde"] = ct.reshape(16, 128)
        m["own0"] = np.array([[float(own[0])]], np.float32)
        m["own1"] = np.array([[float(own[1])]], np.float32)
        per_core.append(m)
    return c, per_core


def build_program(phases=("l0f1", "mamba", "l0f2", "l1f1", "l1attn", "l1f2")):
    nc = bass.Bass("TRN2", target_bir_lowering=False, debug=False)
    common_shapes, percore_shapes = _declare_shapes()
    t = {}
    for name, (shape, dt) in {**common_shapes, **percore_shapes}.items():
        t[name] = nc.dram_tensor(name, shape, dt, kind="ExternalInput").ap()
    xout = nc.dram_tensor("xout", [B_LOC, L, D], F32, kind="ExternalOutput").ap()

    g_ag_in = nc.dram_tensor("g_ag_in", [16, 4096], F32)
    g_ag_out = nc.dram_tensor("g_ag_out", [128, 4096], F32)
    att_ar_in = nc.dram_tensor("att_ar_in", [128, 256], F32)
    att_ar_out = nc.dram_tensor("att_ar_out", [128, 256], F32)
    w1q_cache = nc.dram_tensor("w1q_cache", [8, 128, 4096], BF16)
    w2q_cache = nc.dram_tensor("w2q_cache", [32, 128, 1024], BF16)
    kv_spill = nc.dram_tensor("kv_spill", [2, 2, 1024, 2048], BF16)
    esc_row = nc.dram_tensor("esc_row", [1, B_LOC * L], F32)
    s2_row = nc.dram_tensor("s2_row", [1, B_LOC * L], F32)
    s2e_row = nc.dram_tensor("s2e_row", [1, B_LOC * L], F32)

    with ExitStack() as stack:
        block = stack.enter_context(nc.Block())
        k = K(nc, block, stack)
        k.dram = dict(g_ag_in=g_ag_in, g_ag_out=g_ag_out, att_ar_in=att_ar_in,
                      att_ar_out=att_ar_out, w1q_cache=w1q_cache, w2q_cache=w2q_cache,
                      kv_spill=kv_spill, esc_row=esc_row, s2_row=s2_row, s2e_row=s2e_row)
        k.t = t
        k.mksem("pe", "vec", "sc", "cc", "gp")
        k.mkpool("ldw", 4)
        k.mkpool("st", 4)
        k.mkpool("ldg", 2)
        k.mkpool("ldx", 2)
        k.mkpool("sth", 2)

        k.pb = [stack.enter_context(nc.psum_tensor(f"pb{i}", [128, 512], F32))
                for i in range(8)]
        XRES = k.sb("XRES", [128, NTILE * D], F32)
        k.SWP = k.sb("SWP", [128, 8], F32)
        k.LNTMP = k.sb("LNTMP", [128, 12], F32)
        STATS = k.sb("STATS", [128, NTILE * 2], F32)
        CONST = k.sb("CONST", [128, 144], F32)
        CONSTB = k.sb("CONSTB", [128, 128], BF16)
        k.XRES = XRES
        k.STATS = STATS

        x3 = XRES[:].rearrange("p (g d) -> p g d", g=NTILE)
        k.x3 = x3
        st3 = STATS[:].rearrange("p (g two) -> p g two", g=NTILE)
        k.st3 = st3
        k.ident = CONST[:, 0:128]
        k.ones = CONST[:, 128:129]
        k.identb = CONSTB[:, 0:128]
        k.eps_ln = CONST[:, 136:137]
        k.eps_q = CONST[:, 137:138]

        @block.sync
        def _(e):
            k.dma(e, "ldx", x3, t["xin"].rearrange("s (ti p) d -> p (s ti) d", p=128))
            k.dma(e, "ldx", k.ident, t["ident"])
            k.dma(e, "ldx", k.ones, t["ones"])
            k.dma(e, "ldx", k.identb, t["identb"])
            k.dma(e, "ldx", CONST[:, 136:144], t["cmisc"])

        PH = dict(l0f1=lambda: phase_l0ffn(k, call=0),
                  mamba=lambda: phase_mamba(k),
                  l0f2=lambda: phase_l0ffn(k, call=1),
                  l1f1=lambda: phase_l1ffn(k, call=0),
                  l1attn=lambda: phase_l1attn(k),
                  l1f2=lambda: phase_l1ffn(k, call=1))
        for p in phases:
            PH[p]()

        @block.sync
        def _(e):
            k.wait(e, "vec")
            k.wait(e, "sc")
            k.wait(e, "pe")
            h = k.dma(e, "sth", xout.rearrange("s (ti p) d -> p (s ti) d", p=128), x3)
            k.dwait(e, h)
    return nc


def _declare_shapes():
    common = {
        "Wg": ([8, 128, 16], F32), "Wq": ([16, 8, 128, 64], F32),
        "Wkv": ([16, 8, 128, 128], F32), "Wo": ([1024, 1024], F32),
        "bqT": ([16, 64, 1], F32), "bkvT": ([16, 128, 1], F32),
        "bo_rep": ([128, 1024], F32),
        "lng_rep": ([128, 1024], F32), "lnb_rep": ([128, 1024], F32),
        "dWT": ([8, 128, 8, 128], F32), "BCWT": ([8, 128, 32], F32),
        "A_log": ([8, 128, 16], F32), "dbias": ([128, 8], F32),
        "dtbias": ([128, 8], F32), "WiT": ([8, 128, 16], F32),
        "Wib": ([16, 1], F32), "WrT": ([16, 16], F32), "tau": ([16, 1], F32),
        "selWT": ([16, 8, 128], F32), "selb": ([128, 8], F32),
        "h0T": ([16, 1], F32), "Bb": ([16, 1], F32), "Cb": ([16, 1], F32),
        "qW": ([4, 2, 128, 256], F32), "kW": ([4, 2, 128, 256], F32),
        "vW": ([4, 2, 128, 256], F32), "oW": ([4, 2, 128, 256], F32),
        "qb": ([4, 256], F32), "kb": ([4, 256], F32), "vb": ([4, 256], F32),
        "ob": ([4, 256], F32), "J": ([128, 8], F32),
        "w1T": ([8, 128, 4096], F32), "w2T": ([32, 128, 1024], F32),
        "convw": ([32, 128, 3], F32), "convb": ([32, 128, 1], F32),
        "ident": ([128, 128], F32), "identb": ([128, 128], BF16),
        "ones": ([128, 1], F32), "iotas": ([2, 128, 8, 16], F32),
        "cmisc": ([128, 8], F32),
    }
    percore = {
        "xin": ([B_LOC, L, D], F32),
        "Ctilde": ([16, 128], F32),
        "own0": ([1, 1], F32), "own1": ([1, 1], F32),
    }
    return common, percore


def _snap(k):
    return dict(k.cnt)


def _bar(k, e, snap):
    for n, v in snap.items():
        if n in k.sem and v > 0:
            e.wait_ge(k.sem[n], v)


def ln_stats_all(k):
    b = k.block
    tmp6 = k.LNTMP

    @b.vector
    def _(e):
        e = k.V(e)
        k.dwait_all(e._e, "ldx")
        for g in range(NTILE):
            e.bn_stats(out=tmp6[:, 0:6], in_=k.x3[:, g, 0:512])
            e.bn_stats(out=tmp6[:, 6:12], in_=k.x3[:, g, 512:1024])
            e.bn_aggr(out=k.st3[:, g, :], in_=tmp6[:])
    vdone = k.v("vec")

    @b.scalar
    def _(e):
        e = k.S(e)
        e._e.wait_ge(k.sem["vec"], vdone)
        var = k.st3[:, :, 1]
        e.activation(out=var, in_=var, func=AF.Ln, bias=k.eps_ln)
        e.activation(out=var, in_=var, func=AF.Exp, scale=-0.5)
    return k.v("sc")


def tr_chain(k, pairs, wait_before=None, banks=(6, 7)):
    b = k.block
    pe0 = k.v("pe")
    vec0 = k.v("vec")

    def pout(i, dt):
        bank = k.pb[banks[i % 2]]
        return bank[:, 0:64].bitcast(BF16) if dt == BF16 else bank[:, 0:128]

    @b.tensor
    def _(e):
        if wait_before is not None:
            for nm, v in wait_before:
                e.wait_ge(k.sem[nm], v)
        for i, (src, dst) in enumerate(pairs):
            if i >= 2:
                e.wait_ge(k.sem["vec"], vec0 + i - 1)
            ident = k.identb if src.dtype == BF16 else k.ident
            ins = e.transpose(out=pout(i, src.dtype), in_=src, identity=ident)
            k.inc(ins, "pe")

    @b.vector
    def _(e):
        e = k.V(e)
        for i, (src, dst) in enumerate(pairs):
            e._e.wait_ge(k.sem["pe"], pe0 + i + 1)
            e.tensor_copy(out=dst, in_=pout(i, src.dtype))


def _quant_weights(k):
    nc, b, t = k.nc, k.block, k.t
    qs = ExitStack()
    sb = lambda nm, sh, dt: qs.enter_context(nc.sbuf_tensor(nm + "q", sh, dt))
    WQF = sb("wqf", [128, 4096], F32)
    WQB = sb("wqb", [128, 4096], BF16)
    ACC = sb("accq", [128, 2], F32)
    SCR = sb("scrq", [1, 4], F32)
    k.swp_handles = []
    for (src, dst, nkt, C, swi) in [
            (k.t["w1T"], k.dram["w1q_cache"].ap(), 8, 4096, 0),
            (k.t["w2T"], k.dram["w2q_cache"].ap(), 32, 1024, 2)]:
        nchunk = (nkt * C) // 4096
        ktpc = max(1, 4096 // C)
        view = src.rearrange("kt p c -> p kt c")
        dstv = dst.rearrange("kt p c -> p kt c")
        for ch in range(nchunk):
            vnow = k.v("vec")

            @b.sync
            def _(e, ch=ch, vnow=vnow):
                e.wait_ge(k.sem["vec"], vnow)
                k.ldh = k.dma(e, "ldw", WQF[:, 0:4096], view[:, ch * ktpc:(ch + 1) * ktpc, :])
            ldh = k.ldh

            @b.vector
            def _(e, ch=ch, ldh=ldh):
                e = k.V(e)
                k.dwait(e._e, ldh)
                e.tensor_reduce(out=ACC[:, 1:2], in_=WQF[:, 0:4096], axis=AXX,
                                op=OP.add, apply_absolute_value=True)
                if ch == 0:
                    e.tensor_copy(out=ACC[:, 0:1], in_=ACC[:, 1:2])
                else:
                    e.tensor_tensor(out=ACC[:, 0:1], in0=ACC[:, 0:1],
                                    in1=ACC[:, 1:2], op=OP.add)
        accv = k.v("vec")

        @b.tensor
        def _(e, accv=accv):
            e.wait_ge(k.sem["vec"], accv)
            ins = e.matmul(out=k.pb[5][0:1, 0:1], lhsT=k.ones, rhs=ACC[:, 0:1],
                           start=True, stop=True)
            k.inc(ins, "pe")
        pesum = k.v("pe")

        @b.vector
        def _(e, pesum=pesum, nkt=nkt, C=C):
            e = k.V(e)
            e._e.wait_ge(k.sem["pe"], pesum)
            e.tensor_scalar(out=SCR[0:1, 0:1], in0=k.pb[5][0:1, 0:1],
                            scalar1=1.0 / (nkt * 128 * C), scalar2=0.0,
                            op0=OP.mult, op1=OP.bypass)
            e.tensor_scalar(out=SCR[0:1, 1:2], in0=SCR[0:1, 0:1], scalar1=1e-5,
                            scalar2=0.0, op0=OP.max, op1=OP.bypass)
            e.reciprocal(out=SCR[0:1, 0:1], in_=SCR[0:1, 1:2])
        swv = k.v("vec")

        @b.sync
        def _(e, swv=swv, swi=swi):
            e.wait_ge(k.sem["vec"], swv)
            k.swp_handles.append(k.dma(e, "ldw", k.SWP[:, swi:swi + 2],
                                       SCR[0:1, 0:2].unsqueeze(1).broadcast_to([1, 128, 2])))
        for ch in range(nchunk):
            vnow = k.v("vec")

            @b.sync
            def _(e, ch=ch, vnow=vnow):
                for hh in k.swp_handles:
                    k.dwait(e, hh)
                e.wait_ge(k.sem["vec"], vnow)
                k.ldh = k.dma(e, "ldw", WQF[:, 0:4096], view[:, ch * ktpc:(ch + 1) * ktpc, :])
            ldh = k.ldh

            @b.vector
            def _(e, ch=ch, ldh=ldh, swi=swi):
                e = k.V(e)
                k.dwait(e._e, ldh)
                k.dwait_all(e._e, "st")
                e.tensor_scalar(out=WQF[:, 0:4096], in0=WQF[:, 0:4096],
                                scalar1=k.SWP[:, swi:swi + 1], scalar2=MAGIC,
                                op0=OP.mult, op1=OP.add)
                e.tensor_scalar(out=WQF[:, 0:4096], in0=WQF[:, 0:4096],
                                scalar1=MAGIC, scalar2=-1.0, op0=OP.subtract, op1=OP.max)
                e.tensor_scalar(out=WQB[:, 0:4096], in0=WQF[:, 0:4096], scalar1=1.0,
                                scalar2=0.0, op0=OP.min, op1=OP.bypass)
            qv = k.v("vec")

            @b.gpsimd
            def _(e, ch=ch, qv=qv):
                e.wait_ge(k.sem["vec"], qv)
                k.dma(e, "st", dstv[:, ch * ktpc:(ch + 1) * ktpc, :], WQB[:, 0:4096])
    qs.close()


def phase_l1ffn(k, call):
    nc, b, t = k.nc, k.block, k.t
    tag = f"f{call}"
    SCH = 256
    NSCH = (B_LOC * L) // SCH
    HT = 32
    if "hs_spill" not in k.dram:
        k.dram["hs_spill"] = nc.dram_tensor("hs_spill", [HT, 128, B_LOC * L], F32)
        k.dram["xqT_spill"] = nc.dram_tensor("xqT_spill", [8, 128, B_LOC * L], BF16)
    hs_spill = k.dram["hs_spill"].ap()
    xqT_spill = k.dram["xqT_spill"].ap()
    w1q_c = k.dram["w1q_cache"].ap()
    w2q_c = k.dram["w2q_cache"].ap()
    esc_d = k.dram["esc_row"].ap()
    s2_d = k.dram["s2_row"].ap()
    s2e_d = k.dram["s2e_row"].ap()
    pb = k.pb

    with ExitStack() as ps:
        with ExitStack() as ps1:
            sb = lambda nm, sh, dt: ps1.enter_context(nc.sbuf_tensor(nm + tag, sh, dt))
            RS = sb("rs", [1, B_LOC * L], F32)
            LNA = sb("lna", [128, D], F32)
            XQ = sb("xq", [128, D], BF16)
            XQT = sb("xqt", [128, 8 * 128], BF16)
            SXC = sb("sxc", [128, 8], F32)
            SXT = sb("sxt", [128, 32], F32)
            SXTT = sb("sxtt", [128, 128], F32)
            CW = sb("cw", [128, HT * 4], F32)
            RMAX = sb("rmax", [128, B_LOC * L], F32)
            cw4 = CW[:].rearrange("p (h f) -> p h f", h=HT)

            snap0 = _snap(k)

            @b.sync
            def _(e):
                _bar(k, e, snap0)
                k.ldh = k.dma(e, "ldw", cw4[:, :, 0:3], t["convw"].transpose([1, 0, 2]),
                              allow_slow_non_contiguous=True)
                k.ldh2 = k.dma(e, "ldw", cw4[:, :, 3:4], t["convb"].transpose([1, 0, 2]),
                               allow_slow_non_contiguous=True)
            cwa_h, cwb_h = k.ldh, k.ldh2

            @b.vector
            def _(e, cwa_h=cwa_h, cwb_h=cwb_h):
                k.dwait(e, cwa_h)
                k.dwait(e, cwb_h)

            ln_stats_all(k)
            sc_rdy = k.v("sc")

            for g in range(NTILE):
                @b.vector
                def _(e, g=g):
                    e = k.V(e)
                    if g == 0:
                        e._e.wait_ge(k.sem["sc"], sc_rdy)
                    else:
                        k.dwait_all(e._e, "st")
                    e.tensor_scalar(out=LNA[:], in0=k.x3[:, g, :],
                                    scalar1=k.st3[:, g, 0:1], scalar2=k.st3[:, g, 1:2],
                                    op0=OP.subtract, op1=OP.mult)
                    e.tensor_reduce(out=SXC[:, 0:1], in_=LNA[:], axis=AXX, op=OP.max,
                                    apply_absolute_value=True)
                    e.tensor_scalar(out=SXC[:, 1:2], in0=SXC[:, 0:1], scalar1=1e-5,
                                    scalar2=0.0, op0=OP.max, op1=OP.bypass)
                    e.reciprocal(out=SXC[:, 2:3], in_=SXC[:, 1:2])
                    e.tensor_scalar(out=SXT[:, g:g + 1], in0=SXC[:, 2:3], scalar1=127.0,
                                    scalar2=0.0, op0=OP.mult, op1=OP.bypass)
                    e.tensor_scalar(out=LNA[:], in0=LNA[:], scalar1=SXT[:, g:g + 1],
                                    scalar2=MAGIC, op0=OP.mult, op1=OP.add)
                    e.tensor_scalar(out=LNA[:], in0=LNA[:], scalar1=MAGIC, scalar2=-128.0,
                                    op0=OP.subtract, op1=OP.max)
                    e.tensor_scalar(out=XQ[:], in0=LNA[:], scalar1=127.0,
                                    scalar2=0.0, op0=OP.min, op1=OP.bypass)
                tr_chain(k, [(XQ[:, dc * 128:(dc + 1) * 128], XQT[:, dc * 128:(dc + 1) * 128])
                             for dc in range(8)], wait_before=[("vec", k.v("vec"))])
                vq = k.v("vec")

                @b.gpsimd
                def _(e, g=g, vq=vq):
                    e.wait_ge(k.sem["vec"], vq)
                    k.dma(e, "st", xqT_spill[:, :, g * 128:(g + 1) * 128]
                          .rearrange("kt p c -> p kt c"),
                          XQT[:].rearrange("p (kt c) -> p kt c", kt=8))

            vprev = k.v("vec")

            @b.tensor
            def _(e, vprev=vprev):
                e.wait_ge(k.sem["vec"], vprev)
                ins = e.transpose(out=pb[6][0:32, 0:128], in_=SXT[:, 0:32], identity=k.ident)
                k.inc(ins, "pe")
            trd = k.v("pe")

            @b.vector
            def _(e, trd=trd):
                e = k.V(e)
                e._e.wait_ge(k.sem["pe"], trd)
                e.tensor_copy(out=SXTT[0:32, 0:128], in_=pb[6][0:32, 0:128])
            cpv = k.v("vec")

            @b.gpsimd
            def _(e, cpv=cpv):
                e.wait_ge(k.sem["vec"], cpv)
                k.ldh = k.dma(e, "ldg", RS[0:1, 0:4096].rearrange("o (g p) -> o g p", g=32),
                              SXTT[0:32, 0:128])
            sx_h = k.ldh

            if call == 0:
                _quant_weights(k)

            W1B = sb("w1b", [128, 8 * 1024], BF16)
            XQS = sb("xqs", [128, 2 * 8 * 258], BF16)
            ESL = sb("esl", [128, 2 * 258], F32)
            HTI = sb("hti", [128, 258], F32)
            T0 = sb("t0", [128, 256], F32)
            T2 = sb("t2", [128, 256], F32)
            HSB = sb("hsb", [128, 2 * 256], F32)
            TABS = sb("tabs", [128, 256], F32)
            w1blk = W1B[:].rearrange("p (kt c) -> p kt c", kt=8)
            xqs3 = XQS[:].rearrange("p (u kt c) -> p u kt c", u=2, kt=8)
            esl2 = ESL[:].rearrange("p (u c) -> p u c", u=2)
            hs2 = HSB[:].rearrange("p (u c) -> p u c", u=2)

            @b.vector
            def _(e, sx_h=sx_h):
                e = k.V(e)
                e.memset(RMAX[:], 0.0)
                k.dwait(e._e, sx_h)
                for hh in k.swp_handles:
                    k.dwait(e._e, hh)
                e.reciprocal(out=RS[0:1, 0:4096], in_=RS[0:1, 0:4096])
                e.tensor_scalar(out=RS[0:1, 0:4096], in0=RS[0:1, 0:4096],
                                scalar1=k.SWP[0:1, 1:2], scalar2=0.0,
                                op0=OP.mult, op1=OP.bypass)
            escv = k.v("vec")

            @b.gpsimd
            def _(e, escv=escv):
                e.wait_ge(k.sem["vec"], escv)
                k.ldh = k.dma(e, "ldg", esc_d, RS[0:1, 0:4096])
            esc_h = k.ldh

            k.cnt["_convV"] = k.v("vec")
            k.cnt["_pbank0"] = k.v("vec")
            k.cnt["_pbank1"] = k.v("vec")
            k.cnt["_hsG0"] = 0
            k.cnt["_hsG1"] = 0
            k.cnt["_hsV0"] = k.v("vec")
            k.cnt["_hsV1"] = k.v("vec")
            k.cnt["_hsVL"] = k.v("vec")
            k.cnt["_scv0"] = k.v("vec")
            k.cnt["_scv1"] = k.v("vec")
            hs_h = {0: None, 1: None}

            for hb in range(4):
                @b.sync
                def _(e, hb=hb):
                    if hb >= 1:
                        e.wait_ge(k.sem["pe"], k.cnt[f"_hb{hb-1}pe"])
                    if hb == 0:
                        k.dwait_all(e, "st")
                        k.wait(e, "vec")
                    k.ldh = k.dma(e, "ldw", w1blk,
                                  w1q_c[:, :, hb * 1024:(hb + 1) * 1024]
                                  .rearrange("kt p c -> p kt c"))
                w1_h = k.ldh

                for sc_i in range(NSCH):
                    s0 = sc_i * SCH
                    first = (s0 % L) == 0
                    last = ((s0 + SCH) % L) == 0
                    n = 257 if (first or last) else 258
                    off = 0 if first else 1
                    left = s0 if first else s0 - 1

                    @b.sync
                    def _(e, sc_i=sc_i, left=left, n=n, hb=hb, esc_h=esc_h):
                        if sc_i >= 2:
                            e.wait_ge(k.sem["pe"], k.cnt[f"_s{sc_i-2}"])
                            e.wait_ge(k.sem["vec"], k.cnt[f"_scv{sc_i % 2}"])
                        elif hb > 0:
                            e.wait_ge(k.sem["pe"], k.cnt[f"_hb{hb-1}pe"])
                            e.wait_ge(k.sem["vec"], k.cnt[f"_scv{sc_i % 2}"])
                        k.dwait(e, esc_h)
                        k.ldh = k.dma(e, "ldw", xqs3[:, sc_i % 2, :, 0:n],
                                      xqT_spill[:, :, left:left + n]
                                      .rearrange("kt p c -> p kt c"))
                        k.ldh2 = k.dma(e, "ldw", esl2[:, sc_i % 2, 0:n],
                                       esc_d[0:1, left:left + n].unsqueeze(1)
                                       .broadcast_to([1, 128, n]))
                    xq_h, esl_h = k.ldh, k.ldh2

                    for hj in range(8):
                        ht = hb * 8 + hj
                        par = hj % 2

                        @b.tensor
                        def _(e, hj=hj, n=n, sc_i=sc_i, w1_h=w1_h, xq_h=xq_h):
                            k.dwait(e, w1_h)
                            k.dwait(e, xq_h)
                            e.wait_ge(k.sem["vec"], k.cnt[f"_pbank{hj % 2}"])
                            for kt in range(8):
                                ins = e.matmul(out=pb[hj % 2][:, 0:n],
                                               lhsT=w1blk[:, kt, hj * 128:(hj + 1) * 128],
                                               rhs=xqs3[:, sc_i % 2, kt, 0:n],
                                               start=(kt == 0), stop=(kt == 7))
                            k.inc(ins, "pe")
                        mm_done = k.v("pe")
                        vec0c = k.v("vec")

                        @b.vector
                        def _(e, ht=ht, hj=hj, n=n, off=off, first=first,
                              last=last, sc_i=sc_i, mm_done=mm_done, esl_h=esl_h):
                            e = k.V(e)
                            e._e.wait_ge(k.sem["pe"], mm_done)
                            k.dwait(e._e, esl_h)
                            e.tensor_tensor(out=HTI[:, 0:n], in0=pb[hj % 2][:, 0:n],
                                            in1=esl2[:, sc_i % 2, 0:n], op=OP.mult)
                            if first:
                                e.memset(T0[:, 0:1], 0.0)
                                e.tensor_scalar(out=T0[:, 1:256], in0=HTI[:, 0:255],
                                                scalar1=cw4[:, ht, 0:1], scalar2=0.0,
                                                op0=OP.mult, op1=OP.bypass)
                                e.tensor_scalar(out=T0[:], in0=T0[:],
                                                scalar1=cw4[:, ht, 3:4], scalar2=0.0,
                                                op0=OP.add, op1=OP.bypass)
                            else:
                                e.tensor_scalar(out=T0[:], in0=HTI[:, off - 1:off + 255],
                                                scalar1=cw4[:, ht, 0:1], scalar2=cw4[:, ht, 3:4],
                                                op0=OP.mult, op1=OP.add)
                            if last:
                                e.memset(T2[:, 255:256], 0.0)
                                e.tensor_scalar(out=T2[:, 0:255], in0=HTI[:, off + 1:off + 256],
                                                scalar1=cw4[:, ht, 2:3], scalar2=0.0,
                                                op0=OP.mult, op1=OP.bypass)
                            else:
                                e.tensor_scalar(out=T2[:], in0=HTI[:, off + 1:off + 257],
                                                scalar1=cw4[:, ht, 2:3], scalar2=0.0,
                                                op0=OP.mult, op1=OP.bypass)
                            e.scalar_tensor_tensor(out=T0[:], in0=HTI[:, off:off + 256],
                                                   scalar=cw4[:, ht, 1:2], in1=T0[:],
                                                   op0=OP.mult, op1=OP.add)
                            e.tensor_tensor(out=T0[:], in0=T0[:], in1=T2[:], op=OP.add)
                            k.cnt["_convV"] = k.v("vec")
                        k.cnt[f"_pbank{hj % 2}"] = vec0c + 1
                        cv_done = k.v("vec")

                        @b.scalar
                        def _(e, cv_done=cv_done, par=par):
                            e = k.S(e)
                            e._e.wait_ge(k.sem["vec"], cv_done)
                            if hs_h[par] is not None:
                                k.dwait(e._e, hs_h[par])
                            e._e.wait_ge(k.sem["vec"],
                                         max(k.cnt[f"_hsV{par}"], k.cnt["_hsVL"]))
                            e.activation(out=hs2[:, par, :], in_=T0[:], func=AF.Silu)
                            e.activation(out=TABS[:], in_=hs2[:, par, :], func=AF.Abs)
                        si_done = k.v("sc")

                        @b.vector
                        def _(e, si_done=si_done, s0=s0, par=par):
                            e = k.V(e)
                            e._e.wait_ge(k.sem["sc"], si_done)
                            e.tensor_tensor(out=RMAX[:, s0:s0 + 256],
                                            in0=RMAX[:, s0:s0 + 256],
                                            in1=TABS[:], op=OP.max)
                            k.cnt[f"_hsV{par}"] = k.v("vec")
                            k.cnt["_hsVL"] = k.v("vec")

                        @b.gpsimd
                        def _(e, ht=ht, s0=s0, si_done=si_done, par=par):
                            e.wait_ge(k.sem["sc"], si_done)
                            k.ldh = k.dma(e, "st", hs_spill[ht, :, s0:s0 + 256],
                                          hs2[:, par, :])
                        hs_h[par] = k.ldh
                    k.cnt[f"_s{sc_i}"] = k.v("pe")
                    k.cnt[f"_scv{sc_i % 2}"] = k.v("vec")
                k.cnt[f"_hb{hb}pe"] = k.v("pe")

            SMAX = sb("smax", [128, 32], F32)
            SMAXT = sb("smaxt", [128, 128], F32)
            p1v = k.v("vec")
            k.cnt["_p2v"] = p1v
            for j in range(32):
                @b.tensor
                def _(e, j=j, p1v=p1v):
                    if j == 0:
                        e.wait_ge(k.sem["vec"], p1v)
                    else:
                        e.wait_ge(k.sem["vec"], k.cnt["_p2v"])
                    ins = e.transpose(out=pb[6 + (j % 2)][:, 0:128],
                                      in_=RMAX[:, j * 128:(j + 1) * 128], identity=k.ident)
                    k.inc(ins, "pe")
                trd = k.v("pe")

                @b.vector
                def _(e, j=j, trd=trd):
                    e = k.V(e)
                    e._e.wait_ge(k.sem["pe"], trd)
                    e.tensor_reduce(out=SMAX[:, j:j + 1], in_=pb[6 + (j % 2)][:, 0:128],
                                    axis=AXX, op=OP.max)
                    k.cnt["_p2v"] = k.v("vec")
            p2v = k.v("vec")

            @b.tensor
            def _(e, p2v=p2v):
                e.wait_ge(k.sem["vec"], p2v)
                ins = e.transpose(out=pb[6][0:32, 0:128], in_=SMAX[:, 0:32], identity=k.ident)
                k.inc(ins, "pe")
            t2d = k.v("pe")

            @b.vector
            def _(e, t2d=t2d):
                e = k.V(e)
                e._e.wait_ge(k.sem["pe"], t2d)
                e.tensor_copy(out=SMAXT[0:32, 0:128], in_=pb[6][0:32, 0:128])
            cpd = k.v("vec")

            @b.gpsimd
            def _(e, cpd=cpd):
                e.wait_ge(k.sem["vec"], cpd)
                k.ldh = k.dma(e, "ldg", RS[0:1, 0:4096].rearrange("o (g p) -> o g p", g=32),
                              SMAXT[0:32, 0:128])
            pk_h = k.ldh

            @b.vector
            def _(e, pk_h=pk_h):
                e = k.V(e)
                k.dwait(e._e, pk_h)
                e.tensor_scalar(out=RS[0:1, 0:4096], in0=RS[0:1, 0:4096], scalar1=1e-5,
                                scalar2=0.0, op0=OP.max, op1=OP.bypass)
            clipv = k.v("vec")

            @b.gpsimd
            def _(e, clipv=clipv):
                e.wait_ge(k.sem["vec"], clipv)
                k.ldh = k.dma(e, "ldg", s2e_d, RS[0:1, 0:4096])
            clip_h = k.ldh

            @b.vector
            def _(e, clip_h=clip_h):
                e = k.V(e)
                k.dwait(e._e, clip_h)
                e.reciprocal(out=RS[0:1, 0:4096], in_=RS[0:1, 0:4096])
                e.tensor_scalar(out=RS[0:1, 0:4096], in0=RS[0:1, 0:4096], scalar1=127.0,
                                scalar2=0.0, op0=OP.mult, op1=OP.bypass)
            s2v = k.v("vec")

            @b.gpsimd
            def _(e, s2v=s2v):
                e.wait_ge(k.sem["vec"], s2v)
                k.ldh = k.dma(e, "ldg", s2_d, RS[0:1, 0:4096])
                k.dwait(e, k.ldh)
                k.ldh2 = k.dma(e, "ldg", RS[0:1, 0:4096], s2e_d)
            s2_h, eld_h = k.ldh, k.ldh2

            @b.vector
            def _(e, s2_h=s2_h, eld_h=eld_h):
                e = k.V(e)
                k.dwait(e._e, s2_h)
                k.dwait(e._e, eld_h)
                e.tensor_scalar(out=RS[0:1, 0:4096], in0=RS[0:1, 0:4096],
                                scalar1=1.0 / 127.0, scalar2=0.0, op0=OP.mult, op1=OP.bypass)
                e.tensor_scalar(out=RS[0:1, 0:4096], in0=RS[0:1, 0:4096],
                                scalar1=k.SWP[0:1, 3:4], scalar2=0.0, op0=OP.mult, op1=OP.bypass)
            ev2 = k.v("vec")

            @b.gpsimd
            def _(e, ev2=ev2):
                e.wait_ge(k.sem["vec"], ev2)
                k.ldh = k.dma(e, "ldg", s2e_d, RS[0:1, 0:4096])
            s2e_h = k.ldh

        fence = _snap(k)
        W2R = ps.enter_context(nc.sbuf_tensor("w2r" + tag, [128, HT * 1024], BF16))
        HSL = ps.enter_context(nc.sbuf_tensor("hsl" + tag, [128, 2 * 256], F32))
        HQ = ps.enter_context(nc.sbuf_tensor("hq" + tag, [128, 2 * 256], BF16))
        S2L = ps.enter_context(nc.sbuf_tensor("s2l" + tag, [128, 2 * 256], F32))
        S2E = ps.enter_context(nc.sbuf_tensor("s2e" + tag, [128, 2 * 256], F32))
        YTMP = ps.enter_context(nc.sbuf_tensor("ytmp" + tag, [128, 256], F32))
        w2r3 = W2R[:].rearrange("p (ht c) -> p ht c", ht=HT)
        hsl2 = HSL[:].rearrange("p (u c) -> p u c", u=2)
        hq2 = HQ[:].rearrange("p (u c) -> p u c", u=2)
        s2l2 = S2L[:].rearrange("p (u c) -> p u c", u=2)
        s2e2 = S2E[:].rearrange("p (u c) -> p u c", u=2)

        @b.sync
        def _(e):
            _bar(k, e, fence)
            k.ldh = k.dma(e, "ldw", w2r3, w2q_c.rearrange("kt p c -> p kt c"))
        w2_h = k.ldh

        k.cnt["_p3res0"] = k.v("vec")
        k.cnt["_p3res1"] = k.v("vec")
        k.cnt["_p3ev"] = k.v("vec")
        k.cnt["_p3q0"] = k.v("vec")
        k.cnt["_p3q1"] = k.v("vec")
        hl_h = {0: None, 1: None}
        sl_h = {0: None, 1: None}

        for sc_i in range(NSCH):
            s0 = sc_i * SCH
            par = sc_i % 2

            @b.sync
            def _(e, s0=s0, par=par, sc_i=sc_i, s2_h=s2_h, s2e_h=s2e_h):
                if sc_i >= 2:
                    e.wait_ge(k.sem["vec"], k.cnt[f"_p3res{par}"])
                k.dwait(e, s2_h)
                k.dwait(e, s2e_h)
                k.ldh = k.dma(e, "ldw", s2l2[:, par, :],
                              s2_d[0:1, s0:s0 + 256].unsqueeze(1).broadcast_to([1, 128, 256]))
                k.ldh2 = k.dma(e, "ldw", s2e2[:, par, :],
                               s2e_d[0:1, s0:s0 + 256].unsqueeze(1).broadcast_to([1, 128, 256]))
            sl_h[par] = (k.ldh, k.ldh2)

            for ht in range(HT):
                hpar = ht % 2

                @b.sync
                def _(e, ht=ht, s0=s0, hpar=hpar, sc_i=sc_i):
                    e.wait_ge(k.sem["vec"], k.cnt[f"_p3q{hpar}"])
                    k.ldh = k.dma(e, "ldw", hsl2[:, hpar, :], hs_spill[ht, :, s0:s0 + 256])
                h_h = k.ldh

                @b.vector
                def _(e, h_h=h_h, hpar=hpar, par=par, ht=ht, sc_i=sc_i):
                    e = k.V(e)
                    k.dwait(e._e, h_h)
                    for hh in sl_h[par]:
                        k.dwait(e._e, hh)
                    if ht >= 2:
                        e._e.wait_ge(k.sem["pe"], k.cnt[f"_p3mm{hpar}"])
                    e.tensor_tensor(out=hsl2[:, hpar, :], in0=hsl2[:, hpar, :],
                                    in1=s2l2[:, par, :], op=OP.mult)
                    e.tensor_scalar(out=hsl2[:, hpar, :], in0=hsl2[:, hpar, :],
                                    scalar1=MAGIC, scalar2=MAGIC, op0=OP.add, op1=OP.subtract)
                    e.tensor_scalar(out=hsl2[:, hpar, :], in0=hsl2[:, hpar, :],
                                    scalar1=-128.0, scalar2=127.0, op0=OP.max, op1=OP.min)
                    e.tensor_copy(out=hq2[:, hpar, :], in_=hsl2[:, hpar, :])
                    k.cnt[f"_p3q{hpar}"] = k.v("vec")
                q_rdy = k.v("vec")

                @b.tensor
                def _(e, ht=ht, hpar=hpar, q_rdy=q_rdy, w2_h=w2_h):
                    k.dwait(e, w2_h)
                    e.wait_ge(k.sem["vec"], q_rdy)
                    if ht == 0:
                        e.wait_ge(k.sem["vec"], k.cnt["_p3ev"])
                    for dc in range(8):
                        ins = e.matmul(out=pb[dc][:, 0:256],
                                       lhsT=w2r3[:, ht, dc * 128:(dc + 1) * 128],
                                       rhs=hq2[:, hpar, :],
                                       start=(ht == 0), stop=(ht == HT - 1))
                    k.inc(ins, "pe")
                    k.cnt[f"_p3mm{hpar}"] = k.v("pe")
            mm2_done = k.v("pe")

            for dc in range(8):
                @b.vector
                def _(e, dc=dc, par=par, mm2_done=mm2_done):
                    e = k.V(e)
                    e._e.wait_ge(k.sem["pe"], mm2_done)
                    e.tensor_tensor(out=YTMP[:], in0=pb[dc][:, 0:256],
                                    in1=s2e2[:, par, :], op=OP.mult)
                yv = k.v("vec")
                g0 = s0 // 128

                @b.tensor
                def _(e, yv=yv, dc=dc):
                    e.wait_ge(k.sem["vec"], yv)
                    for j in range(2):
                        ins = e.transpose(out=pb[dc][:, j * 128:(j + 1) * 128],
                                          in_=YTMP[:, j * 128:(j + 1) * 128],
                                          identity=k.ident)
                    k.inc(ins, "pe")
                tv = k.v("pe")

                @b.vector
                def _(e, tv=tv, dc=dc, g0=g0):
                    e = k.V(e)
                    e._e.wait_ge(k.sem["pe"], tv)
                    for j in range(2):
                        e.scalar_tensor_tensor(
                            out=k.x3[:, g0 + j, dc * 128:(dc + 1) * 128],
                            in0=pb[dc][:, j * 128:(j + 1) * 128], scalar=0.5,
                            in1=k.x3[:, g0 + j, dc * 128:(dc + 1) * 128],
                            op0=OP.mult, op1=OP.add)
                k.cnt["_p3ev"] = k.v("vec")
            k.cnt[f"_p3res{par}"] = k.v("vec")


def phase_l0ffn(k, call):
    nc, b, t = k.nc, k.block, k.t
    pb = k.pb
    H = 16
    if "lnl0_spill" not in k.dram:
        k.dram["lnl0_spill"] = nc.dram_tensor("lnl0_spill", [2, 16, 128, D], BF16)
        k.dram["vtok_spill"] = nc.dram_tensor("vtok_spill", [16, 16, 128, 64], BF16)
    lnl0 = k.dram["lnl0_spill"].ap()
    vtok = k.dram["vtok_spill"].ap()
    gag_i = k.dram["g_ag_in"].ap()
    gag_o = k.dram["g_ag_out"].ap()
    aar_i = k.dram["att_ar_in"].ap()
    aar_o = k.dram["att_ar_out"].ap()

    with ExitStack() as ps:
        sb = lambda nm, sh, dt: ps.enter_context(
            nc.sbuf_tensor(f"{nm}g{call}", sh, dt))
        MKB = sb("mkb", [128, 2 * 16 * 16], F32)
        ATT = sb("att", [128, 256], F32)
        PS = sb("ps", [128, 256], F32)
        OWNC = sb("ownc", [128, 2], F32)
        WGB = sb("wgb", [128, 8 * 16], F32)
        WOB = sb("wob", [128, 8 * 1024], BF16)
        LNGB = sb("lngb", [128, D], F32)
        LNBB = sb("lnbb", [128, D], F32)
        BOB = sb("bob", [128, D], F32)
        IOT = sb("iot", [128, 2 * 128], F32)
        mk3 = MKB[:].rearrange("p (s g h) -> p s g h", s=2, g=16)
        att3 = ATT[:].rearrange("p (g h) -> p g h", g=16)
        ps3 = PS[:].rearrange("p (g h) -> p g h", g=16)
        wg3 = WGB[:].rearrange("p (dc h) -> p dc h", dc=8)
        wo3 = WOB[:].rearrange("p (kt d) -> p kt d", kt=8)

        _fence(k)

        @b.sync
        def _(e):
            k.dma(e, "ldw", WGB[:].rearrange("p (dc h) -> p dc h", dc=8),
                  t["Wg"].rearrange("dc p h -> p dc h"))
            k.dma(e, "ldw", LNGB[:], t["lng_rep"])
            k.dma(e, "ldw", LNBB[:], t["lnb_rep"])
            k.dma(e, "ldw", BOB[:], t["bo_rep"])
            k.ldh = k.dma(e, "ldw", IOT[:].rearrange("p (s x) -> p s x", s=2),
                          t["iotas"].rearrange("s p c h -> p s (c h)"))
        iot_h = k.ldh

        @b.gpsimd
        def _(e):
            k.dma(e, "ldg", OWNC[:, 0:1],
                  t["own0"].unsqueeze(1).broadcast_to([1, 128, 1]))
            k.ldh = k.dma(e, "ldg", OWNC[:, 1:2],
                          t["own1"].unsqueeze(1).broadcast_to([1, 128, 1]))
            k.ldh2 = k.dma(e, "ldg", WOB[:].rearrange("p (kt d) -> p kt d", kt=8),
                           t["Wo"].rearrange("(kt p) d -> p kt d", p=128))
        own_h, wo_h = k.ldh, k.ldh2

        sc_rdy = ln_stats_all(k)

        with ExitStack() as s1:
            sb1 = lambda nm, sh, dt: s1.enter_context(
                nc.sbuf_tensor(f"{nm}g{call}", sh, dt))
            LNA = sb1("lna", [128, D], F32)
            XLT = sb1("xlt", [128, 8 * 128], F32)
            GSB = sb1("gsb", [16, 128], F32)
            _fence(k)
            gv_rd = 0
            for g in range(NTILE):
                s2, ti = g // 16, g % 16
                lnw = k.v("pe")

                @b.vector
                def _(e, g=g, lnw=lnw, sc_rdy=sc_rdy):
                    e = k.V(e)
                    e._e.wait_ge(k.sem["sc"], sc_rdy)
                    e._e.wait_ge(k.sem["pe"], lnw)
                    k.dwait_all(e._e, "st")
                    e.tensor_scalar(out=LNA[:], in0=k.x3[:, g, :],
                                    scalar1=k.st3[:, g, 0:1], scalar2=k.st3[:, g, 1:2],
                                    op0=OP.subtract, op1=OP.mult)
                cst = k.v("vec")
                tr_chain(k, [(LNA[:, dc * 128:(dc + 1) * 128],
                              XLT[:, dc * 128:(dc + 1) * 128])
                             for dc in range(8)], wait_before=[("vec", cst)])
                tr_v = k.v("vec")

                @b.gpsimd
                def _(e, s2=s2, ti=ti, cst=cst):
                    e.wait_ge(k.sem["vec"], cst)
                    k.dma(e, "st", lnl0[s2, ti], LNA[:])

                @b.tensor
                def _(e, tr_v=tr_v, gv_rd=gv_rd):
                    e.wait_ge(k.sem["vec"], tr_v)
                    e.wait_ge(k.sem["vec"], gv_rd)
                    for dc in range(8):
                        ins = e.matmul(out=pb[2][0:16, 0:128],
                                       lhsT=wg3[:, dc, :],
                                       rhs=XLT[:, dc * 128:(dc + 1) * 128],
                                       start=(dc == 0), stop=(dc == 7))
                    k.inc(ins, "pe")
                gmm = k.v("pe")

                @b.vector
                def _(e, gmm=gmm):
                    e = k.V(e)
                    e._e.wait_ge(k.sem["pe"], gmm)
                    k.dwait_all(e._e, "sth")
                    e.tensor_copy(out=GSB[0:16, :], in_=pb[2][0:16, 0:128])
                gv_rd = k.v("vec")

                @b.sync
                def _(e, s2=s2, ti=ti, gv_rd=gv_rd):
                    e.wait_ge(k.sem["vec"], gv_rd)
                    k.dma(e, "sth", gag_i[:, s2 * 2048 + ti * 128:
                                          s2 * 2048 + (ti + 1) * 128], GSB[0:16, :])

        @b.gpsimd
        def _(e):
            k.dwait_all(e, "sth")
            k.dwait_all(e, "st")
            ins = e.collective_compute(
                "AllGather", OP.bypass, replica_groups=[list(range(8))],
                ins=[gag_i.opt()], outs=[gag_o.opt()])
            k.inc(ins, "cc")
        ag_done = k.v("cc")

        with ExitStack() as s2s:
            sb2 = lambda nm, sh, dt: s2s.enter_context(
                nc.sbuf_tensor(f"{nm}g{call}", sh, dt))
            GF = sb2("gf", [128, 128], F32)
            SCR = sb2("scr", [128, 2 * 128], F32)
            ESC = sb2("esc", [128, 128], F32)
            MC = sb2("mc", [128, 8], F32)
            NM = sb2("nm", [128, 8], F32)
            SC = sb2("scs", [128, 8], F32)
            KC = sb2("kc", [128, 8], F32)
            MH = sb2("mh", [128, 2 * 16], F32)
            PMT = sb2("pmt", [128, 16], F32)
            EQ = sb2("eq", [128, 128], F32)
            CM = sb2("cm", [128, 2 * 16], F32)
            _fence(k)
            gf_h = None
            for ti in range(16):
                for s2 in range(2):
                    vnow = k.v("vec")

                    @b.sync
                    def _(e, s2=s2, ti=ti, vnow=vnow, ag_done=ag_done):
                        k.wait(e, "cc", ag_done)
                        e.wait_ge(k.sem["vec"], vnow)
                        k.ldh = k.dma(
                            e, "ldw", GF[:],
                            gag_o[:, s2 * 2048 + ti * 128:s2 * 2048 + (ti + 1) * 128])
                    gf_h = k.ldh
                    tr_chain(k, [(GF[:], SCR[:, s2 * 128:(s2 + 1) * 128])],
                             wait_before=[(gf_h[0], gf_h[1]), ("sc", k.v("sc"))])
                    scr3 = SCR[:, s2 * 128:(s2 + 1) * 128].rearrange(
                        "p (c h) -> p c h", c=8)

                    @b.vector
                    def _(e, scr3=scr3):
                        e = k.V(e)
                        for c in range(8):
                            e.tensor_reduce(out=MC[:, c:c + 1], in_=scr3[:, c, :],
                                            axis=AXX, op=OP.max)
                        e.tensor_scalar(out=NM[:], in0=MC[:], scalar1=-1.0,
                                        scalar2=0.0, op0=OP.mult, op1=OP.bypass)
                    nm_v = k.v("vec")

                    @b.scalar
                    def _(e, scr3=scr3, nm_v=nm_v):
                        e = k.S(e)
                        e._e.wait_ge(k.sem["vec"], nm_v)
                        esc3 = ESC[:].rearrange("p (c h) -> p c h", c=8)
                        for c in range(8):
                            e.activation(out=esc3[:, c, :], in_=scr3[:, c, :],
                                         func=AF.Exp, bias=NM[:, c:c + 1])
                    ex_v = k.v("sc")

                    @b.vector
                    def _(e, ex_v=ex_v):
                        e = k.V(e)
                        e._e.wait_ge(k.sem["sc"], ex_v)
                        esc3 = ESC[:].rearrange("p (c h) -> p c h", c=8)
                        for c in range(8):
                            e.tensor_reduce(out=SC[:, c:c + 1], in_=esc3[:, c, :],
                                            axis=AXX, op=OP.add)
                    sc_v = k.v("vec")

                    @b.scalar
                    def _(e, sc_v=sc_v):
                        e = k.S(e)
                        e._e.wait_ge(k.sem["vec"], sc_v)
                        e.activation(out=SC[:], in_=SC[:], func=AF.Ln)
                    ln_v = k.v("sc")

                    @b.vector
                    def _(e, s2=s2, scr3=scr3, ln_v=ln_v):
                        e = k.V(e)
                        e._e.wait_ge(k.sem["sc"], ln_v)
                        e.tensor_tensor(out=KC[:], in0=MC[:], in1=SC[:], op=OP.add)
                        e.tensor_tensor(out=scr3, in0=scr3,
                                        in1=KC[:].unsqueeze(2).broadcast_to([128, 8, 16]),
                                        op=OP.subtract)
                        for h in range(16):
                            e.tensor_reduce(out=MH[:, s2 * 16 + h:s2 * 16 + h + 1],
                                            in_=scr3[:, :, h], axis=AXX, op=OP.max)

                @b.vector
                def _(e, ti=ti, iot_h=iot_h, own_h=own_h):
                    e = k.V(e)
                    k.dwait(e._e, iot_h)
                    k.dwait(e._e, own_h)
                    e.tensor_tensor(out=PMT[:], in0=MH[:, 0:16], in1=MH[:, 16:32],
                                    op=OP.max)
                    for s2 in range(2):
                        scr3 = SCR[:, s2 * 128:(s2 + 1) * 128].rearrange(
                            "p (c h) -> p c h", c=8)
                        eq3 = EQ[:].rearrange("p (c h) -> p c h", c=8)
                        e.tensor_tensor(
                            out=eq3, in0=scr3,
                            in1=PMT[:].unsqueeze(1).broadcast_to([128, 8, 16]),
                            op=OP.is_equal)
                        e.tensor_tensor(out=EQ[:], in0=EQ[:],
                                        in1=IOT[:, s2 * 128:(s2 + 1) * 128],
                                        op=OP.mult)
                        for h in range(16):
                            e.tensor_reduce(
                                out=CM[:, s2 * 16 + h:s2 * 16 + h + 1],
                                in_=eq3[:, :, h], axis=AXX, op=OP.min)
                    e.tensor_tensor(out=PMT[:], in0=CM[:, 0:16], in1=CM[:, 16:32],
                                    op=OP.min)
                    e.tensor_scalar(out=PMT[:], in0=PMT[:], scalar1=16.0,
                                    scalar2=0.0, op0=OP.add, op1=OP.bypass)
                    for s2 in range(2):
                        e.tensor_scalar(out=mk3[:, s2, ti, :], in0=PMT[:],
                                        scalar1=OWNC[:, s2:s2 + 1], scalar2=0.0,
                                        op0=OP.is_equal, op1=OP.bypass)

        if "wqb_cache" not in k.dram:
            k.dram["wqb_cache"] = nc.dram_tensor("wqb_cache", [16, 128, 8, 64], BF16)
            k.dram["wkvb_cache"] = nc.dram_tensor("wkvb_cache", [16, 128, 8, 128], BF16)

            @b.gpsimd
            def _(e):
                for h in range(16):
                    k.dma(e, "st", k.dram["wqb_cache"].ap()[h],
                          t["Wq"][h].rearrange("dc p e -> p dc e"))
                    k.dma(e, "st", k.dram["wkvb_cache"].ap()[h],
                          t["Wkv"][h].rearrange("dc p e -> p dc e"))
        wqc = k.dram["wqb_cache"].ap()
        wkvc = k.dram["wkvb_cache"].ap()

        with ExitStack() as s3a:
            sb3 = lambda nm, sh, dt: s3a.enter_context(
                nc.sbuf_tensor(f"{nm}g{call}", sh, dt))
            LNSLAB = sb3("lnsl", [128, 2 * 4 * 8 * 128], BF16)
            HGT = sb3("hgt", [128, 8 * 512], BF16)
            DDB = sb3("ddb", [128, 8 * 128], BF16)
            QTE = sb3("qte", [64, 512], BF16)
            KVTE = sb3("kvte", [128, 512], BF16)
            QTOK = sb3("qtok", [128, 4 * 64], BF16)
            TTRS = sb3("ttrs", [128, 64], F32)
            KTOK = sb3("ktok", [128, 4 * 64], BF16)
            VTOK = sb3("vtokb", [128, 4 * 64], BF16)
            WQB = sb3("wqb", [128, 2 * 8 * 64], BF16)
            WKVB = sb3("wkvb", [128, 2 * 8 * 128], BF16)
            BQC = sb3("bqc", [64, 16], F32)
            BKVC = sb3("bkvc", [128, 16], F32)
            MSK = sb3("msk", [128, 16], F32)
            ln5 = LNSLAB[:].rearrange("p (s g dc c) -> p s g dc c", s=2, g=4, dc=8)
            hgt3 = HGT[:].rearrange("p (dc c) -> p dc c", dc=8)
            dd3 = DDB[:].rearrange("p (x c) -> p x c", x=8)
            wq4 = WQB[:].rearrange("p (u dc e) -> p u dc e", u=2, dc=8)
            wkv4 = WKVB[:].rearrange("p (u dc e) -> p u dc e", u=2, dc=8)
            _fence(k)
            sc_bank = {0: 0, 1: 0, 2: 0, 3: 0}
            hgt_rd = 0
            dd_rd = 0
            qte_rd = 0
            tb_rd = {0: 0, 1: 0}
            wqb_rd = {0: 0, 1: 0}
            wq_h = {0: None, 1: None}
            vt_st = None

            @b.sync
            def _(e):
                k.dwait_all(e, "st")
                k.ldh = k.dma(e, "ldw", BQC[0:64, :].unsqueeze(2),
                              t["bqT"].rearrange("h p o -> p h o"),
                              allow_slow_non_contiguous=True)
                k.ldh2 = k.dma(e, "ldw", BKVC[:, :].unsqueeze(2),
                               t["bkvT"].rearrange("h p o -> p h o"),
                               allow_slow_non_contiguous=True)
                k.ldh3 = k.dma(e, "ldw", wq4[:, 0], wqc[0])
                k.ldh4 = k.dma(e, "ldw", wkv4[:, 0], wkvc[0])
            bq_h = (k.ldh, k.ldh2)
            wq_h[0] = (k.ldh3, k.ldh4)

            for rc in range(4):
                penow = k.v("pe")

                @b.sync
                def _(e, rc=rc, penow=penow):
                    e.wait_ge(k.sem["pe"], penow)
                    k.lhs = {}
                    for s2 in range(2):
                        for ti2 in range(4):
                            k.lhs[(s2, ti2)] = k.dma(
                                e, "ldw", ln5[:, s2, ti2],
                                lnl0[s2, rc * 4 + ti2]
                                .rearrange("p (dc c) -> p dc c", c=128))
                ls_hs = k.lhs

                for hh in range(16):
                    if not (rc == 3 and hh == 15):
                        nh = (hh + 1) % 16

                        @b.sync
                        def _(e, nh=nh, wrd=wqb_rd[(hh + 1) % 2]):
                            e.wait_ge(k.sem["pe"], wrd)
                            k.ldh = k.dma(e, "ldw", wq4[:, nh % 2], wqc[nh])
                            k.ldh2 = k.dma(e, "ldw", wkv4[:, nh % 2], wkvc[nh])
                        wq_h[(hh + 1) % 2] = (k.ldh, k.ldh2)

                    @b.gpsimd
                    def _(e, rc=rc, hh=hh, dd_rd=dd_rd):
                        e.wait_ge(k.sem["pe"], dd_rd)
                        g = k.G(e)
                        for ti2 in range(4):
                            for s2 in range(2):
                                g.tensor_scalar(
                                    out=dd3[:, ti2 * 2 + s2, :], in0=k.identb,
                                    scalar1=mk3[:, s2, rc * 4 + ti2, hh:hh + 1],
                                    scalar2=0.0, op0=OP.mult, op1=OP.bypass)
                    dd_v = k.v("gp")

                    pe0g = k.v("pe")
                    sc0g = k.v("sc")

                    @b.tensor
                    def _(e, hh=hh, dd_v=dd_v, ls_hs=ls_hs, wqh=wq_h[hh % 2],
                          sb0=sc_bank[0], sb1=sc_bank[1],
                          sb2w=sc_bank[2], sb3w=sc_bank[3], sc0g=sc0g):
                        e.wait_ge(k.sem["gp"], dd_v)
                        for hd in ls_hs.values():
                            k.dwait(e, hd)
                        for dc in range(8):
                            if dc == 0:
                                e.wait_ge(k.sem["sc"], sb0)
                            elif dc == 1:
                                e.wait_ge(k.sem["sc"], sb1)
                            else:
                                e.wait_ge(k.sem["sc"], sc0g + dc - 1)
                            first = True
                            for ti2 in range(4):
                                for s2 in range(2):
                                    ins = e.matmul(
                                        out=pb[dc % 2][:, ti2 * 128:(ti2 + 1) * 128],
                                        lhsT=ln5[:, s2, ti2, dc, :],
                                        rhs=dd3[:, ti2 * 2 + s2, :],
                                        start=first,
                                        stop=(ti2 == 3 and s2 == 1))
                                    first = False
                            k.inc(ins, "pe")
                        k.dwait(e, wqh[0])
                        k.dwait(e, wqh[1])
                        e.wait_ge(k.sem["sc"], sc0g + 8)
                        e.wait_ge(k.sem["sc"], sb2w)
                        for dc in range(8):
                            ins = e.matmul(out=pb[2][0:64, 0:512],
                                           lhsT=wq4[:, hh % 2, dc, :],
                                           rhs=hgt3[:, dc, :],
                                           start=(dc == 0), stop=(dc == 7))
                        k.inc(ins, "pe")
                        e.wait_ge(k.sem["sc"], sb3w)
                        for dc in range(8):
                            ins = e.matmul(out=pb[3][0:128, 0:512],
                                           lhsT=wkv4[:, hh % 2, dc, :],
                                           rhs=hgt3[:, dc, :],
                                           start=(dc == 0), stop=(dc == 7))
                        k.inc(ins, "pe")

                    @b.scalar
                    def _(e, hh=hh, pe0g=pe0g, hgt_rd=hgt_rd, qte_rd=qte_rd,
                          bq_h=bq_h):
                        e = k.S(e)
                        e._e.wait_ge(k.sem["pe"], hgt_rd)
                        for dc in range(8):
                            e._e.wait_ge(k.sem["pe"], pe0g + dc + 1)
                            e.activation(out=hgt3[:, dc, :],
                                         in_=pb[dc % 2][:, 0:512], func=AF.Copy)
                        e._e.wait_ge(k.sem["pe"], qte_rd)
                        k.dwait(e._e, bq_h[0])
                        k.dwait(e._e, bq_h[1])
                        e._e.wait_ge(k.sem["pe"], pe0g + 9)
                        e.activation(out=QTE[0:64, :], in_=pb[2][0:64, 0:512],
                                     func=AF.Identity, bias=BQC[0:64, hh:hh + 1])
                        e._e.wait_ge(k.sem["pe"], pe0g + 10)
                        e.activation(out=KVTE[:], in_=pb[3][:, 0:512],
                                     func=AF.Identity, bias=BKVC[:, hh:hh + 1])
                    assert k.v("pe") == pe0g + 10
                    assert k.v("sc") == sc0g + 10
                    dd_rd = pe0g + 8
                    hgt_rd = pe0g + 10
                    wqb_rd[hh % 2] = pe0g + 10
                    sc_bank[0] = sc0g + 7
                    sc_bank[1] = sc0g + 8
                    sc_bank[2] = sc0g + 9
                    sc_bank[3] = sc0g + 10
                    qkv_sc = k.v("sc")

                    pe0 = k.v("pe")
                    vec0v = k.v("vec")

                    @b.tensor
                    def _(e, qkv_sc=qkv_sc, vec0v=vec0v,
                          tb0=tb_rd[0], tb1=tb_rd[1]):
                        e.wait_ge(k.sem["sc"], qkv_sc)
                        for i in range(12):
                            src, ti2 = i % 3, i // 3
                            if i == 0:
                                e.wait_ge(k.sem["vec"], tb0)
                            elif i == 1:
                                e.wait_ge(k.sem["vec"], tb1)
                            else:
                                e.wait_ge(k.sem["vec"], vec0v + i - 1)
                            if src == 0:
                                in_ = QTE[0:64, ti2 * 128:(ti2 + 1) * 128]
                                idn = k.identb[0:64, 0:64]
                            elif src == 1:
                                in_ = KVTE[0:64, ti2 * 128:(ti2 + 1) * 128]
                                idn = k.identb[0:64, 0:64]
                            else:
                                in_ = KVTE[64:128, ti2 * 128:(ti2 + 1) * 128]
                                idn = k.identb[64:128, 64:128]
                            ins = e.transpose(
                                out=pb[4 + i % 2][:, 0:32].bitcast(BF16),
                                in_=in_, identity=idn)
                            k.inc(ins, "pe")
                    qte_rd = k.v("pe")

                    @b.vector
                    def _(e, hh=hh, rc=rc, pe0=pe0, vt_st=vt_st):
                        e = k.V(e)
                        if vt_st is not None:
                            k.dwait(e._e, vt_st)
                        for i in range(12):
                            src, ti2 = i % 3, i // 3
                            e._e.wait_ge(k.sem["pe"], pe0 + i + 1)
                            dst = (QTOK, KTOK, VTOK)[src]
                            e.tensor_copy(
                                out=dst[:, ti2 * 64:(ti2 + 1) * 64],
                                in_=pb[4 + i % 2][:, 0:32].bitcast(BF16))
                        for ti2 in range(4):
                            e.tensor_tensor(
                                out=TTRS[:],
                                in0=QTOK[:, ti2 * 64:(ti2 + 1) * 64],
                                in1=KTOK[:, ti2 * 64:(ti2 + 1) * 64], op=OP.mult)
                            e.tensor_reduce(
                                out=att3[:, rc * 4 + ti2, hh:hh + 1],
                                in_=TTRS[:], axis=AXX, op=OP.add)
                    tb_rd[0] = k.v("vec")
                    tb_rd[1] = k.v("vec")
                    tok_v = k.v("vec")

                    @b.gpsimd
                    def _(e, hh=hh, rc=rc, tok_v=tok_v):
                        e.wait_ge(k.sem["vec"], tok_v)
                        k.ldh = k.dma(e, "st",
                                      vtok[hh, rc * 4:(rc + 1) * 4]
                                      .rearrange("g p e -> p g e"),
                                      VTOK[:].rearrange("p (g e) -> p g e", g=4))
                    vt_st = k.ldh

            @b.vector
            def _(e):
                e = k.V(e)
                for ti in range(16):
                    e.tensor_tensor(out=MSK[:], in0=mk3[:, 0, ti, :],
                                    in1=mk3[:, 1, ti, :], op=OP.add)
                    e.scalar_tensor_tensor(out=att3[:, ti, :], in0=att3[:, ti, :],
                                           scalar=0.125, in1=MSK[:],
                                           op0=OP.mult, op1=OP.mult)
            attm_v = k.v("vec")

            @b.sync
            def _(e, attm_v=attm_v):
                e.wait_ge(k.sem["vec"], attm_v)
                k.ldh = k.dma(e, "sth", aar_i, ATT[:])
            ar_st = k.ldh

            @b.gpsimd
            def _(e, ar_st=ar_st):
                k.dwait(e, ar_st)
                ins = e.collective_compute(
                    "AllReduce", OP.add, replica_groups=[list(range(8))],
                    ins=[aar_i.opt()], outs=[aar_o.opt()])
                k.inc(ins, "cc")
            ar_done = k.v("cc")

            @b.sync
            def _(e, ar_done=ar_done):
                k.wait(e, "cc", ar_done)
                k.ldh = k.dma(e, "ldw", ATT[:], aar_o)
            att_h = k.ldh

        with ExitStack() as s3b:
            sb4 = lambda nm, sh, dt: s3b.enter_context(
                nc.sbuf_tensor(f"{nm}g{call}", sh, dt))
            MXC = sb4("mxc", [128, 16], F32)
            SMC = sb4("smc", [128, 16], F32)
            MREP = sb4("mrep", [128, 16], F32)
            LREP = sb4("lrep", [128, 16], F32)
            MCOL = sb4("mcol", [16, 1], F32)
            MROW = sb4("mrow", [1, 16], F32)
            LROW = sb4("lrow", [1, 16], F32)
            AVST0 = sb4("avst0", [128, 8 * 128], BF16)
            AVST1 = sb4("avst1", [128, 8 * 128], BF16)
            VTR = sb4("vtr", [128, 16 * 64], BF16)
            PMS = sb4("pms", [128, 32], F32)
            AVH = sb4("avh", [128, 2 * 64], BF16)
            XLN = sb4("xln", [128, D], F32)
            YB = sb4("yb", [128, D], F32)
            NSC = sb4("nsc", [128, 2], F32)
            YST = sb4("yst", [128, 2], F32)
            vtr3 = VTR[:].rearrange("p (h e) -> p h e", h=16)
            _fence(k)

            @b.vector
            def _(e, att_h=att_h):
                e = k.V(e)
                k.dwait(e._e, att_h)
                for h in range(16):
                    e.tensor_reduce(out=MXC[:, h:h + 1], in_=att3[:, :, h],
                                    axis=AXX, op=OP.max)
            mx_v = k.v("vec")

            @b.tensor
            def _(e, mx_v=mx_v):
                e.wait_ge(k.sem["vec"], mx_v)
                ins = e.transpose(out=pb[6][0:16, 0:128], in_=MXC[:],
                                  identity=k.ident)
                k.inc(ins, "pe")
            t1_v = k.v("pe")

            @b.vector
            def _(e, t1_v=t1_v):
                e = k.V(e)
                e._e.wait_ge(k.sem["pe"], t1_v)
                e.tensor_reduce(out=MCOL[0:16, :], in_=pb[6][0:16, 0:128],
                                axis=AXX, op=OP.max)
            mc_v = k.v("vec")

            @b.tensor
            def _(e, mc_v=mc_v):
                e.wait_ge(k.sem["vec"], mc_v)
                ins = e.transpose(out=pb[7][0:1, 0:16], in_=MCOL[0:16, :],
                                  identity=k.ident[0:16, 0:16])
                k.inc(ins, "pe")
            t2_v = k.v("pe")

            @b.vector
            def _(e, t2_v=t2_v):
                e = k.V(e)
                e._e.wait_ge(k.sem["pe"], t2_v)
                e.tensor_copy(out=MROW[0:1, :], in_=pb[7][0:1, 0:16])
            mr_v = k.v("vec")

            @b.gpsimd
            def _(e, mr_v=mr_v):
                e.wait_ge(k.sem["vec"], mr_v)
                k.ldh = k.dma(e, "ldg", MREP[:],
                              MROW[0:1, :].unsqueeze(1).broadcast_to([1, 128, 16]))
            mrep_h = k.ldh

            @b.vector
            def _(e, mrep_h=mrep_h):
                e = k.V(e)
                k.dwait(e._e, mrep_h)
                for ti in range(16):
                    e.tensor_tensor(out=ps3[:, ti, :], in0=att3[:, ti, :],
                                    in1=MREP[:], op=OP.subtract)
            sub_v = k.v("vec")

            @b.scalar
            def _(e, sub_v=sub_v):
                e = k.S(e)
                e._e.wait_ge(k.sem["vec"], sub_v)
                for ti in range(16):
                    e.activation(out=PS[:, ti * 16:(ti + 1) * 16],
                                 in_=PS[:, ti * 16:(ti + 1) * 16], func=AF.Exp)
            exp_v = k.v("sc")

            @b.vector
            def _(e, exp_v=exp_v):
                e = k.V(e)
                e._e.wait_ge(k.sem["sc"], exp_v)
                for h in range(16):
                    e.tensor_reduce(out=SMC[:, h:h + 1], in_=ps3[:, :, h],
                                    axis=AXX, op=OP.add)
            smc_v = k.v("vec")

            @b.tensor
            def _(e, smc_v=smc_v):
                e.wait_ge(k.sem["vec"], smc_v)
                ins = e.matmul(out=pb[2][0:1, 0:16], lhsT=k.ones, rhs=SMC[:],
                               start=True, stop=True)
                k.inc(ins, "pe")
            lsum_v = k.v("pe")

            @b.vector
            def _(e, lsum_v=lsum_v):
                e = k.V(e)
                e._e.wait_ge(k.sem["pe"], lsum_v)
                e.reciprocal(out=LROW[0:1, :], in_=pb[2][0:1, 0:16])
            lr_v = k.v("vec")

            @b.gpsimd
            def _(e, lr_v=lr_v):
                e.wait_ge(k.sem["vec"], lr_v)
                k.ldh = k.dma(e, "ldg", LREP[:],
                              LROW[0:1, :].unsqueeze(1).broadcast_to([1, 128, 16]))
            lrep_h = k.ldh

            @b.vector
            def _(e, lrep_h=lrep_h):
                e = k.V(e)
                k.dwait(e._e, lrep_h)
                for ti in range(16):
                    e.tensor_tensor(out=ps3[:, ti, :], in0=ps3[:, ti, :],
                                    in1=LREP[:], op=OP.mult)
            p_done = k.v("vec")

            a8_rd = {0: 0, 1: 0, 2: 0, 3: 0}
            avh_rd = 0
            avst_rd = 0
            tb2 = {0: 0, 1: 0}
            vtr_rd = 0
            for ti in range(16):
                @b.sync
                def _(e, ti=ti, vtr_rd=vtr_rd):
                    e.wait_ge(k.sem["vec"], vtr_rd)
                    k.dwait_all(e, "st")
                    k.ldh = k.dma(e, "ldw", vtr3,
                                  vtok[:, ti].rearrange("h p e -> p h e"))
                vtr_h = k.ldh

                @b.vector
                def _(e, ti=ti, vtr_h=vtr_h, p_done=p_done):
                    e = k.V(e)
                    k.dwait(e._e, vtr_h)
                    e._e.wait_ge(k.sem["vec"], p_done)
                    for s2 in range(2):
                        e.tensor_tensor(out=PMS[:, s2 * 16:(s2 + 1) * 16],
                                        in0=ps3[:, ti, :], in1=mk3[:, s2, ti, :],
                                        op=OP.mult)

                pe0t = k.v("pe")
                vec0t = k.v("vec")
                avh_prd = {0: avh_rd, 1: avh_rd}
                for idx in range(32):
                    hh, s2 = idx // 2, idx % 2
                    pbase = (hh % 2) * 64

                    @b.vector
                    def _(e, idx=idx, hh=hh, s2=s2, avw=avh_prd[idx % 2]):
                        e = k.V(e)
                        e._e.wait_ge(k.sem["pe"], avw)
                        e.tensor_scalar(out=AVH[:, (idx % 2) * 64:(idx % 2 + 1) * 64],
                                        in0=vtr3[:, hh, :],
                                        scalar1=PMS[:, s2 * 16 + hh:s2 * 16 + hh + 1],
                                        scalar2=0.0, op0=OP.mult, op1=OP.bypass)
                    avh_v = k.v("vec")

                    @b.tensor
                    def _(e, idx=idx, pbase=pbase, avh_v=avh_v, vec0t=vec0t,
                          tbw=tb2[idx % 2]):
                        e.wait_ge(k.sem["vec"], avh_v)
                        if idx < 2:
                            e.wait_ge(k.sem["vec"], tbw)
                        else:
                            e.wait_ge(k.sem["vec"], vec0t + 2 * idx - 2)
                        ins = e.transpose(
                            out=pb[4 + idx % 2][pbase:pbase + 64, 0:64].bitcast(BF16),
                            in_=AVH[:, (idx % 2) * 64:(idx % 2 + 1) * 64],
                            identity=k.identb)
                        k.inc(ins, "pe")
                    avh_prd[idx % 2] = k.v("pe")
                    avh_rd = k.v("pe")
                    trv = k.v("pe")

                    @b.vector
                    def _(e, idx=idx, hh=hh, s2=s2, pbase=pbase, trv=trv,
                          avst_rd=avst_rd):
                        e = k.V(e)
                        e._e.wait_ge(k.sem["pe"], trv)
                        e._e.wait_ge(k.sem["pe"], avst_rd)
                        dst = AVST0 if s2 == 0 else AVST1
                        e.tensor_copy(
                            out=dst[pbase:pbase + 64,
                                    (hh // 2) * 128:(hh // 2 + 1) * 128],
                            in_=pb[4 + idx % 2][pbase:pbase + 64, 0:64]
                            .bitcast(BF16))
                tb2[0] = k.v("vec")
                tb2[1] = k.v("vec")
                av_done = k.v("vec")
                vtr_rd = av_done

                @b.tensor
                def _(e, av_done=av_done, wo_h=wo_h, a8_rd=dict(a8_rd)):
                    e.wait_ge(k.sem["vec"], av_done)
                    k.dwait(e, wo_h)
                    for s2 in range(2):
                        src = AVST0 if s2 == 0 else AVST1
                        for dh in range(2):
                            bank = s2 * 2 + dh
                            e.wait_ge(k.sem["vec"], a8_rd[bank])
                            for kt in range(8):
                                ins = e.matmul(
                                    out=pb[bank][:, 0:512],
                                    lhsT=src[:, kt * 128:(kt + 1) * 128],
                                    rhs=wo3[:, kt, dh * 512:(dh + 1) * 512],
                                    start=(kt == 0), stop=(kt == 7))
                            k.inc(ins, "pe")
                avst_rd = k.v("pe")
                omm = k.v("pe")

                for s2 in range(2):
                    g = s2 * 16 + ti

                    @b.vector
                    def _(e, s2=s2, g=g, ti=ti, omm=omm):
                        e = k.V(e)
                        e._e.wait_ge(k.sem["pe"], omm)
                        e.tensor_reduce(out=NSC[:, s2:s2 + 1],
                                        in_=mk3[:, s2, ti, :], axis=AXX, op=OP.add)
                        e.tensor_scalar(out=XLN[:], in0=k.x3[:, g, :],
                                        scalar1=k.st3[:, g, 0:1],
                                        scalar2=k.st3[:, g, 1:2],
                                        op0=OP.subtract, op1=OP.mult)
                        for dh in range(2):
                            e.scalar_tensor_tensor(
                                out=YB[:, dh * 512:(dh + 1) * 512],
                                in0=BOB[:, dh * 512:(dh + 1) * 512],
                                scalar=NSC[:, s2:s2 + 1],
                                in1=pb[s2 * 2 + dh][:, 0:512],
                                op0=OP.mult, op1=OP.add)
                        e.tensor_tensor(out=YB[:], in0=YB[:], in1=XLN[:], op=OP.add)
                        e.bn_stats(out=k.LNTMP[:, 0:6], in_=YB[:, 0:512])
                        e.bn_stats(out=k.LNTMP[:, 6:12], in_=YB[:, 512:1024])
                        e.bn_aggr(out=YST[:], in_=k.LNTMP[:])
                    yst_v = k.v("vec")

                    @b.scalar
                    def _(e, yst_v=yst_v):
                        e = k.S(e)
                        e._e.wait_ge(k.sem["vec"], yst_v)
                        e.activation(out=YST[:, 1:2], in_=YST[:, 1:2],
                                     func=AF.Ln, bias=k.eps_ln)
                        e.activation(out=YST[:, 1:2], in_=YST[:, 1:2],
                                     func=AF.Exp, scale=-0.5)
                    rstd_v = k.v("sc")

                    @b.vector
                    def _(e, g=g, rstd_v=rstd_v):
                        e = k.V(e)
                        e._e.wait_ge(k.sem["sc"], rstd_v)
                        e.tensor_scalar(out=YB[:], in0=YB[:],
                                        scalar1=YST[:, 0:1], scalar2=YST[:, 1:2],
                                        op0=OP.subtract, op1=OP.mult)
                        e.tensor_tensor(out=YB[:], in0=YB[:], in1=LNGB[:],
                                        op=OP.mult)
                        e.scalar_tensor_tensor(out=k.x3[:, g, :], in0=YB[:],
                                               scalar=0.5, in1=k.x3[:, g, :],
                                               op0=OP.mult, op1=OP.add)
                        e.scalar_tensor_tensor(out=k.x3[:, g, :], in0=LNBB[:],
                                               scalar=0.5, in1=k.x3[:, g, :],
                                               op0=OP.mult, op1=OP.add)
                    a8_rd[s2 * 2 + 0] = k.v("vec")
                    a8_rd[s2 * 2 + 1] = k.v("vec")


def phase_mamba(k):
    nc, b, t = k.nc, k.block, k.t
    SEG = 512
    NSEG = L // SEG
    NB = 2
    NBLK = 16 // NB
    pb = k.pb
    if "bmc_rows" not in k.dram:
        k.dram["bmc_rows"] = nc.dram_tensor("bmc_rows", [2, 2, 16 * SEG], F32)
    bmc = k.dram["bmc_rows"].ap()

    with ExitStack() as ps:
        sb = lambda nm, sh, dt: ps.enter_context(nc.sbuf_tensor(nm + "mb", sh, dt))
        BCW = sb("bcw", [128, 8 * 32], F32)
        ABUF = sb("abuf", [128, 8 * 16], F32)
        RBUF = sb("rbuf", [128, 8 * 16], F32)
        DBIA = sb("dbia", [128, 8], F32)
        SELB = sb("selbm", [128, 8], F32)
        SEL = sb("selm", [128, 8 * 2], F32)
        ECHL = sb("echl", [128, 16], F32)
        CELL = sb("cellm", [32, 16], F32)
        LNA = sb("lnam", [128, D], F32)
        XST = sb("xstm", [128, 8 * SEG], F32)
        DSEG = sb("dsegm", [128, 8 * SEG], F32)
        TMPR = sb("tmprm", [128, SEG], F32)
        YAC = sb("yacm", [128, SEG], F32)
        BCM = sb("bcmm", [32, SEG], F32)
        HST = sb("hstm", [128, 8 * 2 * 16], F32)
        hst4 = HST[:].rearrange("p (dc bb n) -> p dc bb n", dc=8, bb=2)
        bcw3 = BCW[:].rearrange("p (kt c) -> p kt c", kt=8)
        ab3 = ABUF[:].rearrange("p (dc n) -> p dc n", dc=8)
        rb3 = RBUF[:].rearrange("p (dc n) -> p dc n", dc=8)
        sel3 = SEL[:].rearrange("p (dc bb) -> p dc bb", dc=8)
        xst3 = XST[:].rearrange("p (dc tt) -> p dc tt", dc=8)
        dseg3 = DSEG[:].rearrange("p (dd tt) -> p dd tt", dd=8)

        snap0 = _snap(k)

        @b.sync
        def _(e):
            _bar(k, e, snap0)
            k.dma(e, "ldw", bcw3, t["BCWT"].rearrange("a p c -> p a c"))
            k.dma(e, "ldw", ab3, t["A_log"].rearrange("a p c -> p a c"))
            k.dma(e, "ldw", DBIA[:, 0:8], t["dbias"])
            k.dma(e, "ldw", TMPR[:, 0:8], t["dtbias"])
            k.dma(e, "ldw", SELB[:, 0:8], t["selb"])
            k.dma(e, "ldw", CELL[0:16, 2:3], t["Wib"])
            k.dma(e, "ldw", CELL[0:16, 3:4], t["h0T"])
            k.dma(e, "ldw", CELL[0:16, 4:5], t["tau"])
            k.dma(e, "ldw", CELL[0:16, 7:8], t["Bb"])
            k.ldh = k.dma(e, "ldw", CELL[16:32, 7:8], t["Cb"])
        wl_h = k.ldh

        @b.scalar
        def _(e, wl_h=wl_h):
            e = k.S(e)
            k.dwait(e._e, wl_h)
            k.dwait_all(e._e, "ldw")
            e.activation(out=ABUF[:], in_=ABUF[:], func=AF.Exp)
        aex = k.v("sc")

        @b.vector
        def _(e, aex=aex):
            e = k.V(e)
            e._e.wait_ge(k.sem["sc"], aex)
            k.dwait_all(e._e, "ldw")
            e.tensor_scalar(out=ABUF[:], in0=ABUF[:], scalar1=-1.0, scalar2=0.0,
                            op0=OP.mult, op1=OP.bypass)
            e.reciprocal(out=RBUF[:], in_=ABUF[:])
            e.tensor_scalar(out=SELB[:, 0:8], in0=SELB[:, 0:8], scalar1=-1.0,
                            scalar2=0.0, op0=OP.mult, op1=OP.bypass)
            e.tensor_tensor(out=DBIA[:, 0:8], in0=DBIA[:, 0:8], in1=TMPR[:, 0:8], op=OP.add)
            e.tensor_scalar(out=CELL[0:16, 5:6], in0=CELL[0:16, 4:5], scalar1=0.1,
                            scalar2=10.0, op0=OP.max, op1=OP.min)
            e.reciprocal(out=CELL[0:16, 6:7], in_=CELL[0:16, 5:6])

        ln_stats_all(k)
        sc_rdy = k.v("sc")

        with ExitStack() as p0s:
            sb0 = lambda nm, sh, dt: p0s.enter_context(nc.sbuf_tensor(nm + "m0", sh, dt))
            WIT = sb0("wit", [128, 8 * 16], F32)
            SWT = sb0("swt", [16, 8 * 128], F32)
            WRT = sb0("wrt", [16, 16], F32)
            GCT = sb0("gct", [128, 18], F32)
            PRD = sb0("prd", [128, 32], F32)
            wit3 = WIT[:].rearrange("p (kt c) -> p kt c", kt=8)
            swt3 = SWT[:].rearrange("p (dc c) -> p dc c", dc=8)

            @b.sync
            def _(e):
                k.dma(e, "ldw", wit3, t["WiT"].rearrange("a p c -> p a c"))
                k.dma(e, "ldw", swt3, t["selWT"])
                k.ldh = k.dma(e, "ldw", WRT[0:16, 0:16], t["WrT"])
            w2_h = k.ldh

            @b.vector
            def _(e, sc_rdy=sc_rdy):
                e = k.V(e)
                e._e.wait_ge(k.sem["sc"], sc_rdy)
                e.tensor_tensor(out=PRD[:, 0:32], in0=k.st3[:, :, 0], in1=k.st3[:, :, 1],
                                op=OP.mult)
            prdv = k.v("vec")

            @b.tensor
            def _(e, prdv=prdv, sc_rdy=sc_rdy):
                e.wait_ge(k.sem["vec"], prdv)
                e.wait_ge(k.sem["sc"], sc_rdy)
                for bb in range(2):
                    for dc in range(8):
                        for g in range(16):
                            ins = e.matmul(out=pb[5][:, dc * 2 + bb:dc * 2 + bb + 1],
                                           lhsT=k.x3[:, bb * 16 + g, dc * 128:(dc + 1) * 128],
                                           rhs=k.st3[:, bb * 16 + g, 1:2],
                                           start=(g == 0), stop=(g == 15))
                ins2 = e.matmul(out=pb[4][0:1, 0:32], lhsT=k.ones, rhs=PRD[:, 0:32],
                                start=True, stop=True)
                k.inc(ins2, "pe")
            gcp = k.v("pe")

            @b.vector
            def _(e, gcp=gcp):
                e = k.V(e)
                e._e.wait_ge(k.sem["pe"], gcp)
                e.tensor_reduce(out=CELL[0:1, 0:1], in_=pb[4][0:1, 0:16], axis=AXX, op=OP.add)
                e.tensor_reduce(out=CELL[0:1, 1:2], in_=pb[4][0:1, 16:32], axis=AXX, op=OP.add)
            sbv = k.v("vec")

            @b.gpsimd
            def _(e, sbv=sbv):
                e.wait_ge(k.sem["vec"], sbv)
                k.ldh = k.dma(e, "ldg", GCT[:, 16:18],
                              CELL[0:1, 0:2].unsqueeze(1).broadcast_to([1, 128, 2]))
            sb_h = k.ldh

            @b.vector
            def _(e, sb_h=sb_h, gcp=gcp):
                e = k.V(e)
                k.dwait(e._e, sb_h)
                e._e.wait_ge(k.sem["pe"], gcp)
                for bb in range(2):
                    for dc in range(8):
                        e.tensor_scalar(out=GCT[:, dc * 2 + bb:dc * 2 + bb + 1],
                                        in0=pb[5][:, dc * 2 + bb:dc * 2 + bb + 1],
                                        scalar1=GCT[:, 16 + bb:17 + bb], scalar2=1.0 / L,
                                        op0=OP.subtract, op1=OP.mult)
                e.tensor_copy(out=CELL[0:16, 12:13], in_=CELL[0:16, 3:4])
                e.tensor_copy(out=CELL[0:16, 13:14], in_=CELL[0:16, 3:4])
            gctv = k.v("vec")

            @b.tensor
            def _(e, gctv=gctv, w2_h=w2_h):
                e.wait_ge(k.sem["vec"], gctv)
                k.dwait_all(e, "ldw")
                for dc in range(8):
                    ins = e.matmul(out=pb[6][0:16, 0:2], lhsT=wit3[:, dc, :],
                                   rhs=GCT[:, 0:16].rearrange("p (dc bb) -> p dc bb", dc=8)[:, dc, :],
                                   start=(dc == 0), stop=(dc == 7))
                ins2 = e.matmul(out=pb[7][0:16, 0:2], lhsT=WRT[0:16, 0:16],
                                rhs=CELL[0:16, 12:14], start=True, stop=True)
                k.inc(ins2, "pe")
            cellp = k.v("pe")

            @b.scalar
            def _(e, cellp=cellp):
                e = k.S(e)
                e._e.wait_ge(k.sem["pe"], cellp)
                e.activation(out=CELL[0:16, 8:10], in_=pb[6][0:16, 0:2], func=AF.Identity,
                             bias=CELL[0:16, 2:3])
            isv = k.v("sc")

            @b.vector
            def _(e, isv=isv, cellp=cellp):
                e = k.V(e)
                e._e.wait_ge(k.sem["sc"], isv)
                e._e.wait_ge(k.sem["pe"], cellp)
                e.tensor_tensor(out=CELL[0:16, 8:10], in0=CELL[0:16, 8:10],
                                in1=pb[7][0:16, 0:2], op=OP.add)
            zv = k.v("vec")

            @b.scalar
            def _(e, zv=zv):
                e = k.S(e)
                e._e.wait_ge(k.sem["vec"], zv)
                e.activation(out=CELL[0:16, 10:12], in_=CELL[0:16, 8:10], func=AF.Exp,
                             scale=2.0)
            ze = k.v("sc")

            @b.vector
            def _(e, ze=ze):
                e = k.V(e)
                e._e.wait_ge(k.sem["sc"], ze)
                e.tensor_scalar(out=CELL[0:16, 10:12], in0=CELL[0:16, 10:12], scalar1=1.0,
                                scalar2=0.0, op0=OP.add, op1=OP.bypass)
                e.reciprocal(out=CELL[0:16, 10:11], in_=CELL[0:16, 10:11])
                e.reciprocal(out=CELL[0:16, 11:12], in_=CELL[0:16, 11:12])
                e.tensor_scalar(out=CELL[0:16, 10:12], in0=CELL[0:16, 10:12], scalar1=-2.0,
                                scalar2=1.0, op0=OP.mult, op1=OP.add)
                e.tensor_tensor(out=CELL[0:16, 10:12], in0=CELL[0:16, 10:12],
                                in1=CELL[0:16, 12:14], op=OP.subtract)
                e.tensor_scalar(out=CELL[0:16, 10:12], in0=CELL[0:16, 10:12],
                                scalar1=CELL[0:16, 6:7], scalar2=0.1, op0=OP.mult, op1=OP.mult)
                e.tensor_tensor(out=CELL[0:16, 14:16], in0=CELL[0:16, 10:12],
                                in1=CELL[0:16, 12:14], op=OP.add)
            nhv = k.v("vec")

            @b.tensor
            def _(e, nhv=nhv):
                e.wait_ge(k.sem["vec"], nhv)
                k.dwait_all(e, "ldw")
                for dc in range(8):
                    ins = e.matmul(out=pb[6][:, dc * 2:dc * 2 + 2], lhsT=swt3[0:16, dc, :],
                                   rhs=CELL[0:16, 14:16], start=True, stop=True)
                k.inc(ins, "pe")
            selp = k.v("pe")

            @b.scalar
            def _(e, selp=selp):
                e = k.S(e)
                e._e.wait_ge(k.sem["pe"], selp)
                for dc in range(8):
                    e.activation(out=ECHL[:, dc * 2:dc * 2 + 2],
                                 in_=pb[6][:, dc * 2:dc * 2 + 2],
                                 func=AF.Exp, scale=-1.0, bias=SELB[:, dc:dc + 1])
            ech = k.v("sc")

            @b.vector
            def _(e, ech=ech):
                e = k.V(e)
                e._e.wait_ge(k.sem["sc"], ech)
                e.tensor_scalar(out=ECHL[:], in0=ECHL[:], scalar1=1.0, scalar2=0.0,
                                op0=OP.add, op1=OP.bypass)
                e.reciprocal(out=SEL[:], in_=ECHL[:])
                e.memset(HST[:], 0.0)
            selv = k.v("vec")

        DWTC = sb("dwtc", [128, 2 * 8 * 128], F32)
        GB = sb("gbm", [128, 2 * NB * SEG], F32)
        WB = sb("wbm", [128, NB * SEG], F32)
        HB = sb("hbm", [128, NB * SEG], F32)
        BMR = sb("bmrm", [128, NB * SEG], F32)
        CMR = sb("cmrm", [128, NB * SEG], F32)
        dwtc4 = DWTC[:].rearrange("p (u kt c) -> p u kt c", u=2, kt=8)

        k.cnt["_mbxst"] = k.v("vec")
        k.cnt["_mbscan"] = k.v("vec")
        k.cnt["_mbscb0"] = k.v("vec")
        k.cnt["_mbscb1"] = k.v("vec")
        k.cnt["_mbyev"] = k.v("vec")
        k.cnt["_mbdse"] = k.v("sc")
        k.cnt["_mbbcv"] = k.v("sc")
        dwt_h = {0: None, 1: None}
        pk_h = {}
        for bb in range(2):
            for sg in range(NSEG):
                s0 = sg * SEG
                tiles = [bb * 16 + (s0 // 128) + i for i in range(4)]

                for i, g in enumerate(tiles):
                    lnw = k.v("pe")

                    @b.vector
                    def _(e, g=g, lnw=lnw, selv=selv):
                        e = k.V(e)
                        e._e.wait_ge(k.sem["vec"], selv)
                        e._e.wait_ge(k.sem["pe"], lnw)
                        e.tensor_scalar(out=LNA[:], in0=k.x3[:, g, :],
                                        scalar1=k.st3[:, g, 0:1], scalar2=k.st3[:, g, 1:2],
                                        op0=OP.subtract, op1=OP.mult)
                    lnv = k.v("vec")
                    for dc in range(8):
                        @b.tensor
                        def _(e, lnv=lnv, dc=dc, i=i):
                            e.wait_ge(k.sem["vec"], lnv)
                            if dc >= 2:
                                e.wait_ge(k.sem["vec"], k.cnt["_mbxst"])
                            elif i == 0 and dc == 0:
                                e.wait_ge(k.sem["vec"], k.cnt["_mbxstall"]
                                          if "_mbxstall" in k.cnt else 0)
                            ins = e.transpose(out=pb[6 + (dc % 2)][:, 0:128],
                                              in_=LNA[:, dc * 128:(dc + 1) * 128],
                                              identity=k.ident)
                            k.inc(ins, "pe")
                        trp = k.v("pe")

                        @b.vector
                        def _(e, trp=trp, dc=dc, i=i, bb=bb):
                            e = k.V(e)
                            e._e.wait_ge(k.sem["pe"], trp)
                            e.tensor_scalar(out=xst3[:, dc, i * 128:(i + 1) * 128],
                                            in0=pb[6 + (dc % 2)][:, 0:128],
                                            scalar1=sel3[:, dc, bb:bb + 1], scalar2=0.0,
                                            op0=OP.mult, op1=OP.bypass)
                        k.cnt["_mbxst"] = k.v("vec")
                k.cnt["_mbxstall"] = k.v("vec")
                xstv = k.v("vec")

                scb = k.v("sc")
                for dd in range(8):
                    @b.sync
                    def _(e, dd=dd, bb=bb, sg=sg):
                        if bb == 0 and sg == 0 and dd == 0:
                            k.dwait_all(e, "ldw")
                            k.wait(e, "pe")
                            k.wait(e, "vec")
                            k.wait(e, "sc")
                        if dwt_h[dd % 2] is not None:
                            e.wait_ge(k.sem["pe"], k.cnt[f"_mbdmm{dd % 2}"])
                        k.ldh = k.dma(e, "ldw", dwtc4[:, dd % 2, :, :],
                                      t["dWT"].rearrange("a p dd c -> p a dd c")[:, :, dd, :])
                    dwt_h[dd % 2] = k.ldh
                    dh = k.ldh

                    @b.tensor
                    def _(e, dd=dd, xstv=xstv, dh=dh, scb=scb):
                        e.wait_ge(k.sem["vec"], xstv)
                        k.dwait(e, dh)
                        if dd >= 2:
                            e.wait_ge(k.sem["sc"], scb + 2 * (dd - 1))
                        for kt in range(8):
                            ins = e.matmul(out=pb[dd % 2][:, 0:512],
                                           lhsT=dwtc4[:, dd % 2, kt, :], rhs=xst3[:, kt, :],
                                           start=(kt == 0), stop=(kt == 7))
                        k.inc(ins, "pe")
                        k.cnt[f"_mbdmm{dd % 2}"] = k.v("pe")
                    dmm = k.v("pe")

                    @b.scalar
                    def _(e, dd=dd, dmm=dmm):
                        e = k.S(e)
                        e._e.wait_ge(k.sem["pe"], dmm)
                        e._e.wait_ge(k.sem["vec"], k.cnt["_mbscan"])
                        e.activation(out=dseg3[:, dd, :], in_=pb[dd % 2][:, 0:512],
                                     func=AF.Exp, bias=DBIA[:, dd:dd + 1])
                        e.activation(out=dseg3[:, dd, :], in_=dseg3[:, dd, :],
                                     func=AF.Ln, bias=1.0)
                dse = k.v("sc")

                @b.tensor
                def _(e, xstv=xstv):
                    e.wait_ge(k.sem["vec"], xstv)
                    e.wait_ge(k.sem["sc"], k.cnt["_mbbcv"])
                    for kt in range(8):
                        ins = e.matmul(out=pb[2][0:32, 0:512], lhsT=bcw3[:, kt, :],
                                       rhs=xst3[:, kt, :], start=(kt == 0), stop=(kt == 7))
                    k.inc(ins, "pe")
                bcp = k.v("pe")

                @b.scalar
                def _(e, bcp=bcp, sg=sg):
                    e = k.S(e)
                    e._e.wait_ge(k.sem["pe"], bcp)
                    for par in (0, 1):
                        for hh in pk_h.get(par, []):
                            k.dwait(e._e, hh)
                    e.activation(out=BCM[0:32, 0:512], in_=pb[2][0:32, 0:512],
                                 func=AF.Identity, bias=CELL[0:32, 7:8])
                bce = k.v("sc")
                k.cnt["_mbbcv"] = bce

                @b.gpsimd
                def _(e, bce=bce, sg=sg):
                    e.wait_ge(k.sem["sc"], bce)
                    e.wait_ge(k.sem["vec"], k.cnt["_mbscan"])
                    h1 = k.dma(e, "ldg", bmc[sg % 2, 0, :].unsqueeze(0)
                               .rearrange("o (n tt) -> o n tt", n=16),
                               BCM[0:16, 0:512])
                    h2 = k.dma(e, "ldg", bmc[sg % 2, 1, :].unsqueeze(0)
                               .rearrange("o (n tt) -> o n tt", n=16),
                               BCM[16:32, 0:512])
                    pk_h[sg % 2] = [h1, h2]
                    k.cnt[f"_mbpk{sg % 2}"] = 1
                pkb, pkc = pk_h[sg % 2]

                for dc in range(8):
                    for nb in range(NBLK):
                        n0 = nb * NB
                        it = dc * NBLK + nb
                        par = it % 2
                        gb = GB[:, par * NB * SEG:(par + 1) * NB * SEG]
                        bmr = BMR[:]
                        cmr = CMR[:]

                        @b.sync
                        def _(e, sg=sg, n0=n0, pkb=pkb, pkc=pkc, par=par,
                              bmr=bmr, cmr=cmr):
                            k.dwait(e, pkb)
                            k.dwait(e, pkc)
                            e.wait_ge(k.sem["vec"], k.cnt[f"_mbscb{par}"])
                            e.wait_ge(k.sem["vec"], k.cnt["_mbscan"])
                            k.ldh = k.dma(e, "ldw", bmr,
                                          bmc[sg % 2, 0, n0 * SEG:(n0 + NB) * SEG]
                                          .unsqueeze(0).unsqueeze(1)
                                          .broadcast_to([1, 128, NB * SEG]))
                            k.ldh2 = k.dma(e, "ldw", cmr,
                                           bmc[sg % 2, 1, n0 * SEG:(n0 + NB) * SEG]
                                           .unsqueeze(0).unsqueeze(1)
                                           .broadcast_to([1, 128, NB * SEG]))
                        bm_h, cm_h = k.ldh, k.ldh2

                        @b.scalar
                        def _(e, dc=dc, n0=n0, dse=dse, par=par, gb=gb):
                            e = k.S(e)
                            e._e.wait_ge(k.sem["sc"], dse)
                            e._e.wait_ge(k.sem["vec"], k.cnt[f"_mbscb{par}"])
                            for jn in range(NB):
                                e.activation(out=gb[:, jn * SEG:(jn + 1) * SEG],
                                             in_=dseg3[:, dc, :], func=AF.Exp,
                                             scale=ab3[:, dc, n0 + jn:n0 + jn + 1])
                        gse = k.v("sc")

                        @b.vector
                        def _(e, dc=dc, n0=n0, gse=gse, bm_h=bm_h, cm_h=cm_h,
                              bb=bb, gb=gb, bmr=bmr, cmr=cmr):
                            e = k.V(e)
                            e._e.wait_ge(k.sem["sc"], gse)
                            k.dwait(e._e, bm_h)
                            e.tensor_tensor(out=WB[:].rearrange("p (n tt) -> p n tt", n=NB),
                                            in0=bmr.rearrange("p (n tt) -> p n tt", n=NB),
                                            in1=xst3[:, dc, :].unsqueeze(1)
                                            .broadcast_to([128, NB, SEG]),
                                            op=OP.mult)
                            e.scalar_tensor_tensor(out=WB[:], in0=gb, scalar=-1.0,
                                                   in1=WB[:], op0=OP.add, op1=OP.mult)
                            for jn in range(NB):
                                e.tensor_tensor_scan(
                                    out=HB[:, jn * SEG:(jn + 1) * SEG],
                                    data0=gb[:, jn * SEG:(jn + 1) * SEG],
                                    data1=WB[:, jn * SEG:(jn + 1) * SEG],
                                    initial=hst4[:, dc, bb, n0 + jn:n0 + jn + 1],
                                    op0=OP.mult, op1=OP.add)
                                e.tensor_copy(out=hst4[:, dc, bb, n0 + jn:n0 + jn + 1],
                                              in_=HB[:, (jn + 1) * SEG - 1:(jn + 1) * SEG])
                                if jn == 0:
                                    k.dwait(e._e, cm_h)
                                e.scalar_tensor_tensor(
                                    out=HB[:, jn * SEG:(jn + 1) * SEG],
                                    in0=HB[:, jn * SEG:(jn + 1) * SEG],
                                    scalar=rb3[:, dc, n0 + jn:n0 + jn + 1],
                                    in1=cmr[:, jn * SEG:(jn + 1) * SEG],
                                    op0=OP.mult, op1=OP.mult)
                            if n0 == 0:
                                e.tensor_tensor(out=YAC[:], in0=HB[:, 0:SEG],
                                                in1=HB[:, SEG:2 * SEG], op=OP.add)
                            else:
                                e.tensor_tensor(out=TMPR[:], in0=HB[:, 0:SEG],
                                                in1=HB[:, SEG:2 * SEG], op=OP.add)
                                e.tensor_tensor(out=YAC[:], in0=YAC[:], in1=TMPR[:],
                                                op=OP.add)
                            k.cnt["_mbscan"] = k.v("vec")
                            k.cnt[f"_mbscb{par}"] = k.v("vec")
                    yv = k.v("vec")

                    for i in range(4):
                        @b.tensor
                        def _(e, yv=yv, i=i):
                            e.wait_ge(k.sem["vec"], yv)
                            if i >= 2:
                                e.wait_ge(k.sem["vec"], k.cnt["_mbyev"])
                            ins = e.transpose(out=pb[4 + (i % 2)][:, 0:128],
                                              in_=YAC[:, i * 128:(i + 1) * 128],
                                              identity=k.ident)
                            k.inc(ins, "pe")
                        ytp = k.v("pe")

                        @b.vector
                        def _(e, ytp=ytp, dc=dc, i=i, tiles=tiles):
                            e = k.V(e)
                            e._e.wait_ge(k.sem["pe"], ytp)
                            e.tensor_tensor(out=k.x3[:, tiles[i], dc * 128:(dc + 1) * 128],
                                            in0=k.x3[:, tiles[i], dc * 128:(dc + 1) * 128],
                                            in1=pb[4 + (i % 2)][:, 0:128], op=OP.add)
                        k.cnt["_mbyev"] = k.v("vec")


_QUAT_TABLE = {
    0: [(0, 1.0), (1, -1.0), (2, -1.0), (3, -1.0)],
    1: [(1, 1.0), (0, 1.0), (3, 1.0), (2, -1.0)],
    2: [(2, 1.0), (3, -1.0), (0, 1.0), (1, 1.0)],
    3: [(3, 1.0), (2, 1.0), (1, -1.0), (0, 1.0)],
}
_QUAT_CW = {}
for _comp, _lst in _QUAT_TABLE.items():
    for _w, (_c, _s) in enumerate(_lst):
        _QUAT_CW[(_c, _comp)] = (_w, _s)


def _fence(k):
    b = k.block
    snap = _snap(k)

    @b.sync
    def _(e):
        _bar(k, e, snap)

    @b.tensor
    def _(e):
        _bar(k, e, snap)

    @b.vector
    def _(e):
        _bar(k, e, snap)

    @b.scalar
    def _(e):
        _bar(k, e, snap)

    @b.gpsimd
    def _(e):
        _bar(k, e, snap)


def phase_l1attn(k):
    nc, b, t = k.nc, k.block, k.t
    pb = k.pb
    H = 16
    if "lnt_spill" not in k.dram:
        k.dram["lnt_spill"] = nc.dram_tensor("lnt_spill", [8, 128, L], BF16)
    if "bo_row" not in k.dram:
        k.dram["bo_row"] = nc.dram_tensor("bo_row", [1, 512], BF16)
    lnt_sp = k.dram["lnt_spill"].ap()
    bo_row = k.dram["bo_row"].ap()
    kv_sp = k.dram["kv_spill"].ap()

    with ExitStack() as ps:
        sb = lambda nm, sh, dt: ps.enter_context(nc.sbuf_tensor(nm + "at", sh, dt))
        QBUF = sb("qbuf", [128, 8 * L], BF16)
        QL = sb("ql", [128, 8 * 4 * 128], BF16)
        BIA = sb("bia", [128, 8], F32)
        BL = sb("bl", [128, 8], F32)
        BO8 = sb("bo8", [4, 128], BF16)
        BOROW = sb("borow", [1, 512], BF16)
        ONER = sb("oner", [1, 128], BF16)
        ONERF = sb("onerf", [1, 128], F32)
        AVS = sb("avs", [64, 512], BF16)
        RLS = sb("rls", [1, 512], F32)
        JC = sb("jc", [128, 8], F32)
        CB = sb("cb", [128, 2], F32)
        QBUF3 = QBUF[:].rearrange("p (dc c) -> p dc c", dc=8)
        ql4 = QL[:].rearrange("p (dc comp c) -> p dc comp c", dc=8, comp=4)
        x34 = k.x3.rearrange("p g (comp two c) -> p g comp two c", two=2, c=128)

        _fence(k)

        @b.sync
        def _(e):
            k.ldh = k.dma(e, "ldw", JC[:, 0:8], t["J"])
        j_h = k.ldh

        @b.vector
        def _(e, j_h=j_h):
            e = k.V(e)
            e.memset(ONER[0:1, 0:128], 1.0)
            e.memset(ONERF[0:1, 0:128], 1.0)
            k.dwait(e._e, j_h)
            TWO_PI = float(2.0 * np.pi)
            e.tensor_scalar(out=BIA[:, 0:8], in0=JC[:, 0:8], scalar1=1.0 / TWO_PI,
                            scalar2=MAGIC, op0=OP.mult, op1=OP.add)
            e.tensor_scalar(out=BIA[:, 0:8], in0=BIA[:, 0:8], scalar1=MAGIC, scalar2=0.0,
                            op0=OP.subtract, op1=OP.bypass)
            e.scalar_tensor_tensor(out=BIA[:, 0:8], in0=BIA[:, 0:8], scalar=-TWO_PI,
                                   in1=JC[:, 0:8], op0=OP.mult, op1=OP.add)
        jr_v = k.v("vec")

        @b.scalar
        def _(e, jr_v=jr_v):
            e = k.S(e)
            e._e.wait_ge(k.sem["vec"], jr_v)
            e.activation(out=BIA[:, 0:8], in_=BIA[:, 0:8], func=AF.Sin)
        sin_v = k.v("sc")

        @b.vector
        def _(e, sin_v=sin_v):
            e = k.V(e)
            e._e.wait_ge(k.sem["sc"], sin_v)
            e.tensor_reduce(out=CB[:, 1:2], in_=BIA[:, 0:8], axis=AXX, op=OP.add)
        jsum = k.v("vec")

        @b.tensor
        def _(e, jsum=jsum):
            e.wait_ge(k.sem["vec"], jsum)
            ins = e.matmul(out=pb[5][0:1, 0:1], lhsT=k.ones, rhs=CB[:, 1:2],
                           start=True, stop=True)
            k.inc(ins, "pe")
        jp = k.v("pe")

        @b.vector
        def _(e, jp=jp):
            e = k.V(e)
            e._e.wait_ge(k.sem["pe"], jp)
            e.tensor_scalar(out=CB[0:1, 1:2], in0=pb[5][0:1, 0:1],
                            scalar1=0.1 / 1024.0, scalar2=0.0, op0=OP.mult, op1=OP.bypass)
        cb_v = k.v("vec")

        @b.gpsimd
        def _(e, cb_v=cb_v):
            e.wait_ge(k.sem["vec"], cb_v)
            k.ldh = k.dma(e, "ldg", CB[:, 0:1],
                          CB[0:1, 1:2].unsqueeze(1).broadcast_to([1, 128, 1]))
        cbr_h = k.ldh

        sc_rdy = ln_stats_all(k)

        sc_bank = {0: 0, 1: 0, 4: 0, 5: 0}
        pt_rd = {0: 0, 1: 0, 2: 0, 3: 0}
        av_rd = {0: 0, 1: 0}
        k.cnt["_atfl"] = k.v("pe")
        qf_rd = 0
        vst_h = None
        avs_h = None
        rlh = None

        for bb in range(2):
            with ExitStack() as s1:
                sb1 = lambda nm, sh, dt: s1.enter_context(
                    nc.sbuf_tensor(f"{nm}at{bb}", sh, dt))
                LTS = sb1("lts", [128, D], BF16)
                LTS2 = sb1("lts2", [128, D], BF16)
                _fence(k)
                st_h = None
                for ti in range(16):
                    g = bb * 16 + ti
                    lnw = k.v("pe")

                    @b.vector
                    def _(e, g=g, lnw=lnw, sc_rdy=sc_rdy):
                        e = k.V(e)
                        e._e.wait_ge(k.sem["sc"], sc_rdy)
                        e._e.wait_ge(k.sem["pe"], lnw)
                        e.tensor_scalar(out=LTS[:], in0=k.x3[:, g, :],
                                        scalar1=k.st3[:, g, 0:1], scalar2=k.st3[:, g, 1:2],
                                        op0=OP.subtract, op1=OP.mult)
                    cst = k.v("vec")
                    wb = [("vec", cst)]
                    if st_h is not None:
                        wb.append((st_h[0], st_h[1]))
                    tr_chain(k, [(LTS[:, dc * 128:(dc + 1) * 128],
                                  LTS2[:, dc * 128:(dc + 1) * 128])
                                 for dc in range(8)], wait_before=wb)
                    vq = k.v("vec")

                    @b.gpsimd
                    def _(e, ti=ti, vq=vq):
                        e.wait_ge(k.sem["vec"], vq)
                        k.ldh = k.dma(
                            e, "st",
                            lnt_sp[:, :, ti * 128:(ti + 1) * 128]
                            .rearrange("dc p c -> p dc c"),
                            LTS2[:].rearrange("p (dc c) -> p dc c", dc=8))
                    st_h = k.ldh

            with ExitStack() as s2:
                sb2 = lambda nm, sh, dt: s2.enter_context(
                    nc.sbuf_tensor(f"{nm}at{bb}", sh, dt))
                WS = sb2("ws", [128, 4 * 2 * 256], F32)
                QF = sb2("qf", [128, 4 * 256], F32)
                QS = sb2("qs", [128, 4 * 256], BF16)
                N2 = sb2("n2", [128, 256], F32)
                SQ = sb2("sq", [128, 256], F32)
                INV = sb2("inv", [128, 256], F32)
                LNC = sb2("lnc", [128, 2 * 8 * 256], BF16)
                ws4 = WS[:].rearrange("p (w u c) -> p w u c", w=4, u=2)
                lnc3 = LNC[:].rearrange("p (u dc c) -> p u dc c", u=2, dc=8)
                qs3 = QS[:].rearrange("p (comp c) -> p comp c", comp=4)
                _fence(k)
                mm_rd = {0: 0, 1: 0}

                for proj in ("q", "k", "v"):
                    penow = k.v("pe")
                    scnow = k.v("sc")
                    vecnow = k.v("vec")

                    @b.sync
                    def _(e, proj=proj, penow=penow, scnow=scnow, vecnow=vecnow):
                        e.wait_ge(k.sem["pe"], penow)
                        e.wait_ge(k.sem["sc"], scnow)
                        e.wait_ge(k.sem["vec"], vecnow)
                        k.dwait_all(e, "st")
                        k.ldh = k.dma(e, "ldw",
                                      WS[:].rearrange("p (w u c) -> p w u c", w=4, u=2),
                                      t[proj + "W"].rearrange("w u p c -> p w u c"))
                        k.ldh2 = k.dma(e, "ldw", BL[:, 0:8],
                                       t[proj + "b"].rearrange("w (u c) -> c (w u)", u=2),
                                       allow_slow_non_contiguous=True)
                    w_h, b_h = k.ldh, k.ldh2

                    for oh in range(2):
                        penow = k.v("pe")
                        scnow = k.v("sc")

                        @b.vector
                        def _(e, oh=oh, w_h=w_h, b_h=b_h, penow=penow, scnow=scnow):
                            e = k.V(e)
                            k.dwait(e._e, w_h)
                            k.dwait(e._e, b_h)
                            e._e.wait_ge(k.sem["pe"], penow)
                            e._e.wait_ge(k.sem["sc"], scnow)
                            for c in range(4):
                                for comp in range(4):
                                    w, s = _QUAT_CW[(c, comp)]
                                    for rh in range(2):
                                        e.tensor_scalar(
                                            out=ql4[:, c * 2 + rh, comp, :],
                                            in0=ws4[:, w, rh, oh * 128:(oh + 1) * 128],
                                            scalar1=float(s), scalar2=0.0,
                                            op0=OP.mult, op1=OP.bypass)
                            for comp in range(4):
                                for wi, (c, s) in enumerate(_QUAT_TABLE[comp]):
                                    if wi == 0:
                                        e.tensor_scalar(
                                            out=BIA[:, comp:comp + 1],
                                            in0=BL[:, wi * 2 + oh:wi * 2 + oh + 1],
                                            scalar1=float(s), scalar2=0.0,
                                            op0=OP.mult, op1=OP.bypass)
                                    else:
                                        e.scalar_tensor_tensor(
                                            out=BIA[:, comp:comp + 1],
                                            in0=BL[:, wi * 2 + oh:wi * 2 + oh + 1],
                                            scalar=float(s),
                                            in1=BIA[:, comp:comp + 1],
                                            op0=OP.mult, op1=OP.add)
                        ql_v = k.v("vec")

                        for rc in range(8):
                            @b.sync
                            def _(e, rc=rc):
                                e.wait_ge(k.sem["pe"], mm_rd[rc % 2])
                                k.dwait_all(e, "st")
                                k.ldh = k.dma(e, "ldw", lnc3[:, rc % 2, :, :],
                                              lnt_sp[:, :, rc * 256:(rc + 1) * 256]
                                              .rearrange("dc p c -> p dc c"))
                            ln_h = k.ldh

                            pe0p = k.v("pe")
                            sc0p = k.v("sc")
                            vec0p = k.v("vec")

                            @b.tensor
                            def _(e, rc=rc, ln_h=ln_h, ql_v=ql_v, sc0p=sc0p,
                                  sb0=sc_bank[0], sb1=sc_bank[1]):
                                e.wait_ge(k.sem["vec"], ql_v)
                                k.dwait(e, ln_h)
                                for comp in range(4):
                                    if comp == 0:
                                        e.wait_ge(k.sem["sc"], sb0)
                                    elif comp == 1:
                                        e.wait_ge(k.sem["sc"], sb1)
                                    else:
                                        e.wait_ge(k.sem["sc"], sc0p + comp - 1)
                                    for dc in range(8):
                                        ins = e.matmul(out=pb[comp % 2][:, 0:256],
                                                       lhsT=ql4[:, dc, comp, :],
                                                       rhs=lnc3[:, rc % 2, dc, :],
                                                       start=(dc == 0), stop=(dc == 7))
                                    k.inc(ins, "pe")
                            mm_rd[rc % 2] = pe0p + 4

                            @b.scalar
                            def _(e, pe0p=pe0p, vec0p=vec0p, proj=proj,
                                  qfw=qf_rd, vh=vst_h):
                                e = k.S(e)
                                if proj == "v" and vh is not None:
                                    k.dwait(e._e, vh)
                                if proj != "v":
                                    e._e.wait_ge(k.sem["vec"], qfw)
                                for comp in range(4):
                                    e._e.wait_ge(k.sem["pe"], pe0p + comp + 1)
                                    if proj == "v":
                                        e.activation(out=qs3[:, comp, :],
                                                     in_=pb[comp % 2][:, 0:256],
                                                     func=AF.Identity,
                                                     bias=BIA[:, comp:comp + 1])
                                    else:
                                        e.activation(
                                            out=QF[:, comp * 256:(comp + 1) * 256],
                                            in_=pb[comp % 2][:, 0:256],
                                            func=AF.Identity,
                                            bias=BIA[:, comp:comp + 1])
                                if proj != "v":
                                    e._e.wait_ge(k.sem["vec"], vec0p + 7)
                                    e.activation(out=INV[:], in_=N2[:],
                                                 func=AF.Ln, bias=k.eps_q)
                                    e.activation(out=INV[:], in_=INV[:],
                                                 func=AF.Exp, scale=-0.5)
                            sc_bank[0] = sc0p + 3
                            sc_bank[1] = sc0p + 4
                            ev4 = sc0p + 4
                            inv_v = k.v("sc")

                            if proj != "v":
                                @b.vector
                                def _(e, ev4=ev4):
                                    e = k.V(e)
                                    e._e.wait_ge(k.sem["sc"], ev4)
                                    e.tensor_tensor(out=N2[:], in0=QF[:, 0:256],
                                                    in1=QF[:, 0:256], op=OP.mult)
                                    for comp in range(1, 4):
                                        e.tensor_tensor(
                                            out=SQ[:],
                                            in0=QF[:, comp * 256:(comp + 1) * 256],
                                            in1=QF[:, comp * 256:(comp + 1) * 256],
                                            op=OP.mult)
                                        e.tensor_tensor(out=N2[:], in0=N2[:],
                                                        in1=SQ[:], op=OP.add)
                                assert k.v("vec") == vec0p + 7

                                @b.vector
                                def _(e, inv_v=inv_v, proj=proj, oh=oh, rc=rc,
                                      vh=vst_h):
                                    e = k.V(e)
                                    e._e.wait_ge(k.sem["sc"], inv_v)
                                    if proj == "k" and vh is not None:
                                        k.dwait(e._e, vh)
                                    for comp in range(4):
                                        if proj == "q":
                                            outp = QBUF3[:, comp * 2 + oh,
                                                         rc * 256:(rc + 1) * 256]
                                        else:
                                            outp = qs3[:, comp, :]
                                        e.tensor_tensor(
                                            out=outp,
                                            in0=QF[:, comp * 256:(comp + 1) * 256],
                                            in1=INV[:], op=OP.mult)
                                qf_rd = k.v("vec")
                            ship_v = k.v("vec")

                            if proj != "q":
                                jdx = 0 if proj == "k" else 1

                                @b.gpsimd
                                def _(e, jdx=jdx, oh=oh, rc=rc, bb=bb,
                                      ship_v=ship_v, ev4=ev4, proj=proj):
                                    if proj == "k":
                                        e.wait_ge(k.sem["vec"], ship_v)
                                    else:
                                        e.wait_ge(k.sem["sc"], ev4)
                                    kvv = kv_sp[bb, jdx].rearrange(
                                        "(comp two p) c -> p comp two c", two=2, p=128)
                                    k.ldh = k.dma(e, "st",
                                                  kvv[:, :, oh, rc * 256:(rc + 1) * 256],
                                                  qs3)
                                vst_h = k.ldh

            with ExitStack() as s3:
                sb3 = lambda nm, sh, dt: s3.enter_context(
                    nc.sbuf_tensor(f"{nm}at{bb}", sh, dt))
                KH = sb3("kh", [128, L], BF16)
                VHF = sb3("vhf", [128, L], BF16)
                VT = sb3("vt", [128, 16 * 65], BF16)
                PT = sb3("pt", [128, 4 * 512], BF16)
                RL = sb3("rl", [128, 512], F32)
                _fence(k)

                @b.vector
                def _(e):
                    e = k.V(e)
                    for kk in range(16):
                        e.memset(VT[:, kk * 65 + 64:kk * 65 + 65], 1.0)
                vt_ones = k.v("vec")
                kvh = {0: None, 1: None}

                @b.sync
                def _(e, bb=bb):
                    k.dwait_all(e, "st")
                    k.ldh = k.dma(e, "ldw", KH[0:64, :], kv_sp[bb, 0, 0:64, :])
                    k.ldh2 = k.dma(e, "ldw", VHF[0:64, :], kv_sp[bb, 1, 0:64, :])
                kvh[0] = (k.ldh, k.ldh2)

                for hh in range(H):
                    dch = hh // 2
                    pbase = (hh % 2) * 64
                    atfl = k.cnt["_atfl"]

                    if hh + 1 < H:
                        nb2 = ((hh + 1) % 2) * 64

                        @b.sync
                        def _(e, hh=hh, bb=bb, atfl=atfl, nb2=nb2):
                            e.wait_ge(k.sem["pe"], atfl)
                            k.ldh = k.dma(e, "ldw", KH[nb2:nb2 + 64, :],
                                          kv_sp[bb, 0, (hh + 1) * 64:
                                                (hh + 2) * 64, :])
                            k.ldh2 = k.dma(e, "ldw", VHF[nb2:nb2 + 64, :],
                                           kv_sp[bb, 1, (hh + 1) * 64:
                                                 (hh + 2) * 64, :])
                        kvh[(hh + 1) % 2] = (k.ldh, k.ldh2)
                    kh_h, vh_h = kvh[hh % 2]

                    pe0 = k.v("pe")
                    vec0v = k.v("vec")

                    @b.tensor
                    def _(e, vh_h=vh_h, vec0v=vec0v, pbase=pbase,
                          vecnow=k.v("vec")):
                        k.dwait(e, vh_h)
                        e.wait_ge(k.sem["vec"], vecnow)
                        for ck in range(16):
                            if ck >= 2:
                                e.wait_ge(k.sem["vec"], vec0v + ck - 1)
                            ins = e.transpose(
                                out=pb[6 + ck % 2][:, 0:32].bitcast(BF16),
                                in_=VHF[pbase:pbase + 64, ck * 128:(ck + 1) * 128],
                                identity=k.identb[pbase:pbase + 64,
                                                  pbase:pbase + 64])
                            k.inc(ins, "pe")

                    @b.vector
                    def _(e, pe0=pe0, atfl=atfl, vt_ones=vt_ones):
                        e = k.V(e)
                        e._e.wait_ge(k.sem["pe"], atfl)
                        e._e.wait_ge(k.sem["vec"], vt_ones)
                        for ck in range(16):
                            e._e.wait_ge(k.sem["pe"], pe0 + ck + 1)
                            e.tensor_copy(out=VT[:, ck * 65:ck * 65 + 64],
                                          in_=pb[6 + ck % 2][:, 0:32].bitcast(BF16))
                    vt_rdy = k.v("vec")

                    for tqb in range(4):
                        avb = 2 + tqb % 2
                        BK = (0, 1, 4, 5)
                        pe0f = k.v("pe")
                        sc0f = k.v("sc")
                        smap, pmap = {}, {}
                        cnt = pe0f
                        for u in range(4):
                            cnt += 1
                            smap[u] = cnt
                        for tt in range(16):
                            cnt += 1
                            pmap[tt] = cnt
                            if tt + 4 < 16:
                                cnt += 1
                                smap[tt + 4] = cnt

                        @b.tensor
                        def _(e, tqb=tqb, hh=hh, kh_h=kh_h, dch=dch,
                              pbase=pbase, sbk=dict(sc_bank), BK=BK,
                              vt_rdy=vt_rdy, avw=av_rd[tqb % 2], avb=avb,
                              sc0f=sc0f):
                            k.dwait(e, kh_h)
                            e.wait_ge(k.sem["vec"], vt_rdy)

                            def sc_mm(u):
                                if u < 4:
                                    e.wait_ge(k.sem["sc"], sbk[BK[u % 4]])
                                else:
                                    e.wait_ge(k.sem["sc"], sc0f + u - 3)
                                ins = e.matmul(
                                    out=pb[BK[u % 4]][:, 0:512],
                                    lhsT=KH[pbase:pbase + 64,
                                            u * 128:(u + 1) * 128],
                                    rhs=QBUF3[pbase:pbase + 64, dch,
                                              tqb * 512:(tqb + 1) * 512],
                                    start=True, stop=True)
                                k.inc(ins, "pe")

                            def pv_mm(t):
                                e.wait_ge(k.sem["sc"], sc0f + t + 1)
                                if t == 0:
                                    e.wait_ge(k.sem["vec"], avw)
                                ins = e.matmul(
                                    out=pb[avb][0:65, 0:512],
                                    lhsT=VT[:, t * 65:t * 65 + 65],
                                    rhs=PT[:, (t % 4) * 512:(t % 4 + 1) * 512],
                                    start=(t == 0), stop=(t == 15))
                                k.inc(ins, "pe")

                            for u in range(4):
                                sc_mm(u)
                            for t in range(16):
                                pv_mm(t)
                                if t + 4 < 16:
                                    sc_mm(t + 4)

                        @b.scalar
                        def _(e, smap=dict(smap), pmap=dict(pmap),
                              ptd=dict(pt_rd), cbr_h=cbr_h, BK=BK):
                            e = k.S(e)
                            k.dwait(e._e, cbr_h)
                            for t in range(16):
                                e._e.wait_ge(k.sem["pe"], smap[t])
                                if t < 4:
                                    e._e.wait_ge(k.sem["pe"], ptd[t % 4])
                                else:
                                    e._e.wait_ge(k.sem["pe"], pmap[t - 4])
                                e.activation(
                                    out=PT[:, (t % 4) * 512:(t % 4 + 1) * 512],
                                    in_=pb[BK[t % 4]][:, 0:512],
                                    func=AF.Exp, scale=0.125, bias=CB[:, 0:1])
                        assert k.v("pe") == pe0f + 32
                        assert k.v("sc") == sc0f + 16
                        sc_bank[0] = sc0f + 13
                        sc_bank[1] = sc0f + 14
                        sc_bank[4] = sc0f + 15
                        sc_bank[5] = sc0f + 16
                        for tt in range(12, 16):
                            pt_rd[tt % 4] = pmap[tt]
                        pv_last = k.v("pe")

                        rcb = 6 + tqb % 2

                        @b.vector
                        def _(e, pv_last=pv_last, avb=avb,
                              rlw=k.cnt.get("_rlmm", 0)):
                            e = k.V(e)
                            e._e.wait_ge(k.sem["pe"], pv_last)
                            e._e.wait_ge(k.sem["pe"], rlw)
                            e.reciprocal(out=RLS[0:1, :], in_=pb[avb][64:65, 0:512])
                        rec_v = k.v("vec")

                        @b.tensor
                        def _(e, rec_v=rec_v, rcb=rcb, vt_rdy=vt_rdy,
                              rlrd=k.cnt.get("_rlrd%d" % (6 + tqb % 2), 0)):
                            e.wait_ge(k.sem["vec"], rec_v)
                            e.wait_ge(k.sem["vec"], rlrd)
                            e.wait_ge(k.sem["vec"], vt_rdy)
                            ins = e.matmul(out=pb[rcb][:, 0:512],
                                           lhsT=ONERF[0:1, :], rhs=RLS[0:1, :],
                                           start=True, stop=True)
                            k.inc(ins, "pe")
                        k.cnt["_rlmm"] = k.v("pe")
                        rlp = k.v("pe")

                        odd = pbase == 64

                        @b.vector
                        def _(e, rlp=rlp, avb=avb, rcb=rcb, odd=odd, dch=dch,
                              tqb=tqb, avs_h=avs_h):
                            e = k.V(e)
                            e._e.wait_ge(k.sem["pe"], rlp)
                            if odd and avs_h is not None:
                                k.dwait(e._e, avs_h)
                            e.tensor_copy(out=RL[0:64, :], in_=pb[rcb][0:64, 0:512])
                            outp = (AVS[0:64, :] if odd else
                                    QBUF3[0:64, dch, tqb * 512:(tqb + 1) * 512])
                            e.tensor_tensor(out=outp, in0=pb[avb][0:64, 0:512],
                                            in1=RL[0:64, :], op=OP.mult)
                        ttv = k.v("vec")
                        av_rd[avb % 2] = ttv
                        k.cnt["_rlrd%d" % rcb] = ttv

                        if odd:
                            @b.gpsimd
                            def _(e, ttv=ttv, dch=dch, tqb=tqb):
                                e.wait_ge(k.sem["vec"], ttv)
                                k.ldh = k.dma(e, "st",
                                              QBUF3[64:128, dch,
                                                    tqb * 512:(tqb + 1) * 512],
                                              AVS[0:64, :])
                            avs_h = k.ldh
                    k.cnt["_atfl"] = k.v("pe")

            with ExitStack() as s4:
                sb4 = lambda nm, sh, dt: s4.enter_context(
                    nc.sbuf_tensor(f"{nm}at{bb}", sh, dt))
                WS2 = sb4("ws2", [128, 4 * 2 * 256], F32)
                ws4o = WS2[:].rearrange("p (w u c) -> p w u c", w=4, u=2)
                _fence(k)
                ob_rd = {0: 0, 1: 0}

                @b.sync
                def _(e):
                    k.dwait_all(e, "st")
                    k.ldh = k.dma(e, "ldw",
                                  WS2[:].rearrange("p (w u c) -> p w u c", w=4, u=2),
                                  t["oW"].rearrange("w u p c -> p w u c"))
                    k.ldh2 = k.dma(e, "ldw", BL[:, 0:8],
                                   t["ob"].rearrange("w (u c) -> c (w u)", u=2),
                                   allow_slow_non_contiguous=True)
                w_h, b_h = k.ldh, k.ldh2

                boh = None
                for oh in range(2):
                    penow = k.v("pe")
                    scnow = k.v("sc")

                    @b.vector
                    def _(e, oh=oh, w_h=w_h, b_h=b_h, penow=penow, scnow=scnow):
                        e = k.V(e)
                        k.dwait(e._e, w_h)
                        k.dwait(e._e, b_h)
                        e._e.wait_ge(k.sem["pe"], penow)
                        e._e.wait_ge(k.sem["sc"], scnow)
                        for c in range(4):
                            for comp in range(4):
                                w, s = _QUAT_CW[(c, comp)]
                                for rh in range(2):
                                    e.tensor_scalar(
                                        out=ql4[:, c * 2 + rh, comp, :],
                                        in0=ws4o[:, w, rh, oh * 128:(oh + 1) * 128],
                                        scalar1=float(s), scalar2=0.0,
                                        op0=OP.mult, op1=OP.bypass)
                        for comp in range(4):
                            for wi, (c, s) in enumerate(_QUAT_TABLE[comp]):
                                if wi == 0:
                                    e.tensor_scalar(
                                        out=BIA[:, comp:comp + 1],
                                        in0=BL[:, wi * 2 + oh:wi * 2 + oh + 1],
                                        scalar1=float(s), scalar2=0.0,
                                        op0=OP.mult, op1=OP.bypass)
                                else:
                                    e.scalar_tensor_tensor(
                                        out=BIA[:, comp:comp + 1],
                                        in0=BL[:, wi * 2 + oh:wi * 2 + oh + 1],
                                        scalar=float(s),
                                        in1=BIA[:, comp:comp + 1],
                                        op0=OP.mult, op1=OP.add)
                    asm_v = k.v("vec")

                    @b.tensor
                    def _(e, asm_v=asm_v, vecnow=k.v("vec")):
                        e.wait_ge(k.sem["vec"], vecnow)
                        ins = e.transpose(out=pb[6][0:4, 0:128], in_=BIA[:, 0:4],
                                          identity=k.ident)
                        k.inc(ins, "pe")
                    btr = k.v("pe")

                    @b.vector
                    def _(e, btr=btr):
                        e = k.V(e)
                        e._e.wait_ge(k.sem["pe"], btr)
                        e.tensor_copy(out=BO8[0:4, :], in_=pb[6][0:4, 0:128])
                    bo_v = k.v("vec")

                    @b.gpsimd
                    def _(e, bo_v=bo_v, boh=boh):
                        e.wait_ge(k.sem["vec"], bo_v)
                        if boh is not None:
                            k.dwait(e, boh)
                        k.ldh = k.dma(e, "st",
                                      bo_row.rearrange("one (a c) -> a one c", a=4),
                                      BO8[0:4, :].unsqueeze(1))
                    bst = k.ldh

                    @b.sync
                    def _(e, bst=bst, penow=k.v("pe")):
                        k.dwait(e, bst)
                        e.wait_ge(k.sem["pe"], penow)
                        k.ldh = k.dma(e, "ldw", BOROW[0:1, :], bo_row)
                    boh = k.ldh

                    for g in range(16):
                        og = oh * 16 + g
                        gg = bb * 16 + g

                        @b.tensor
                        def _(e, g=g, og=og, oh=oh, boh=boh, asm_v=asm_v,
                              obw=ob_rd[og % 2]):
                            e.wait_ge(k.sem["vec"], asm_v)
                            e.wait_ge(k.sem["vec"], obw)
                            k.dwait(e, boh)
                            for dc in range(8):
                                e.matmul(out=pb[og % 2][:, 0:512],
                                         lhsT=QBUF3[:, dc, g * 128:(g + 1) * 128],
                                         rhs=QL[:].rearrange("p (dc c) -> p dc c", dc=8)
                                         [:, dc, :],
                                         start=(dc == 0), stop=False)
                            ins = e.matmul(out=pb[og % 2][:, 0:512],
                                           lhsT=ONER[0:1, :], rhs=BOROW[0:1, :],
                                           start=False, stop=True)
                            k.inc(ins, "pe")
                        omm = k.v("pe")

                        @b.vector
                        def _(e, og=og, gg=gg, oh=oh, omm=omm):
                            e = k.V(e)
                            e._e.wait_ge(k.sem["pe"], omm)
                            e.tensor_tensor(
                                out=x34[:, gg, :, oh, :],
                                in0=pb[og % 2][:, 0:512]
                                .rearrange("p (comp c) -> p comp c", c=128),
                                in1=x34[:, gg, :, oh, :], op=OP.add)
                        ob_rd[og % 2] = k.v("vec")


_PHASES_ALL = ("l0f1", "mamba", "l0f2", "l1f1", "l1attn", "l1f2")


def kernel(**inputs):
    common, per_core = host_prepare(inputs)
    nc = build_program(_PHASES_ALL)
    in_maps = [{**common, **m} for m in per_core]
    res = run_bass_kernel_spmd(nc, in_maps, core_ids=list(range(NCORES)))
    out = np.empty((16, L, D), np.float32)
    for c in range(NCORES):
        out[2 * c:2 * c + 2] = res.results[c]["xout"]
    return out
